# version 01
# baseline (speedup 1.0000x reference)
import numpy as np
import concourse.bass as bass
import concourse.mybir as mybir
from concourse.bass_utils import run_bass_kernel_spmd

F32 = mybir.dt.float32
BF16 = mybir.dt.bfloat16
I32 = mybir.dt.int32
U8 = mybir.dt.uint8
AF = mybir.ActivationFunctionType
ALU = mybir.AluOpType
AX = mybir.AxisListType

D = 1024
H = 16
DH = 64
F0 = 3072
F1 = 2816
INV127 = 1.0 / 127.0
LN_EPS = 1e-5
RMS_EPS = float(np.finfo(np.float32).eps)
L1_EPS = 1e-6

STREAMS = ("v", "s", "t", "g", "gd", "q")


class Sched:

    def __init__(self, nc):
        self.nc = nc
        self.csem = {e: nc.alloc_semaphore(f"S{e}") for e in ("v", "s", "t", "g")}
        self.ccum = {e: 0 for e in ("v", "s", "t", "g")}
        self.dsem = {}
        self.dcum = {}
        self.thunks = {e: [] for e in ("v", "s", "t", "g", "q")}
        self.last_write = {}
        self.readers = {}
        self.hiwater = {e: {} for e in ("v", "s", "t", "g", "q")}

    def _chain_sem(self, chain):
        if chain not in self.dsem:
            self.dsem[chain] = self.nc.alloc_semaphore(f"D{len(self.dsem)}")
            self.dcum[chain] = 0
        return self.dsem[chain]

    def emit(self, stream, fn, reads=(), writes=(), dma=False, chain=None):
        engine = "g" if stream == "gd" else stream
        waits = {}

        def need(tok):
            if tok is not None:
                sem, v = tok
                if v > waits.get(sem, 0):
                    waits[sem] = v

        for k in reads:
            need(self.last_write.get(k))
        for k in writes:
            need(self.last_write.get(k))
            for tok in self.readers.get(k, {}).items():
                need((tok[0], tok[1]))
        if dma:
            if chain is None:
                w0 = writes[0]
                chain = (w0[0] if isinstance(w0, tuple) else w0,)
            ck = (stream,) + tuple(chain)
            sem = self._chain_sem(ck)
            prev = self.dcum[ck]
            if prev:
                if prev > waits.get(sem, 0):
                    waits[sem] = prev
            self.dcum[ck] += 16
            mytok = (sem, self.dcum[ck])
        else:
            sem = self.csem[engine]
            self.ccum[engine] += 1
            mytok = (sem, self.ccum[engine])
        hw = self.hiwater[engine]
        wl = []
        own = self.csem.get(engine)
        for s2, v in waits.items():
            if s2 is own and not dma:
                if v >= self.ccum[engine]:
                    continue
            if v > hw.get(s2, 0):
                hw[s2] = v
                wl.append((s2, v))

        nc = self.nc

        def thunk(eng, wl=tuple(wl), fn=fn, sem=sem, inc=(16 if dma else 1), dma=dma):
            for s2, v in wl:
                eng.wait_ge(s2, v)
            n0 = nc.n_instructions()
            fn(eng).then_inc(sem, inc)
            if dma:
                assert nc.n_instructions() == n0 + 1, \
                    f"DMA emit split into {nc.n_instructions() - n0} instructions"

        self.thunks[engine if engine != "g" else "g"].append(thunk)
        if engine == "q":
            self.thunks["q"] = self.thunks["q"]
        for k in writes:
            self.last_write[k] = mytok
            self.readers[k] = {}
        for k in reads:
            rd = self.readers.setdefault(k, {})
            if mytok[1] > rd.get(mytok[0], 0):
                rd[mytok[0]] = mytok[1]
        return mytok

    def rekey(self, old_keys, new_keys):
        self.emit("v", lambda e: e.engine_nop(), reads=list(old_keys), writes=list(new_keys))

    def finalize(self):
        nc = self.nc
        finals = [(self.csem[e], self.ccum[e]) for e in ("v", "s", "t", "g") if self.ccum[e]]
        finals += [(self.dsem[c], self.dcum[c]) for c in self.dsem]
        with nc.allow_low_precision(reason="bf16 matmul operand staging; fp32 accumulation"), \
                nc.Block() as block:
            @block.vector
            def _(e):
                for th in self.thunks["v"]:
                    th(e)

            @block.scalar
            def _(e):
                for th in self.thunks["s"]:
                    th(e)

            @block.tensor
            def _(e):
                for th in self.thunks["t"]:
                    th(e)

            @block.sync
            def _(e):
                for th in self.thunks["q"]:
                    th(e)

            @block.gpsimd
            def _(e):
                for th in self.thunks["g"]:
                    th(e)
                for sem, v in finals:
                    e.wait_ge(sem, v)


def build(T=4096, upto="full", dbg=False):
    nT = T // 128
    nc = bass.Bass("TRN2", target_bir_lowering=False)

    def din(name, shape, dt=F32):
        return nc.dram_tensor(name, shape, dt, kind="ExternalInput").ap()

    x_in = din("x", [T, D])
    W = {}
    for nm in ["Wq", "Wk", "Wv", "Wo", "Wg", "Wog"]:
        W["a0_" + nm] = din("a0_" + nm, [D, D])
    W["a0_Wd"] = din("a0_Wd", [H, D])
    W["a0_bd"] = din("a0_bd", [H])
    W["a0_nw"] = din("a0_nw", [D])
    W["f0_wg"] = din("f0_wg", [F0, D], I32)
    W["f0_wu"] = din("f0_wu", [F0, D], I32)
    W["f0_wd"] = din("f0_wd", [D, F0], I32)
    for nm in ["Wq", "Wk", "Wv", "Wo"]:
        W["a1_" + nm] = din("a1_" + nm, [D, D])
    for nm in ["bq", "bk", "bv", "bo"]:
        W["a1_" + nm] = din("a1_" + nm, [D])
    W["a1_rw"] = din("a1_rw", [H, DH, DH])
    RG1 = F1 // 8
    RGD = D // 8
    wsh_g = din("wsh_g", [RG1, D])
    wsh_u = din("wsh_u", [RG1, D])
    wsh_d = din("wsh_d", [RGD, F1])
    emask_in = din("emask", [16])
    m1_in = din("m1", [4])
    out_ext = nc.dram_tensor("out", [T, D], F32, kind="ExternalOutput").ap()

    def dram(name, shape, dt):
        return nc.dram_tensor(name, shape, dt, kind="Internal").ap()

    stg = {}
    for nm in ["a0_Wq", "a0_Wk", "a0_Wv", "a0_Wo", "a0_Wg", "a0_Wog", "a1_Wq", "a1_Wk"]:
        stg[nm] = dram("stg_" + nm, [D, D], BF16)
    stg["f0_wg"] = dram("stg_f0_wg", [F0, D], BF16)
    stg["f0_wu"] = dram("stg_f0_wu", [F0, D], BF16)
    stg["f0_wd"] = dram("stg_f0_wd", [D, F0], BF16)
    qr_s = dram("qr_s", [D, D], BF16)
    kr_s = dram("kr_s", [D, D], BF16)
    bqr_s = dram("bqr_s", [D], F32)
    bkr_s = dram("bkr_s", [D], F32)
    xnfm_s = dram("xnfm_s", [8, 128, T], BF16)
    q_s = dram("q_s", [T, D], BF16)
    kt_s = dram("kt_s", [T, D], BF16)
    v_s = dram("v_s", [T, D], BF16)
    spref_s = dram("spref_s", [nT, 128, 512], BF16)
    partial_s = dram("partial_s", [T, D], F32)
    h0_s = dram("h0_s", [T, F0], BF16)
    xnew_s = dram("xnew_s", [T, D], F32)
    xn1_s = dram("xn1_s", [T, D], F32)
    qp_s = dram("qp_s", [T, D], BF16)
    partial2_s = dram("partial2_s", [T, D], F32)
    xq_s = dram("xq_s", [8, 128, T], BF16)
    hq_s = dram("hq_s", [T, F1], BF16)
    cc1_in_s = dram("cc1_in_s", [128, 512], F32)
    cc1_in_g = dram("cc1_in_g", [1, H], F32)
    cc1_out_s = dram("cc1_out_s", [512, 512], F32)
    cc1_out_g = dram("cc1_out_g", [4, H], F32)
    cc2_in_kv = dram("cc2_in_kv", [128, 512], F32)
    cc2_in_kp = dram("cc2_in_kp", [128, 8], F32)
    cc2_out_kv = dram("cc2_out_kv", [128, 512], F32)
    cc2_out_kp = dram("cc2_out_kp", [128, 8], F32)
    ccw_in_g = dram("ccw_in_g", [RG1, D], BF16)
    ccw_in_u = dram("ccw_in_u", [RG1, D], BF16)
    ccw_in_d = dram("ccw_in_d", [RGD, F1], BF16)
    wgq_s = dram("wgq_s", [F1, D], BF16)
    if dbg:
        dbg_o = dram("dbg_o", [T, D], F32)
        dbg_g = dram("dbg_g", [T, D], BF16)
        dbg_r = dram("dbg_r", [T], F32)
        dbg_m = dram("dbg_m", [T, D], F32)
        dbg_m2 = dram("dbg_m2", [T, D], F32)
        dbg_s = dram("dbg_s", [T, D], BF16)
        dbg_D = dram("dbg_D", [128, 128], F32)
        dbg_gchm = dram("dbg_gchm", [16, 128], F32)
    wuq_s = dram("wuq_s", [F1, D], BF16)
    wdq_s = dram("wdq_s", [D, F1], BF16)

    S = nc.alloc_sbuf_tensor
    ones_f = S("ones_f", [128, 128], F32)
    ltri_f = S("ltri_f", [128, 128], F32)
    id_b = S("id_b", [128, 128], BF16)
    id_f = S("id_f", [128, 128], F32)
    onehot = S("onehot", [16, 16], F32)
    onesb = S("onesb", [128, 1], BF16)
    cst = S("cst", [128, 4], F32)
    bd_rep = S("bd_rep", [128, H], F32)
    rep = [S(f"rep{i}", [128, D], F32) for i in range(3)]
    nw_rep, bqr_rep, bkr_rep = rep[0], rep[1], rep[2]
    bv_rep = rep[0]
    bo_rep = rep[1]
    emask_rep = S("emask_rep", [128, 16], F32)
    m1_rep = S("m1_rep", [128, 4], F32)
    rw_sb = S("rw_sb", [128, 8 * DH], BF16)
    bq_sb = S("bq_sb", [128, 8], BF16)
    bk_sb = S("bk_sb", [128, 8], BF16)
    wd0_sb = S("wd0_sb", [16, D], BF16)
    wdT_sb = S("wdT_sb", [128, 8 * 16], BF16)
    arA = S("arA", [128, 24576], BF16)
    arB = S("arB", [128, 24576], BF16)
    S_run = S("S_run", [128, 512], F32)
    G_run = S("G_run", [128, H], F32)
    Gc_st = S("Gc_st", [128, nT * H], F32)
    Gpre_st = S("Gpre_st", [128, nT * H], F32)
    S_in = S("S_in", [128, 512], F32)
    gall = S("gall", [128, 4 * H], F32)
    ccoef = S("ccoef", [128, 64], F32)
    cfar = S("cfar", [128, 32], F32)
    kvg_sb = S("kvg_sb", [128, 512], BF16)
    kpg_sb = S("kpg_sb", [128, 8], BF16)
    xt = [S(f"xt{i}", [128, D], F32) for i in range(2)]
    yt = [S(f"yt{i}", [128, D], F32) for i in range(2)]
    bt1 = [S(f"bt1_{i}", [128, D], BF16) for i in range(2)]
    bt2 = [S(f"bt2_{i}", [128, D], BF16) for i in range(2)]
    bt3 = [S(f"bt3_{i}", [128, D], BF16) for i in range(2)]
    bt4 = [S(f"bt4_{i}", [128, D], BF16) for i in range(2)]
    sprefb = [S(f"sprefb{i}", [128, 512], BF16) for i in range(2)]
    sprefc = S("sprefc", [128, 512], BF16)
    ht = [S(f"ht{i}", [128, F0], BF16) for i in range(2)]
    hfm = S("hfm", [128, F0], BF16)
    mrow = S("mrow", [128, 1], F32)
    sqrow = S("sqrow", [128, 1], F32)
    vrow = S("vrow", [128, 1], F32)
    rstd = S("rstd", [128, 1], F32)
    tmpA = S("tmpA", [128, D], F32)
    gsig = S("gsig", [128, D], BF16)
    negGc = S("negGc", [128, H], F32)
    a_t = S("a_t", [128, H], F32)
    zrow = S("zrow", [128, H], F32)
    azrow = S("azrow", [128, H], F32)
    erow = S("erow", [128, H], F32)
    lrow = S("lrow", [128, H], F32)
    gdec = S("gdec", [128, H], F32)
    csum = S("csum", [128, H], F32)
    lam = S("lam", [128, H], F32)
    lam_ar = S("lam_ar", [128, 8], F32)
    ek = S("ek", [128, H], F32)
    gchm = S("gchm", [16, 128], F32)
    Dt = [S(f"Dt{i}", [128, 128], BF16) for i in range(2)]
    At = [S(f"At{i}", [128, 128], BF16) for i in range(2)]
    qxn = S("qxn", [128, 512], F32)
    qtmp = S("qtmp", [128, 512], F32)
    qi1 = S("qi1", [128, 512], I32)
    qa = S("qa", [128, 512], F32)
    qb = S("qb", [128, 512], F32)
    qaz = S("qaz", [128, 512], F32)
    qm1 = S("qm1", [128, 512], U8)
    qm2 = S("qm2", [128, 512], U8)
    qbm = S("qbm", [128, 32], F32)
    qsc = S("qsc", [128, 32], F32)
    qrc = S("qrc", [128, 32], F32)
    hq32 = S("hq32", [128, 512], F32)
    sg32 = S("sg32", [128, 512], F32)
    wstg = [S("wstg0", [128, 1024], BF16)]
    qrstg = S("qrstg", [128, 512], BF16)

    P = nc.alloc_psum_tensor
    PC = P("PC", [128, 1024], F32)
    PD = P("PD", [128, 1024], F32)
    PA = P("PA", [128, 512], F32)
    PB = P("PB", [128, 512], F32)
    PE = P("PE", [128, 512], F32)
    PT = P("PT", [128, 1024], BF16)

    sch = Sched(nc)
    E = sch.emit

    def dma_q(dst, src, reads, writes, chain=None):
        E("q", lambda e: e.dma_start(out=dst, in_=src), reads=reads, writes=writes, dma=True,
          chain=chain)

    def dma_g(dst, src, reads, writes, slow=False, chain=None):
        E("gd", lambda e: e.dma_start(out=dst, in_=src, allow_slow_non_contiguous=slow),
          reads=reads, writes=writes, dma=True, chain=chain)

    def tp_q(dst, src, reads, writes, chain=None):
        E("q", lambda e: e.dma_start_transpose(dst, src), reads=reads, writes=writes, dma=True,
          chain=chain)

    def emit_ln(src, sk, dst, dk, dst2=None, d2k=None):
        E("v", lambda e: e.reduce_sum(mrow[:, :], src[:, :], axis=AX.X), [sk], ["mrow"])
        E("s", lambda e: e.activation(dst[:, :], src[:, :], AF.Square, accum_out=sqrow[:, :]),
          [sk], ["sqrow", dk])
        E("v", lambda e: e.tensor_scalar_mul(mrow[:, :], mrow[:, :], 1.0 / D), ["mrow"], ["mrow"])
        E("v", lambda e: e.tensor_scalar_mul(sqrow[:, :], sqrow[:, :], 1.0 / D), ["sqrow"], ["sqrow"])
        E("v", lambda e: e.tensor_tensor(vrow[:, :], mrow[:, :], mrow[:, :], ALU.mult),
          ["mrow"], ["vrow"])
        E("v", lambda e: e.tensor_tensor(vrow[:, :], sqrow[:, :], vrow[:, :], ALU.subtract),
          ["sqrow", "vrow"], ["vrow"])
        E("s", lambda e: e.activation(rstd[:, :], vrow[:, :], AF.Ln, bias=cst[:, 0:1]),
          ["vrow", "cst"], ["rstd"])
        E("s", lambda e: e.activation(rstd[:, :], rstd[:, :], AF.Exp, scale=-0.5),
          ["rstd"], ["rstd"])
        E("v", lambda e: e.tensor_scalar(dst[:, :], src[:, :], mrow[:, 0:1], rstd[:, 0:1],
                                         ALU.subtract, ALU.mult),
          [sk, "mrow", "rstd"], [dk])
        if dst2 is not None:
            E("v", lambda e: e.tensor_scalar(dst2[:, :], src[:, :], mrow[:, 0:1], rstd[:, 0:1],
                                             ALU.subtract, ALU.mult),
              [sk, "mrow", "rstd"], [d2k])

    def emit_tp8(src, sk, dst, dk, nblk=8, doff=0):
        for b in range(nblk):
            E("t", lambda e, b=b: e.transpose(PT[:, b * 128:(b + 1) * 128],
                                              src[:, b * 128:(b + 1) * 128], id_b[:, :]),
              [sk, "id_b"], ["PT"])
        E("v", lambda e: e.tensor_copy(dst[:, doff:doff + nblk * 128], PT[:, 0:nblk * 128]),
          ["PT"], [dk])

    def emit_proj(fm, fmk, wview, wk, psum, pk, N, nk=8, wN=None):
        wN = N if wN is None else wN
        nj = (N + 511) // 512
        for c in range(nk):
            for j in range(nj):
                n0, n1 = j * 512, min(N, (j + 1) * 512)
                E("t", lambda e, c=c, n0=n0, n1=n1: e.matmul(
                    psum[:, n0:n1], fm[:, c * 128:(c + 1) * 128], wview[:, c, n0:n1],
                    start=(c == 0), stop=(c == nk - 1)),
                  [fmk, wk], [pk])

    def emit_fp4(src, sk, dst, dk, Wd):
        nb = Wd // 16
        sv = src[:, 0:Wd].rearrange("p (b i) -> p b i", i=16)
        E("v", lambda e: e.reduce_max(qbm[:, 0:nb], sv, axis=AX.X, apply_absolute_value=True),
          [sk], ["qbm"])
        E("v", lambda e: e.tensor_scalar(qsc[:, 0:nb], qbm[:, 0:nb], 1.0 / 6.0, 1e-12,
                                         ALU.mult, ALU.max), ["qbm"], ["qsc"])
        E("v", lambda e: e.reciprocal(qrc[:, 0:nb], qsc[:, 0:nb]), ["qsc"], ["qrc"])
        rcb = qrc[:, 0:nb].broadcast_to([128, nb, 16])
        E("v", lambda e: e.tensor_tensor(qxn[:, 0:Wd].rearrange("p (b i) -> p b i", i=16),
                                         sv, rcb, ALU.mult), [sk, "qrc"], ["qxn"])
        E("v", lambda e: e.tensor_scalar_mul(qi1[:, 0:Wd], qxn[:, 0:Wd], 2.0), ["qxn"], ["qi1"])
        E("v", lambda e: e.tensor_scalar_mul(qa[:, 0:Wd], qi1[:, 0:Wd], 0.5), ["qi1"], ["qa"])
        E("v", lambda e: e.tensor_copy(qi1[:, 0:Wd], qxn[:, 0:Wd]), ["qxn"], ["qi1"])
        E("g", lambda e: e.tensor_scalar(qb[:, 0:Wd], qi1[:, 0:Wd], 4.0, -4.0, ALU.min, ALU.max),
          ["qi1"], ["qb"])
        E("s", lambda e: e.activation(qaz[:, 0:Wd], qxn[:, 0:Wd], AF.Abs),
          ["qxn"], ["qaz"])
        E("g", lambda e: e.tensor_single_scalar(qm1[:, 0:Wd], qaz[:, 0:Wd], 2.25, ALU.is_lt),
          ["qaz"], ["qm1"])
        E("g", lambda e: e.tensor_single_scalar(qm2[:, 0:Wd], qaz[:, 0:Wd], 5.0, ALU.is_lt),
          ["qaz"], ["qm2"])
        E("g", lambda e: e.tensor_single_scalar(qaz[:, 0:Wd], qxn[:, 0:Wd], 0.0, ALU.is_ge),
          ["qxn", "qm1", "qm2"], ["qaz"])
        E("g", lambda e: e.tensor_scalar(qaz[:, 0:Wd], qaz[:, 0:Wd], 12.0, -6.0,
                                         ALU.mult, ALU.add), ["qaz"], ["qaz"])
        E("v", lambda e: e.select(qtmp[:, 0:Wd], qm2[:, 0:Wd], qb[:, 0:Wd], qaz[:, 0:Wd],
                                  add_drain=True),
          ["qm2", "qb", "qaz"], ["qtmp"])
        E("v", lambda e: e.select(qb[:, 0:Wd], qm1[:, 0:Wd], qa[:, 0:Wd], qtmp[:, 0:Wd],
                                  add_drain=True),
          ["qm1", "qa", "qtmp"], ["qb"])
        scb = qsc[:, 0:nb].broadcast_to([128, nb, 16])
        E("v", lambda e: e.tensor_tensor(dst[:, 0:Wd].rearrange("p (b i) -> p b i", i=16),
                                         qb[:, 0:Wd].rearrange("p (b i) -> p b i", i=16),
                                         scb, ALU.mult), ["qb", "qsc"], [dk])

    def arA_view(slot, width=8192, cdim=8):
        off = slot * 8192
        return arA[:, off:off + width].rearrange("p (c n) -> p c n", c=cdim), ("arA", slot)

    def arB_view(slot, width=8192, cdim=8):
        off = slot * 8192
        return arB[:, off:off + width].rearrange("p (c n) -> p c n", c=cdim), ("arB", slot)

    E("g", lambda e: e.memset(ones_f[:, :], 1.0), [], ["ones_f"])
    E("g", lambda e: e.memset(ltri_f[:, :], 1.0), [], ["ltri_f"])
    E("g", lambda e: e.affine_select(ltri_f[:, :], ltri_f[:, :], [[1, 128]], ALU.is_ge, 0.0,
                                     base=0, channel_multiplier=-1), ["ltri_f"], ["ltri_f"])
    E("g", lambda e: e.memset(id_b[:, :], 1.0), [], ["id_b"])
    E("g", lambda e: e.affine_select(id_b[:, :], id_b[:, :], [[1, 128]], ALU.is_equal, 0.0,
                                     base=0, channel_multiplier=-1), ["id_b"], ["id_b"])
    E("g", lambda e: e.memset(id_f[:, :], 1.0), [], ["id_f"])
    E("g", lambda e: e.affine_select(id_f[:, :], id_f[:, :], [[1, 128]], ALU.is_equal, 0.0,
                                     base=0, channel_multiplier=-1), ["id_f"], ["id_f"])
    E("g", lambda e: e.memset(onehot[:, :], 1.0), [], ["onehot"])
    E("g", lambda e: e.affine_select(onehot[:, :], onehot[:, :], [[1, 16]],
                                     ALU.is_equal, 0.0, base=0, channel_multiplier=-1),
      ["onehot"], ["onehot"])
    E("g", lambda e: e.memset(onesb[:, :], 1.0), [], ["onesb"])
    E("g", lambda e: e.memset(cst[:, 0:1], LN_EPS), [], ["cst"])
    E("g", lambda e: e.memset(cst[:, 1:2], RMS_EPS), ["cst"], ["cst"])
    E("g", lambda e: e.memset(cst[:, 2:3], 1.0), ["cst"], ["cst"])
    E("g", lambda e: e.memset(cst[:, 3:4], L1_EPS), ["cst"], ["cst"])
    dma_g(bd_rep[:, :], W["a0_bd"].partition_broadcast(128), [], ["bd_rep"])
    dma_g(nw_rep[:, :], W["a0_nw"].partition_broadcast(128), [], ["rep0"])
    dma_g(emask_rep[:, :], emask_in.partition_broadcast(128), [], ["emask_rep"])
    dma_g(m1_rep[:, :], m1_in.partition_broadcast(128), [], ["m1_rep"])
    dma_g(rw_sb[:, :].rearrange("p (a f) -> p a f", a=8),
          W["a1_rw"].rearrange("(a b) d f -> (b d) a f", b=2), [], ["rw_sb"], slow=True)
    dma_g(bq_sb[:, :], W["a1_bq"].rearrange("(a b d) -> (b d) a", a=8, b=2), [], ["bq_sb"],
          slow=True)
    dma_g(bk_sb[:, :], W["a1_bk"].rearrange("(a b d) -> (b d) a", a=8, b=2), [], ["bk_sb"],
          slow=True)
    for nm in ["a0_Wq", "a0_Wk", "a0_Wg", "a0_Wv", "a0_Wog", "a0_Wo",
               "f0_wg", "f0_wu", "f0_wd", "a1_Wq", "a1_Wk"]:
        dma_g(stg[nm][:, :], W[nm][:, :], [], ["stg_" + nm])

    P1W = {}
    for i, nm in enumerate(["a0_Wq", "a0_Wk", "a0_Wv"]):
        view, key = arA_view(i)
        P1W[nm] = (view, key)
        for c in range(8):
            tp_q(view[:, c, :], stg[nm][:, c * 128:(c + 1) * 128], ["stg_" + nm], [key], chain=(key[0],))
    vg, kg = arB_view(2)
    P1W["a0_Wg"] = (vg, kg)
    for c in range(8):
        tp_q(vg[:, c, :], stg["a0_Wg"][:, c * 128:(c + 1) * 128], ["stg_a0_Wg"], [kg], chain=(kg[0],))
    dma_g(wd0_sb[:, :], W["a0_Wd"][:, :], [], ["wd0_sb"])
    for c in range(8):
        E("t", lambda e, c=c: e.transpose(PT[:, c * 16:(c + 1) * 16],
                                          wd0_sb[:, c * 128:(c + 1) * 128], id_b[0:16, 0:16]),
          ["wd0_sb", "id_b"], ["PT"])
    E("v", lambda e: e.tensor_copy(wdT_sb[:, :], PT[:, 0:128]), ["PT"], ["wdT_sb"])
    wdT_view = wdT_sb[:, :].rearrange("p (c n) -> p c n", c=8)
    P2W = {}
    for i, nm in enumerate(["a0_Wog", "a0_Wo"]):
        view, key = arB_view(i)
        P2W[nm] = (view, key)
        for c in range(8):
            tp_q(view[:, c, :], stg[nm][:, c * 128:(c + 1) * 128], ["stg_" + nm], [key], chain=(key[0],))

    E("v", lambda e: e.memset(S_run[:, :], 0.0), [], ["S_run"])
    E("v", lambda e: e.memset(G_run[:, :], 0.0), [], ["G_run"])

    def p1_tile(t):
        p = t % 2
        xk, b1k, b2k, b3k, b4k = ("xt", p), ("bt1", p), ("bt2", p), ("bt3", p), ("bt4", p)
        dma_q(xt[p][:, :], x_in[t * 128:(t + 1) * 128, :], [], [xk])
        emit_ln(xt[p], xk, bt1[p], b1k)
        emit_tp8(bt1[p], b1k, bt2[p], b2k)
        dma_g(xnfm_s[:, :, t * 128:(t + 1) * 128].rearrange("a p n -> p a n"),
              bt2[p][:, :].rearrange("p (a n) -> p a n", a=8), [b2k], [("xnfm_s", t)])
        emit_proj(bt2[p], b2k, *P1W["a0_Wg"], PC, "PC", D)
        for c in range(8):
            E("t", lambda e, c=c: e.matmul(PA[:, 0:16], bt2[p][:, c * 128:(c + 1) * 128],
                                           wdT_view[:, c, :], start=(c == 0), stop=(c == 7)),
              [b2k, "wdT_sb"], ["PA"])
        E("v", lambda e: e.tensor_tensor(zrow[:, :], PA[:, 0:16], bd_rep[:, :], ALU.add),
          ["PA", "bd_rep"], ["zrow"])
        E("s", lambda e: e.activation(azrow[:, :], zrow[:, :], AF.Abs),
          ["zrow"], ["azrow"])
        E("s", lambda e: e.activation(erow[:, :], azrow[:, :], AF.Exp, scale=-1.0),
          ["azrow"], ["erow"])
        E("s", lambda e: e.activation(lrow[:, :], erow[:, :], AF.Ln, bias=cst[:, 2:3]),
          ["erow", "cst"], ["lrow"])
        E("v", lambda e: e.tensor_scalar(gdec[:, :], zrow[:, :], -1.0, 0.0, ALU.mult, ALU.min),
          ["zrow"], ["gdec"])
        E("v", lambda e: e.tensor_tensor(gdec[:, :], gdec[:, :], lrow[:, :], ALU.subtract),
          ["gdec", "lrow"], ["gdec"])
        E("s", lambda e: e.activation(tmpA[:, :], PC[:, :], AF.Exp, scale=-1.0),
          ["PC"], ["tmpA"])
        E("v", lambda e: e.tensor_scalar_add(tmpA[:, :], tmpA[:, :], 1.0), ["tmpA"], ["tmpA"])
        E("v", lambda e: e.reciprocal(gsig[:, :], tmpA[:, :]), ["tmpA"], ["gsig"])
        E("t", lambda e: e.matmul(PB[:, 0:16], ltri_f[:, :], gdec[:, :], start=True, stop=True),
          ["ltri_f", "gdec"], ["PB"])
        E("t", lambda e: e.matmul(PB[:, 16:32], ones_f[:, :], gdec[:, :], start=True, stop=True),
          ["ones_f", "gdec", "PB"], ["PB"])
        E("v", lambda e: e.tensor_copy(Gc_st[:, t * H:(t + 1) * H], PB[:, 0:16]),
          ["PB"], [("Gc_st", t)])
        E("v", lambda e: e.tensor_copy(csum[:, :], PB[:, 16:32]), ["PB"], ["csum"])
        E("v", lambda e: e.tensor_copy(Gpre_st[:, t * H:(t + 1) * H], G_run[:, :]),
          ["G_run"], [("Gpre_st", t)])
        E("v", lambda e: e.tensor_tensor(G_run[:, :], G_run[:, :], csum[:, :], ALU.add),
          ["G_run", "csum"], ["G_run"])
        E("s", lambda e: e.activation(lam[:, :], csum[:, :], AF.Exp), ["csum"], ["lam"])
        E("v", lambda e: e.tensor_copy(lam_ar[0:64, :], lam[0:64, 0:16:2]), ["lam"], ["lam_ar"])
        E("v", lambda e: e.tensor_copy(lam_ar[64:128, :], lam[64:128, 1:16:2]),
          ["lam", "lam_ar"], ["lam_ar"])
        emit_proj(bt2[p], b2k, *P1W["a0_Wq"], PC, "PC", D)
        emit_proj(bt2[p], b2k, *P1W["a0_Wk"], PD, "PD", D)
        E("v", lambda e: e.tensor_copy(bt3[p][:, :], PC[:, :]), ["PC"], [b3k])
        E("v", lambda e: e.tensor_tensor(bt1[p][:, :], PD[:, :], gsig[:, :], ALU.mult),
          ["PD", "gsig"], [b1k])
        emit_proj(bt2[p], b2k, *P1W["a0_Wv"], PC, "PC", D)
        E("v", lambda e: e.tensor_copy(bt4[p][:, :], PC[:, :]), ["PC"], [b4k])
        E("v", lambda e: e.tensor_tensor(ek[:, :], csum[:, :], Gc_st[:, t * H:(t + 1) * H],
                                         ALU.subtract), ["csum", ("Gc_st", t)], ["ek"])
        E("s", lambda e: e.activation(ek[:, :], ek[:, :], AF.Exp), ["ek"], ["ek"])
        ekb = ek[:, :].broadcast_to([128, H, DH])
        E("v", lambda e, ekb=ekb: e.tensor_tensor(
            bt2[p][:, :].rearrange("p (h d) -> p h d", h=H),
            bt1[p][:, :].rearrange("p (h d) -> p h d", h=H), ekb, ALU.mult),
          [b1k, "ek"], [b2k])
        for h in range(H):
            par, hp = h % 2, h // 2
            E("t", lambda e, h=h, par=par, hp=hp: e.matmul(
                PE[par * 64:par * 64 + 64, hp * 64:(hp + 1) * 64],
                bt2[p][:, h * 64:(h + 1) * 64], bt4[p][:, h * 64:(h + 1) * 64],
                start=True, stop=True, tile_position=(0, par * 64)),
              [b2k, b4k], ["PE"])
        E("v", lambda e: e.tensor_copy(sprefb[p][:, :], S_run[:, :]), ["S_run"], [("sprefb", p)])
        dma_g(spref_s[t, :, :], sprefb[p][:, :], [("sprefb", p)], [("spref_s", t)])
        lab = lam_ar[:, :].broadcast_to([128, 8, DH])
        E("v", lambda e, lab=lab: e.tensor_tensor(
            S_run[:, :].rearrange("p (a d) -> p a d", a=8),
            S_run[:, :].rearrange("p (a d) -> p a d", a=8), lab, ALU.mult),
          ["S_run", "lam_ar"], ["S_run"])
        E("v", lambda e: e.tensor_tensor(S_run[:, :], S_run[:, :], PE[:, :], ALU.add),
          ["S_run", "PE"], ["S_run"])
        dma_g(q_s[t * 128:(t + 1) * 128, :], bt3[p][:, :], [b3k], [("q_s", t)])
        dma_g(kt_s[t * 128:(t + 1) * 128, :], bt1[p][:, :], [b1k], [("kt_s", t)])
        dma_g(v_s[t * 128:(t + 1) * 128, :], bt4[p][:, :], [b4k], [("v_s", t)])

    for t in range(nT):
        p1_tile(t)
    RGB = [[0, 1, 2, 3], [4, 5, 6, 7]]
    dma_g(cc1_in_s[:, :], S_run[:, :], ["S_run"], ["cc1_in_s"])
    dma_g(cc1_in_g[:, :], G_run[0:1, :], ["G_run"], ["cc1_in_g"])
    E("g", lambda e: e.collective_compute("AllGather", ALU.bypass, replica_groups=RGB,
                                          ins=[cc1_in_s[:, :].opt()],
                                          outs=[cc1_out_s[:, :].opt()]),
      ["cc1_in_s"], ["cc1_out_s"])
    E("g", lambda e: e.collective_compute("AllGather", ALU.bypass, replica_groups=RGB,
                                          ins=[cc1_in_g[:, :].opt()],
                                          outs=[cc1_out_g[:, :].opt()]),
      ["cc1_in_g"], ["cc1_out_g"])
    dma_g(gall[:, :], cc1_out_g[:, :].rearrange("r h -> (r h)").partition_broadcast(128),
          ["cc1_out_g"], ["gall"])
    for u in range(4):
        def mk(e, u=u):
            em = emask_rep[:, u:16:4].broadcast_to([128, 4, H])
            gu = gall[:, u * H:(u + 1) * H].broadcast_to([128, H, 4]).transpose([0, 2, 1])
            return e.tensor_tensor(tmpA[:, 0:64].rearrange("p (a h) -> p a h", a=4), em, gu,
                                   ALU.mult)
        E("v", mk, ["emask_rep", "gall"], ["tmpA"])
        if u == 0:
            E("v", lambda e: e.tensor_copy(ccoef[:, :], tmpA[:, 0:64]), ["tmpA"], ["ccoef"])
        else:
            E("v", lambda e: e.tensor_tensor(ccoef[:, :], ccoef[:, :], tmpA[:, 0:64], ALU.add),
              ["ccoef", "tmpA"], ["ccoef"])
    E("s", lambda e: e.activation(ccoef[:, :], ccoef[:, :], AF.Exp), ["ccoef"], ["ccoef"])
    m1b = m1_rep[:, :].broadcast_to([128, 4, H])
    E("v", lambda e: e.tensor_tensor(ccoef[:, :].rearrange("p (a h) -> p a h", a=4),
                                     ccoef[:, :].rearrange("p (a h) -> p a h", a=4), m1b,
                                     ALU.mult), ["ccoef", "m1_rep"], ["ccoef"])
    E("v", lambda e: e.memset(S_in[:, :], 0.0), [], ["S_in"])
    for pp in range(4):
        E("v", lambda e, pp=pp: e.tensor_copy(cfar[0:64, pp * 8:(pp + 1) * 8],
                                              ccoef[0:64, pp * H:(pp + 1) * H:2]),
          ["ccoef"], ["cfar"])
        E("v", lambda e, pp=pp: e.tensor_copy(cfar[64:128, pp * 8:(pp + 1) * 8],
                                              ccoef[64:128, pp * H + 1:(pp + 1) * H:2]),
          ["ccoef", "cfar"], ["cfar"])
        dma_q(qxn[:, 0:512], cc1_out_s[pp * 128:(pp + 1) * 128, :], ["cc1_out_s"], ["qxn"])
        E("v", lambda e, pp=pp: e.tensor_tensor(
            tmpA[:, 0:512].rearrange("p (a d) -> p a d", a=8),
            qxn[:, 0:512].rearrange("p (a d) -> p a d", a=8),
            cfar[:, pp * 8:(pp + 1) * 8].broadcast_to([128, 8, DH]), ALU.mult),
          ["qxn", "cfar"], ["tmpA"])
        E("v", lambda e: e.tensor_tensor(S_in[:, :], S_in[:, :], tmpA[:, 0:512], ALU.add),
          ["S_in", "tmpA"], ["S_in"])

    sch.rekey(["PA", "PB"], [("PA", 0), ("PA", 1), ("PA", 2), ("PA", 3),
                             ("PB", 0), ("PB", 1), ("PB", 2), ("PB", 3)])
    def p2_tile(t):
        p = t % 2
        b1k, b2k, b3k, b4k = ("bt1", p), ("bt2", p), ("bt3", p), ("bt4", p)
        hk = ("ht", p)
        dma_q(bt3[p][:, :], q_s[t * 128:(t + 1) * 128, :], [("q_s", t)], [b3k])
        dma_q(bt1[p][:, :], kt_s[t * 128:(t + 1) * 128, :], [("kt_s", t)], [b1k])
        dma_q(bt4[p][:, :], v_s[t * 128:(t + 1) * 128, :], [("v_s", t)], [b4k])
        dma_q(bt2[p][:, :].rearrange("p (a n) -> p a n", a=8),
              xnfm_s[:, :, t * 128:(t + 1) * 128].rearrange("a p n -> p a n"),
              [("xnfm_s", t)], [b2k])
        dma_q(sprefb[p][:, :], spref_s[t, :, :], [("spref_s", t)], [("sprefb", p)])
        dma_q(xt[p][:, :], x_in[t * 128:(t + 1) * 128, :], [], [("xt", p)])
        emit_proj(bt2[p], b2k, *P2W["a0_Wog"], PC, "PC", D)
        E("s", lambda e: e.activation(tmpA[:, :], PC[:, :], AF.Exp, scale=-1.0), ["PC"], ["tmpA"])
        E("v", lambda e: e.tensor_scalar_add(tmpA[:, :], tmpA[:, :], 1.0), ["tmpA"], ["tmpA"])
        E("v", lambda e: e.reciprocal(gsig[:, :], tmpA[:, :]), ["tmpA"], ["gsig"])
        emit_tp8(bt3[p], b3k, ht[p], hk, doff=0)
        emit_tp8(bt1[p], b1k, ht[p], hk, doff=1024)
        qfm = ht[p][:, 0:1024]
        kfm = ht[p][:, 1024:2048]
        E("t", lambda e: e.transpose(PE[0:16, 0:128], Gc_st[:, t * H:(t + 1) * H], id_f[:, :]),
          [("Gc_st", t), "id_f"], ["PE"])
        E("v", lambda e: e.tensor_copy(gchm[:, :], PE[0:16, 0:128]), ["PE"], ["gchm"])
        E("v", lambda e: e.tensor_scalar_mul(negGc[:, :], Gc_st[:, t * H:(t + 1) * H], -1.0),
          [("Gc_st", t)], ["negGc"])
        E("s", lambda e: e.activation(a_t[:, :], Gc_st[:, t * H:(t + 1) * H], AF.Exp),
          [("Gc_st", t)], ["a_t"])
        E("s", lambda e: e.activation(lam[:, :], Gpre_st[:, t * H:(t + 1) * H], AF.Exp),
          [("Gpre_st", t)], ["lam"])
        E("v", lambda e: e.tensor_copy(lam_ar[0:64, :], lam[0:64, 0:16:2]), ["lam"], ["lam_ar"])
        E("v", lambda e: e.tensor_copy(lam_ar[64:128, :], lam[64:128, 1:16:2]),
          ["lam", "lam_ar"], ["lam_ar"])
        E("v", lambda e: e.tensor_tensor(
            tmpA[:, 0:512].rearrange("p (a d) -> p a d", a=8),
            S_in[:, :].rearrange("p (a d) -> p a d", a=8),
            lam_ar[:, :].broadcast_to([128, 8, DH]), ALU.mult),
          ["S_in", "lam_ar"], ["tmpA"])
        E("v", lambda e: e.tensor_tensor(sprefc[:, :], sprefb[p][:, :], tmpA[:, 0:512], ALU.add),
          [("sprefb", p), "tmpA"], ["sprefc"])
        for h in range(H):
            par, hp = h % 2, h // 2
            slot, ds = h % 4, h % 2
            E("t", lambda e, h=h, slot=slot: e.matmul(
                PA[:, slot * 128:(slot + 1) * 128],
                onehot[:, h:h + 1].broadcast_to([16, 1, 128]),
                gchm[:, :], start=True, stop=True),
              ["onehot", "gchm"], [("PA", slot)])
            E("s", lambda e, h=h, slot=slot, ds=ds: e.activation(
                Dt[ds][:, :], PA[:, slot * 128:(slot + 1) * 128], AF.Exp,
                bias=negGc[:, h:h + 1]),
              [("PA", slot), "negGc"], [("Dt", ds)])
            E("t", lambda e, par=par, hp=hp, slot=slot: e.matmul(
                PB[:, slot * 128:(slot + 1) * 128],
                kfm[par * 64:par * 64 + 64, hp * 128:(hp + 1) * 128],
                qfm[par * 64:par * 64 + 64, hp * 128:(hp + 1) * 128],
                start=True, stop=True),
              [hk], [("PB", slot)])
            E("v", lambda e, slot=slot, ds=ds: e.tensor_tensor(
                At[ds][:, :], PB[:, slot * 128:(slot + 1) * 128], Dt[ds][:, :], ALU.mult),
              [("PB", slot), ("Dt", ds)], [("At", ds)])
            E("g", lambda e, ds=ds: e.affine_select(
                At[ds][:, :], At[ds][:, :], [[1, 128]], ALU.is_ge, 0.0,
                base=0, channel_multiplier=-1),
              [("At", ds)], [("At", ds)])
            if dbg and t == 0 and h == 0:
                dma_g(dbg_D[:, :], Dt[ds][:, :], [("Dt", ds)], ["dbg_D"], chain=("dbgD",))
                dma_g(dbg_gchm[:, :], gchm[:, :], ["gchm"], ["dbg_gchm"], chain=("dbgG",))
            E("t", lambda e, h=h, ds=ds: e.matmul(
                PC[:, h * 64:(h + 1) * 64], At[ds][:, :], bt4[p][:, h * 64:(h + 1) * 64],
                start=True, stop=True),
              [("At", ds), b4k], ["PC"])
            E("t", lambda e, h=h, par=par, hp=hp: e.matmul(
                PD[:, h * 64:(h + 1) * 64],
                qfm[par * 64:par * 64 + 64, hp * 128:(hp + 1) * 128],
                sprefc[par * 64:par * 64 + 64, hp * 64:(hp + 1) * 64],
                start=True, stop=True),
              [hk, "sprefc"], ["PD"])
        E("v", lambda e: e.tensor_tensor(
            tmpA[:, :].rearrange("p (h d) -> p h d", h=H),
            PD[:, :].rearrange("p (h d) -> p h d", h=H),
            a_t[:, :].broadcast_to([128, H, DH]), ALU.mult),
          ["PD", "a_t"], ["tmpA"])
        E("v", lambda e: e.tensor_tensor(tmpA[:, :], tmpA[:, :], PC[:, :], ALU.add),
          ["tmpA", "PC"], ["tmpA"])
        if dbg:
            dma_g(dbg_o[t * 128:(t + 1) * 128, :], tmpA[:, :], ["tmpA"], [("dbg_o", t)], chain=("dbgO",))
        E("s", lambda e: e.activation(bt1[p][:, :], tmpA[:, :], AF.Square,
                                      accum_out=sqrow[:, :]),
          ["tmpA"], ["sqrow", b1k])
        E("v", lambda e: e.tensor_scalar_mul(sqrow[:, :], sqrow[:, :], 1.0 / D),
          ["sqrow"], ["sqrow"])
        E("s", lambda e: e.activation(rstd[:, :], sqrow[:, :], AF.Ln, bias=cst[:, 1:2]),
          ["sqrow", "cst"], ["rstd"])
        E("s", lambda e: e.activation(rstd[:, :], rstd[:, :], AF.Exp, scale=-0.5),
          ["rstd"], ["rstd"])
        if dbg:
            dma_g(dbg_m2[t * 128:(t + 1) * 128, :], tmpA[:, :], ["tmpA"], [("dbg_m2", t)])
        E("v", lambda e: e.tensor_scalar(tmpA[:, :], tmpA[:, :], rstd[:, 0:1], None, ALU.mult),
          ["tmpA", "rstd"], ["tmpA"])
        E("v", lambda e: e.tensor_tensor(tmpA[:, :], tmpA[:, :], nw_rep[:, :], ALU.mult),
          ["tmpA", "rep0"], ["tmpA"])
        if dbg:
            dma_g(dbg_m[t * 128:(t + 1) * 128, :], tmpA[:, :], ["tmpA"], [("dbg_m", t)])
            dma_g(dbg_s[t * 128:(t + 1) * 128, :], gsig[:, :], ["gsig"], [("dbg_s", t)])
        E("v", lambda e: e.tensor_tensor(bt1[p][:, :], tmpA[:, :], gsig[:, :], ALU.mult),
          ["tmpA", "gsig"], [b1k])
        if dbg:
            dma_g(dbg_g[t * 128:(t + 1) * 128, :], bt1[p][:, :], [b1k], [("dbg_g", t)], chain=("dbgg",))
            dma_g(dbg_r[t * 128:(t + 1) * 128], rstd[:, :], ["rstd"], [("dbg_r", t)], chain=("dbgr",))
        emit_tp8(bt1[p], b1k, ht[p], hk, doff=2048)
        emit_proj(ht[p][:, 2048:3072], hk, *P2W["a0_Wo"], PD, "PD", D)
        E("v", lambda e: e.tensor_tensor(yt[p][:, :], PD[:, :], xt[p][:, :], ALU.add),
          ["PD", ("xt", p)], [("yt", p)])
        dma_g(partial_s[t * 128:(t + 1) * 128, :], yt[p][:, :], [("yt", p)], [("partial_s", t)])

    for t in range(nT):
        p2_tile(t)
    sch.rekey([("PA", 0), ("PA", 1), ("PA", 2), ("PA", 3),
               ("PB", 0), ("PB", 1), ("PB", 2), ("PB", 3)], ["PA", "PB"])

    NF0 = F0 // 512

    def load_f0_pair(i6):
        half = (i6 % 2) * 8192
        gv_ = arA[:, half:half + 4096].rearrange("p (c n) -> p c n", c=8)
        gk_ = ("arA", (i6 % 2) * 2)
        uv_ = arA[:, half + 4096:half + 8192].rearrange("p (c n) -> p c n", c=8)
        uk_ = ("arA", (i6 % 2) * 2 + 1)
        for c in range(8):
            tp_q(gv_[:, c, :], stg["f0_wg"][i6 * 512:(i6 + 1) * 512, c * 128:(c + 1) * 128],
                 ["stg_f0_wg"], [gk_], chain=(gk_[0],))
            tp_q(uv_[:, c, :], stg["f0_wu"][i6 * 512:(i6 + 1) * 512, c * 128:(c + 1) * 128],
                 ["stg_f0_wu"], [uk_], chain=(uk_[0],))
        return (gv_, gk_), (uv_, uk_)

    def p3a_tile(t, i6, gv, gk, uv, uk):
        p = t % 2
        b2k = ("bt2", p)
        dma_q(bt2[p][:, :].rearrange("p (a n) -> p a n", a=8),
              xnfm_s[:, :, t * 128:(t + 1) * 128].rearrange("a p n -> p a n"),
              [("xnfm_s", t)], [b2k])
        emit_proj(bt2[p], b2k, gv, gk, PC, "PC", 512)
        emit_proj(bt2[p], b2k, uv, uk, PD, "PD", 512)
        E("s", lambda e: e.activation(sg32[:, :], PC[:, 0:512], AF.Silu, scale=INV127),
          ["PC"], ["sg32"])
        E("v", lambda e: e.tensor_tensor(bt1[p][:, 0:512], sg32[:, :], PD[:, 0:512],
                                         ALU.mult), ["sg32", "PD"], [("bt1", p)])
        dma_g(h0_s[t * 128:(t + 1) * 128, i6 * 512:(i6 + 1) * 512], bt1[p][:, 0:512],
              [("bt1", p)], [("h0_s", t, i6)])

    f0v = {0: load_f0_pair(0)}
    for i6 in range(NF0):
        if i6 + 1 < NF0:
            f0v[i6 + 1] = load_f0_pair(i6 + 1)
        (gv, gk), (uv, uk) = f0v.pop(i6)
        for t in range(nT):
            p3a_tile(t, i6, gv, gk, uv, uk)

    w3d_view = arB[:, 0:F0 // 128 * 1024].rearrange("p (c n) -> p c n", c=F0 // 128)
    w3d_keys = ("arB", 0)
    for c in range(F0 // 128):
        tp_q(w3d_view[:, c, :], stg["f0_wd"][:, c * 128:(c + 1) * 128],
             ["stg_f0_wd"], [("arB", 0), ("arB", 1), ("arB", 2)], chain=("arB",))
    def p3b_tile(t):
        p = t % 2
        hk = ("ht", p)
        dma_q(ht[p][:, :], h0_s[t * 128:(t + 1) * 128, :],
              [("h0_s", t, i) for i in range(6)], [hk])
        dma_q(yt[p][:, :], partial_s[t * 128:(t + 1) * 128, :], [("partial_s", t)], [("yt", p)])
        for r in range(3):
            for b in range(8):
                bb = r * 8 + b
                E("t", lambda e, bb=bb, b=b: e.transpose(
                    PT[:, b * 128:(b + 1) * 128], ht[p][:, bb * 128:(bb + 1) * 128], id_b[:, :]),
                  [hk, "id_b"], ["PT"])
            E("v", lambda e, r=r: e.tensor_copy(hfm[:, r * 1024:(r + 1) * 1024], PT[:, :]),
              ["PT"], ["hfm"])
        for c in range(F0 // 128):
            for j in range(2):
                E("t", lambda e, c=c, j=j: e.matmul(
                    PC[:, j * 512:(j + 1) * 512], hfm[:, c * 128:(c + 1) * 128],
                    w3d_view[:, c, j * 512:(j + 1) * 512],
                    start=(c == 0), stop=(c == F0 // 128 - 1)),
                  ["hfm", w3d_keys], ["PC"])
        E("v", lambda e: e.scalar_tensor_tensor(xt[p][:, :], PC[:, :], INV127 * INV127,
                                                yt[p][:, :], ALU.mult, ALU.add),
          ["PC", ("yt", p)], [("xt", p)])
        dma_g(xnew_s[t * 128:(t + 1) * 128, :], xt[p][:, :], [("xt", p)], [("xnew_s", t)])

    for t in range(nT):
        p3b_tile(t)
    if upto == "l0":
        for t in range(nT):
            dma_g(out_ext[t * 128:(t + 1) * 128, :], xnew_s[t * 128:(t + 1) * 128, :],
                  [("xnew_s", t)], [("out", t)])
        sch.finalize()
        return nc

    RG8 = [[0, 1, 2, 3, 4, 5, 6, 7]]
    rows3 = [(0, 128), (128, 256), (256, RG1)]
    for (src_, dst, keyd) in [(wsh_g, ccw_in_g, "ccw_in_g"), (wsh_u, ccw_in_u, "ccw_in_u")]:
        for (r0, r1) in rows3:
            n = r1 - r0
            for j2 in range(2):
                E("q", lambda e, src_=src_, r0=r0, r1=r1, j2=j2: e.dma_start(
                    out=qtmp[0:r1 - r0, :], in_=src_[r0:r1, j2 * 512:(j2 + 1) * 512]),
                    ["qtmp"], ["qtmp"], dma=True)
                emit_fp4(qtmp, "qtmp", bt4[0], ("bt4", 0), 512)
                E("gd", lambda e, dst=dst, r0=r0, n=n, j2=j2: e.dma_start(
                    out=dst[r0:r0 + n, j2 * 512:(j2 + 1) * 512], in_=bt4[0][0:n, 0:512]),
                    [("bt4", 0)], [keyd], dma=True)
    W6 = [512] * 5 + [256]
    for i6 in range(6):
        w6 = W6[i6]
        r0 = 512 * i6
        E("q", lambda e, r0=r0, w6=w6: e.dma_start(
            out=qtmp[0:RGD, 0:w6], in_=wsh_d[:, r0:r0 + w6]), ["qtmp"], ["qtmp"], dma=True)
        emit_fp4(qtmp, "qtmp", bt4[0], ("bt4", 0), w6)
        E("gd", lambda e, r0=r0, w6=w6: e.dma_start(
            out=ccw_in_d[:, r0:r0 + w6], in_=bt4[0][0:RGD, 0:w6]),
            [("bt4", 0)], ["ccw_in_d"], dma=True)
    E("g", lambda e: e.collective_compute("AllGather", ALU.bypass, replica_groups=RG8,
                                          ins=[ccw_in_g[:, :].opt()], outs=[wgq_s[:, :].opt()]),
      ["ccw_in_g"], ["wgq_s"])
    E("g", lambda e: e.collective_compute("AllGather", ALU.bypass, replica_groups=RG8,
                                          ins=[ccw_in_u[:, :].opt()], outs=[wuq_s[:, :].opt()]),
      ["ccw_in_u"], ["wuq_s"])
    E("g", lambda e: e.collective_compute("AllGather", ALU.bypass, replica_groups=RG8,
                                          ins=[ccw_in_d[:, :].opt()], outs=[wdq_s[:, :].opt()]),
      ["ccw_in_d"], ["wdq_s"])
    for (wnm, dst_s, dkey, bsb, bkey, bdst, bdkey) in [
            ("a1_Wq", qr_s, "qr_s", bq_sb, "bq_sb", bqr_s, "bqr_s"),
            ("a1_Wk", kr_s, "kr_s", bk_sb, "bk_sb", bkr_s, "bkr_s")]:
        for half in range(2):
            for hh in range(8):
                h = half * 8 + hh
                par, hp = h % 2, h // 2
                E("t", lambda e, par=par, hp=hp, hh=hh, bsb=bsb: e.matmul(
                    PB[0:1, hh * 64:(hh + 1) * 64],
                    bsb[par * 64:par * 64 + 64, hp:hp + 1],
                    rw_sb[par * 64:par * 64 + 64, hp * 64:(hp + 1) * 64],
                    start=True, stop=True),
                  [bkey, "rw_sb"], ["PB"])
            E("v", lambda e: e.tensor_copy(tmpA[0:1, 0:512], PB[0:1, 0:512]), ["PB"], ["tmpA"])
            E("gd", lambda e, bdst=bdst, half=half: e.dma_start(
                out=bdst[half * 512:(half + 1) * 512], in_=tmpA[0:1, 0:512]),
              ["tmpA"], [bdkey], dma=True)
        for co in range(8):
            wp = 0
            wk = ("wstg", wp)
            tp_q(wstg[wp][:, :], stg[wnm][:, co * 128:(co + 1) * 128], ["stg_" + wnm], [wk],
                 chain=("wstg",))
            for hh in range(2):
                h = co * 2 + hh
                par, hp = h % 2, h // 2
                for ci in range(8):
                    E("t", lambda e, par=par, hp=hp, ci=ci, wp=wp: e.matmul(
                        PA[:, ci * 64:(ci + 1) * 64],
                        wstg[wp][par * 64:par * 64 + 64, ci * 128:(ci + 1) * 128],
                        rw_sb[par * 64:par * 64 + 64, hp * 64:(hp + 1) * 64],
                        start=True, stop=True),
                      [wk, "rw_sb"], ["PA"])
                E("v", lambda e: e.tensor_copy(qrstg[:, :], PA[:, :]), ["PA"], ["qrstg"])
                E("gd", lambda e, h=h, dst_s=dst_s: e.dma_start(
                    out=dst_s[:, h * 64:(h + 1) * 64].rearrange("(c p) f -> p c f", c=8),
                    in_=qrstg[:, :].rearrange("p (c f) -> p c f", c=8)),
                  ["qrstg"], [(dkey, h)], dma=True)
    dma_g(bqr_rep[:, :], bqr_s.partition_broadcast(128), ["bqr_s"], ["rep1"])
    dma_g(bkr_rep[:, :], bkr_s.partition_broadcast(128), ["bkr_s"], ["rep2"])
    dma_g(bv_rep[:, :], W["a1_bv"].partition_broadcast(128), [], ["rep0"])
    qrv, qrk = arA_view(0)
    krv, krk = arA_view(1)
    wvv, wvk = arA_view(2)
    for c in range(8):
        E("q", lambda e, c=c: e.dma_start(out=qrv[:, c, :], in_=qr_s[c * 128:(c + 1) * 128, :]),
          [(("qr_s", hx)) for hx in range(H)], [qrk], dma=True, chain=(qrk[0],))
        E("q", lambda e, c=c: e.dma_start(out=krv[:, c, :], in_=kr_s[c * 128:(c + 1) * 128, :]),
          [(("kr_s", hx)) for hx in range(H)], [krk], dma=True, chain=(krk[0],))
        dma_g(wvv[:, c, :], W["a1_Wv"][c * 128:(c + 1) * 128, :], [], [wvk], chain=(wvk[0],))
    wo1v, wo1k = arB_view(0)
    for c in range(8):
        dma_g(wo1v[:, c, :], W["a1_Wo"][c * 128:(c + 1) * 128, :], [], [wo1k], chain=(wo1k[0],))

    def p4_tile(t):
        p = t % 2
        b1k, b2k, b3k, b4k = ("bt1", p), ("bt2", p), ("bt3", p), ("bt4", p)
        dma_q(xt[p][:, :], xnew_s[t * 128:(t + 1) * 128, :], [("xnew_s", t)], [("xt", p)])
        emit_ln(xt[p], ("xt", p), bt1[p], b1k, dst2=yt[p], d2k=("yt", p))
        dma_g(xn1_s[t * 128:(t + 1) * 128, :], yt[p][:, :], [("yt", p)], [("xn1_s", t)])
        emit_tp8(bt1[p], b1k, bt2[p], b2k)
        emit_proj(bt2[p], b2k, qrv, qrk, PC, "PC", D)
        emit_proj(bt2[p], b2k, krv, krk, PD, "PD", D)
        E("v", lambda e: e.tensor_tensor(tmpA[:, :], PC[:, :], bqr_rep[:, :], ALU.add),
          ["PC", "rep1"], ["tmpA"])
        E("v", lambda e: e.tensor_scalar(bt3[p][:, :], tmpA[:, :], 0.0, L1_EPS,
                                         ALU.max, ALU.add), ["tmpA"], [b3k])
        E("v", lambda e: e.tensor_tensor(tmpA[:, :], PD[:, :], bkr_rep[:, :], ALU.add),
          ["PD", "rep2"], ["tmpA"])
        E("v", lambda e: e.tensor_scalar(bt4[p][:, :], tmpA[:, :], 0.0, L1_EPS,
                                         ALU.max, ALU.add), ["tmpA"], [b4k])
        dma_g(qp_s[t * 128:(t + 1) * 128, :], bt3[p][:, :], [b3k], [("qp_s", t)])
        emit_proj(bt2[p], b2k, wvv, wvk, PC, "PC", D)
        E("v", lambda e: e.tensor_tensor(bt1[p][:, :], PC[:, :], bv_rep[:, :], ALU.add),
          ["PC", "rep0"], [b1k])
        for h in range(H):
            par, hp = h % 2, h // 2
            first = (t == 0 and h == 0)
            E("t", lambda e, h=h, par=par, hp=hp, first=first: e.matmul(
                PE[par * 64:par * 64 + 64, hp * 64:(hp + 1) * 64],
                bt4[p][:, h * 64:(h + 1) * 64], bt1[p][:, h * 64:(h + 1) * 64],
                start=first, stop=(t == nT - 1 and h == H - 1),
                tile_position=(0, par * 64), skip_group_check=True),
              [b4k, b1k], ["PE"])
            E("t", lambda e, h=h, par=par, hp=hp, first=first: e.matmul(
                PA[par * 64:par * 64 + 64, hp:hp + 1],
                bt4[p][:, h * 64:(h + 1) * 64], onesb[:, :],
                start=first, stop=(t == nT - 1 and h == H - 1),
                tile_position=(0, par * 64), skip_group_check=True),
              [b4k, "onesb"], ["PA"])

    for t in range(nT):
        p4_tile(t)
    dma_g(bo_rep[:, :], W["a1_bo"].partition_broadcast(128), [], ["rep1"])

    E("v", lambda e: e.tensor_copy(tmpA[:, 0:512], PE[:, :]), ["PE"], ["tmpA"])
    dma_g(cc2_in_kv[:, :], tmpA[:, 0:512], ["tmpA"], ["cc2_in_kv"])
    E("v", lambda e: e.tensor_copy(lam[:, 0:8], PA[:, 0:8]), ["PA"], ["lam"])
    dma_g(cc2_in_kp[:, :], lam[:, 0:8], ["lam"], ["cc2_in_kp"])
    E("g", lambda e: e.collective_compute("AllReduce", ALU.add, replica_groups=RGB,
                                          ins=[cc2_in_kv[:, :].opt()],
                                          outs=[cc2_out_kv[:, :].opt()]),
      ["cc2_in_kv"], ["cc2_out_kv"])
    E("g", lambda e: e.collective_compute("AllReduce", ALU.add, replica_groups=RGB,
                                          ins=[cc2_in_kp[:, :].opt()],
                                          outs=[cc2_out_kp[:, :].opt()]),
      ["cc2_in_kp"], ["cc2_out_kp"])
    dma_g(kvg_sb[:, :], cc2_out_kv[:, :], ["cc2_out_kv"], ["kvg_sb"])
    dma_g(kpg_sb[:, :], cc2_out_kp[:, :], ["cc2_out_kp"], ["kpg_sb"])

    def p5_tile(t):
        p = t % 2
        b1k, b3k = ("bt1", p), ("bt3", p)
        hk = ("ht", p)
        dma_q(bt3[p][:, :], qp_s[t * 128:(t + 1) * 128, :], [("qp_s", t)], [b3k])
        dma_q(xt[p][:, :], xnew_s[t * 128:(t + 1) * 128, :], [("xnew_s", t)], [("xt", p)])
        emit_tp8(bt3[p], b3k, ht[p], hk, doff=0)
        for h in range(H):
            par, hp = h % 2, h // 2
            E("t", lambda e, h=h, par=par, hp=hp: e.matmul(
                PC[:, h * 64:(h + 1) * 64],
                ht[p][par * 64:par * 64 + 64, hp * 128:(hp + 1) * 128],
                kvg_sb[par * 64:par * 64 + 64, hp * 64:(hp + 1) * 64],
                start=True, stop=True),
              [hk, "kvg_sb"], ["PC"])
            E("t", lambda e, h=h, par=par, hp=hp: e.matmul(
                PB[:, h:h + 1],
                ht[p][par * 64:par * 64 + 64, hp * 128:(hp + 1) * 128],
                kpg_sb[par * 64:par * 64 + 64, hp:hp + 1],
                start=True, stop=True),
              [hk, "kpg_sb"], ["PB"])
        E("v", lambda e: e.tensor_scalar_add(lam[:, :], PB[:, 0:16], L1_EPS), ["PB"], ["lam"])
        E("v", lambda e: e.reciprocal(ek[:, :], lam[:, :]), ["lam"], ["ek"])
        E("v", lambda e: e.tensor_tensor(
            bt1[p][:, :].rearrange("p (h d) -> p h d", h=H),
            PC[:, :].rearrange("p (h d) -> p h d", h=H),
            ek[:, :].broadcast_to([128, H, DH]), ALU.mult),
          ["PC", "ek"], [b1k])
        emit_tp8(bt1[p], b1k, ht[p], hk, doff=1024)
        emit_proj(ht[p][:, 1024:2048], hk, wo1v, wo1k, PD, "PD", D)
        E("v", lambda e: e.tensor_tensor(tmpA[:, :], PD[:, :], bo_rep[:, :], ALU.add),
          ["PD", "rep1"], ["tmpA"])
        E("v", lambda e: e.tensor_tensor(yt[p][:, :], tmpA[:, :], xt[p][:, :], ALU.add),
          ["tmpA", ("xt", p)], [("yt", p)])
        dma_g(partial2_s[t * 128:(t + 1) * 128, :], yt[p][:, :], [("yt", p)],
              [("partial2_s", t)])

    for t in range(nT):
        p5_tile(t)
    W6a = [512] * 5 + [256]

    def load_f1_pair(i6):
        w6 = W6a[i6]
        half = (i6 % 2) * 8192
        gv_ = arA[:, half:half + 8 * w6].rearrange("p (c n) -> p c n", c=8)
        gk_ = ("arA", (i6 % 2) * 2)
        uv_ = arA[:, half + 8 * w6:half + 16 * w6].rearrange("p (c n) -> p c n", c=8)
        uk_ = ("arA", (i6 % 2) * 2 + 1)
        r0 = 512 * i6
        for c in range(8):
            tp_q(gv_[:, c, :], wgq_s[r0:r0 + w6, c * 128:(c + 1) * 128], ["wgq_s"], [gk_],
                 chain=(gk_[0],))
            tp_q(uv_[:, c, :], wuq_s[r0:r0 + w6, c * 128:(c + 1) * 128], ["wuq_s"], [uk_],
                 chain=(uk_[0],))
        return (gv_, gk_), (uv_, uk_)

    def p6a_tile(t, i6, gv, gk, uv, uk, w6, r0):
        p = t % 2
        b1k, b2k, b4k = ("bt1", p), ("bt2", p), ("bt4", p)
        if i6 == 0:
            dma_q(xt[p][:, :], xn1_s[t * 128:(t + 1) * 128, :], [("xn1_s", t)], [("xt", p)])
            emit_fp4(xt[p][:, 0:512], ("xt", p), bt1[p][:, 0:512], b1k, 512)
            emit_fp4(xt[p][:, 512:1024], ("xt", p), bt1[p][:, 512:1024], b1k, 512)
            emit_tp8(bt1[p], b1k, bt2[p], b2k)
            dma_g(xq_s[:, :, t * 128:(t + 1) * 128].rearrange("a p n -> p a n"),
                  bt2[p][:, :].rearrange("p (a n) -> p a n", a=8), [b2k], [("xq_s", t)])
        else:
            dma_q(bt2[p][:, :].rearrange("p (a n) -> p a n", a=8),
                  xq_s[:, :, t * 128:(t + 1) * 128].rearrange("a p n -> p a n"),
                  [("xq_s", t)], [b2k])
        emit_proj(bt2[p], b2k, gv, gk, PC, "PC", w6)
        emit_proj(bt2[p], b2k, uv, uk, PD, "PD", w6)
        E("s", lambda e, w6=w6: e.activation(sg32[:, 0:w6], PC[:, 0:w6], AF.Silu),
          ["PC"], ["sg32"])
        E("v", lambda e, w6=w6: e.tensor_tensor(hq32[:, 0:w6], sg32[:, 0:w6], PD[:, 0:w6],
                                                ALU.mult), ["sg32", "PD"], ["hq32"])
        emit_fp4(hq32, "hq32", bt4[p], b4k, w6)
        dma_g(hq_s[t * 128:(t + 1) * 128, r0:r0 + w6], bt4[p][:, 0:w6], [b4k],
              [("hq_s", t, i6)])

    f1v = {0: load_f1_pair(0)}
    for i6 in range(6):
        if i6 + 1 < 6:
            f1v[i6 + 1] = load_f1_pair(i6 + 1)
        (gv, gk), (uv, uk) = f1v.pop(i6)
        for t in range(nT):
            p6a_tile(t, i6, gv, gk, uv, uk, W6a[i6], 512 * i6)

    w6d_view = arB[:, 0:F1 // 128 * 1024].rearrange("p (c n) -> p c n", c=F1 // 128)
    w6d_key = ("arB", 0)
    for c in range(F1 // 128):
        tp_q(w6d_view[:, c, :], wdq_s[:, c * 128:(c + 1) * 128],
             ["wdq_s"], [("arB", 0), ("arB", 1), ("arB", 2)], chain=("arB",))
    def p6b_tile(t):
        p = t % 2
        hk = ("ht", p)
        dma_q(ht[p][:, 0:F1], hq_s[t * 128:(t + 1) * 128, :],
              [("hq_s", t, i) for i in range(6)], [hk])
        dma_q(yt[p][:, :], partial2_s[t * 128:(t + 1) * 128, :], [("partial2_s", t)],
              [("yt", p)])
        nch = F1 // 128
        for r in range((nch + 7) // 8):
            nb = min(8, nch - r * 8)
            for b in range(nb):
                bb = r * 8 + b
                E("t", lambda e, bb=bb, b=b: e.transpose(
                    PT[:, b * 128:(b + 1) * 128], ht[p][:, bb * 128:(bb + 1) * 128], id_b[:, :]),
                  [hk, "id_b"], ["PT"])
            E("v", lambda e, r=r, nb=nb: e.tensor_copy(
                hfm[:, r * 1024:r * 1024 + nb * 128], PT[:, 0:nb * 128]), ["PT"], ["hfm"])
        for c in range(nch):
            for j in range(2):
                E("t", lambda e, c=c, j=j: e.matmul(
                    PC[:, j * 512:(j + 1) * 512], hfm[:, c * 128:(c + 1) * 128],
                    w6d_view[:, c, j * 512:(j + 1) * 512],
                    start=(c == 0), stop=(c == nch - 1)),
                  ["hfm", w6d_key], ["PC"])
        E("v", lambda e: e.tensor_tensor(xt[p][:, :], PC[:, :], yt[p][:, :], ALU.add),
          ["PC", ("yt", p)], [("xt", p)])
        dma_g(out_ext[t * 128:(t + 1) * 128, :], xt[p][:, :], [("xt", p)], [("out", t)])

    for t in range(nT):
        p6b_tile(t)
    sch.finalize()
    return nc


_BUILT = {}


def _inmaps(inputs, T):
    x = np.asarray(inputs["L0_attn_x"], np.float32)
    nm = {
        "a0_Wq": "L0_attn_Wq", "a0_Wk": "L0_attn_Wk", "a0_Wv": "L0_attn_Wv",
        "a0_Wo": "L0_attn_Wo", "a0_Wg": "L0_attn_Wg", "a0_Wog": "L0_attn_Wog",
        "a0_Wd": "L0_attn_Wd", "a0_bd": "L0_attn_bd", "a0_nw": "L0_attn_norm_w",
        "f0_wg": "L0_ffn_w_gate", "f0_wu": "L0_ffn_w_up", "f0_wd": "L0_ffn_w_down",
        "a1_Wq": "L1_attn_Wq", "a1_bq": "L1_attn_bq", "a1_Wk": "L1_attn_Wk",
        "a1_bk": "L1_attn_bk", "a1_Wv": "L1_attn_Wv", "a1_bv": "L1_attn_bv",
        "a1_Wo": "L1_attn_Wo", "a1_bo": "L1_attn_bo", "a1_rw": "L1_attn_rw",
    }
    base = {k: np.ascontiguousarray(np.asarray(inputs[v])) for k, v in nm.items()}
    RG1 = F1 // 8
    RGD = D // 8
    maps = []
    for c in range(8):
        b, qq = c // 4, c % 4
        r = c % 4
        emask = np.zeros((4, 4), np.float32)
        m1 = np.zeros((4,), np.float32)
        for pp in range(4):
            if pp < r:
                m1[pp] = 1.0
                for u in range(pp + 1, r):
                    emask[pp, u] = 1.0
        m = dict(base)
        m["x"] = np.ascontiguousarray(x[b, qq * T:(qq + 1) * T])
        m["emask"] = emask.reshape(-1)
        m["m1"] = m1
        fwg = np.asarray(inputs["L1_ffn_w_gate"], np.float32)
        fwu = np.asarray(inputs["L1_ffn_w_up"], np.float32)
        fwd = np.asarray(inputs["L1_ffn_w_down"], np.float32)
        m["wsh_g"] = np.ascontiguousarray(fwg[c * RG1:(c + 1) * RG1])
        m["wsh_u"] = np.ascontiguousarray(fwu[c * RG1:(c + 1) * RG1])
        m["wsh_d"] = np.ascontiguousarray(fwd[c * RGD:(c + 1) * RGD])
        maps.append(m)
    return maps


def kernel(**inputs):
    T = 4096
    key = ("full", T)
    if key not in _BUILT:
        _BUILT[key] = build(T, "full")
    nc = _BUILT[key]
    maps = _inmaps(inputs, T)
    res = run_bass_kernel_spmd(nc, maps, core_ids=list(range(8)))
    out = np.empty((2, 16384, D), np.float32)
    for c, r in enumerate(res.results):
        out[c // 4, (c % 4) * T:(c % 4 + 1) * T] = r["out"]
    return out


# version 04
# speedup vs baseline: 1.0003x; 1.0003x over previous
import numpy as np
import concourse.bass as bass
import concourse.mybir as mybir
from concourse.bass_utils import run_bass_kernel_spmd

F32 = mybir.dt.float32
BF16 = mybir.dt.bfloat16
I32 = mybir.dt.int32
U8 = mybir.dt.uint8
AF = mybir.ActivationFunctionType
ALU = mybir.AluOpType
AX = mybir.AxisListType

D = 1024
H = 16
DH = 64
F0 = 3072
F1 = 2816
INV127 = 1.0 / 127.0
LN_EPS = 1e-5
RMS_EPS = float(np.finfo(np.float32).eps)
L1_EPS = 1e-6

STREAMS = ("v", "s", "t", "g", "gd", "q", "ad")


class Sched:

    def __init__(self, nc):
        self.nc = nc
        self.csem = {e: nc.alloc_semaphore(f"S{e}") for e in ("v", "s", "t", "g")}
        self.ccum = {e: 0 for e in ("v", "s", "t", "g")}
        self.dsem = {}
        self.dcum = {}
        self.thunks = {e: [] for e in ("v", "s", "t", "g", "q")}
        self.last_write = {}
        self.readers = {}
        self.hiwater = {e: {} for e in ("v", "s", "t", "g", "q")}

    def _chain_sem(self, chain):
        if chain not in self.dsem:
            self.dsem[chain] = self.nc.alloc_semaphore(f"D{len(self.dsem)}")
            self.dcum[chain] = 0
        return self.dsem[chain]

    def emit(self, stream, fn, reads=(), writes=(), dma=False, chain=None):
        engine = {"gd": "g", "ad": "s"}.get(stream, stream)
        waits = {}

        def need(tok):
            if tok is not None:
                sem, v = tok
                if v > waits.get(sem, 0):
                    waits[sem] = v

        for k in reads:
            need(self.last_write.get(k))
        for k in writes:
            need(self.last_write.get(k))
            for tok in self.readers.get(k, {}).items():
                need((tok[0], tok[1]))
        if dma:
            if chain is None:
                w0 = writes[0]
                chain = (w0[0] if isinstance(w0, tuple) else w0,)
            ck = (stream,) + tuple(chain)
            sem = self._chain_sem(ck)
            prev = self.dcum[ck]
            if prev:
                if prev > waits.get(sem, 0):
                    waits[sem] = prev
            self.dcum[ck] += 16
            mytok = (sem, self.dcum[ck])
        else:
            sem = self.csem[engine]
            self.ccum[engine] += 1
            mytok = (sem, self.ccum[engine])
        hw = self.hiwater[engine]
        wl = []
        own = self.csem.get(engine)
        for s2, v in waits.items():
            if s2 is own and not dma:
                if v >= self.ccum[engine]:
                    continue
            if v > hw.get(s2, 0):
                hw[s2] = v
                wl.append((s2, v))

        nc = self.nc

        def thunk(eng, wl=tuple(wl), fn=fn, sem=sem, inc=(16 if dma else 1), dma=dma):
            for s2, v in wl:
                eng.wait_ge(s2, v)
            n0 = nc.n_instructions()
            fn(eng).then_inc(sem, inc)
            if dma:
                assert nc.n_instructions() == n0 + 1, \
                    f"DMA emit split into {nc.n_instructions() - n0} instructions"

        self.thunks[engine if engine != "g" else "g"].append(thunk)
        if engine == "q":
            self.thunks["q"] = self.thunks["q"]
        for k in writes:
            self.last_write[k] = mytok
            self.readers[k] = {}
        for k in reads:
            rd = self.readers.setdefault(k, {})
            if mytok[1] > rd.get(mytok[0], 0):
                rd[mytok[0]] = mytok[1]
        return mytok

    def rekey(self, old_keys, new_keys):
        self.emit("v", lambda e: e.engine_nop(), reads=list(old_keys), writes=list(new_keys))

    def finalize(self):
        nc = self.nc
        finals = [(self.csem[e], self.ccum[e]) for e in ("v", "s", "t", "g") if self.ccum[e]]
        finals += [(self.dsem[c], self.dcum[c]) for c in self.dsem]
        with nc.allow_low_precision(reason="bf16 matmul operand staging; fp32 accumulation"), \
                nc.Block() as block:
            @block.vector
            def _(e):
                for th in self.thunks["v"]:
                    th(e)

            @block.scalar
            def _(e):
                for th in self.thunks["s"]:
                    th(e)

            @block.tensor
            def _(e):
                for th in self.thunks["t"]:
                    th(e)

            @block.sync
            def _(e):
                for th in self.thunks["q"]:
                    th(e)

            @block.gpsimd
            def _(e):
                for th in self.thunks["g"]:
                    th(e)
                for sem, v in finals:
                    e.wait_ge(sem, v)


def build(T=4096, upto="full", dbg=False):
    nT = T // 128
    nc = bass.Bass("TRN2", target_bir_lowering=False)

    def din(name, shape, dt=F32):
        return nc.dram_tensor(name, shape, dt, kind="ExternalInput").ap()

    x_in = din("x", [T, D])
    W = {}
    for nm in ["Wq", "Wk", "Wv", "Wo", "Wg", "Wog"]:
        W["a0_" + nm] = din("a0_" + nm, [D, D])
    W["a0_Wd"] = din("a0_Wd", [H, D])
    W["a0_bd"] = din("a0_bd", [H])
    W["a0_nw"] = din("a0_nw", [D])
    W["f0_wg"] = din("f0_wg", [F0, D], I32)
    W["f0_wu"] = din("f0_wu", [F0, D], I32)
    W["f0_wd"] = din("f0_wd", [D, F0], I32)
    for nm in ["Wq", "Wk", "Wv", "Wo"]:
        W["a1_" + nm] = din("a1_" + nm, [D, D])
    for nm in ["bq", "bk", "bv", "bo"]:
        W["a1_" + nm] = din("a1_" + nm, [D])
    W["a1_rw"] = din("a1_rw", [H, DH, DH])
    RG1 = F1 // 8
    RGD = D // 8
    wsh_g = din("wsh_g", [RG1, D])
    wsh_u = din("wsh_u", [RG1, D])
    wsh_d = din("wsh_d", [RGD, F1])
    emask_in = din("emask", [16])
    m1_in = din("m1", [4])
    out_ext = nc.dram_tensor("out", [T, D], F32, kind="ExternalOutput").ap()

    def dram(name, shape, dt):
        return nc.dram_tensor(name, shape, dt, kind="Internal").ap()

    stg = {}
    for nm in ["a0_Wq", "a0_Wk", "a0_Wv", "a0_Wo", "a0_Wg", "a0_Wog", "a1_Wq", "a1_Wk"]:
        stg[nm] = dram("stg_" + nm, [D, D], BF16)
    stg["f0_wg"] = dram("stg_f0_wg", [F0, D], BF16)
    stg["f0_wu"] = dram("stg_f0_wu", [F0, D], BF16)
    stg["f0_wd"] = dram("stg_f0_wd", [D, F0], BF16)
    qr_s = dram("qr_s", [D, D], BF16)
    kr_s = dram("kr_s", [D, D], BF16)
    bqr_s = dram("bqr_s", [D], F32)
    bkr_s = dram("bkr_s", [D], F32)
    xnfm_s = dram("xnfm_s", [8, 128, T], BF16)
    q_s = dram("q_s", [T, D], BF16)
    kt_s = dram("kt_s", [T, D], BF16)
    v_s = dram("v_s", [T, D], BF16)
    spref_s = dram("spref_s", [nT, 128, 512], BF16)
    partial_s = dram("partial_s", [T, D], F32)
    h0_s = dram("h0_s", [T, F0], BF16)
    xnew_s = dram("xnew_s", [T, D], F32)
    xn1_s = dram("xn1_s", [T, D], F32)
    qp_s = dram("qp_s", [T, D], BF16)
    partial2_s = dram("partial2_s", [T, D], F32)
    xq_s = dram("xq_s", [8, 128, T], BF16)
    hq_s = dram("hq_s", [T, F1], BF16)
    cc1_in_s = dram("cc1_in_s", [128, 512], F32)
    cc1_in_g = dram("cc1_in_g", [1, H], F32)
    cc1_out_s = dram("cc1_out_s", [512, 512], F32)
    cc1_out_g = dram("cc1_out_g", [4, H], F32)
    cc2_in_kv = dram("cc2_in_kv", [128, 512], F32)
    cc2_in_kp = dram("cc2_in_kp", [128, 8], F32)
    cc2_out_kv = dram("cc2_out_kv", [128, 512], F32)
    cc2_out_kp = dram("cc2_out_kp", [128, 8], F32)
    ccw_in_g = dram("ccw_in_g", [RG1, D], BF16)
    ccw_in_u = dram("ccw_in_u", [RG1, D], BF16)
    ccw_in_d = dram("ccw_in_d", [RGD, F1], BF16)
    wgq_s = dram("wgq_s", [F1, D], BF16)
    if dbg:
        dbg_o = dram("dbg_o", [T, D], F32)
        dbg_g = dram("dbg_g", [T, D], BF16)
        dbg_r = dram("dbg_r", [T], F32)
        dbg_m = dram("dbg_m", [T, D], F32)
        dbg_m2 = dram("dbg_m2", [T, D], F32)
        dbg_s = dram("dbg_s", [T, D], BF16)
        dbg_D = dram("dbg_D", [128, 128], F32)
        dbg_gchm = dram("dbg_gchm", [16, 128], F32)
    wuq_s = dram("wuq_s", [F1, D], BF16)
    wdq_s = dram("wdq_s", [D, F1], BF16)

    S = nc.alloc_sbuf_tensor
    ones_f = S("ones_f", [128, 128], F32)
    ltri_f = S("ltri_f", [128, 128], F32)
    id_b = S("id_b", [128, 128], BF16)
    id_f = S("id_f", [128, 128], F32)
    onehot = S("onehot", [16, 16], F32)
    onesb = S("onesb", [128, 1], BF16)
    cst = S("cst", [128, 4], F32)
    bd_rep = S("bd_rep", [128, H], F32)
    rep = [S(f"rep{i}", [128, D], F32) for i in range(3)]
    nw_rep, bqr_rep, bkr_rep = rep[0], rep[1], rep[2]
    bv_rep = rep[0]
    bo_rep = rep[1]
    emask_rep = S("emask_rep", [128, 16], F32)
    m1_rep = S("m1_rep", [128, 4], F32)
    rw_sb = S("rw_sb", [128, 8 * DH], BF16)
    bq_sb = S("bq_sb", [128, 8], BF16)
    bk_sb = S("bk_sb", [128, 8], BF16)
    wdT_sb = S("wdT_sb", [128, 8 * 16], BF16)
    arA = S("arA", [128, 24576], BF16)
    arB = S("arB", [128, 24576], BF16)
    S_run = S("S_run", [128, 512], F32)
    G_run = S("G_run", [128, H], F32)
    Gc_st = S("Gc_st", [128, nT * H], F32)
    Gpre_st = S("Gpre_st", [128, nT * H], F32)
    S_in = S("S_in", [128, 512], F32)
    gall = S("gall", [128, 4 * H], F32)
    ccoef = S("ccoef", [128, 64], F32)
    cfar = S("cfar", [128, 32], F32)
    kvg_sb = S("kvg_sb", [128, 512], BF16)
    kpg_sb = S("kpg_sb", [128, 8], BF16)
    xt = [S(f"xt{i}", [128, D], F32) for i in range(2)]
    yt = [S(f"yt{i}", [128, D], F32) for i in range(2)]
    bt1 = [S(f"bt1_{i}", [128, D], BF16) for i in range(2)]
    bt2 = [S(f"bt2_{i}", [128, D], BF16) for i in range(2)]
    bt3 = [S(f"bt3_{i}", [128, D], BF16) for i in range(2)]
    bt4 = [S(f"bt4_{i}", [128, D], BF16) for i in range(2)]
    sprefb = [S(f"sprefb{i}", [128, 512], BF16) for i in range(2)]
    sprefcL = [S(f"sprefc{i}", [128, 512], BF16) for i in range(2)]
    ht = [S(f"ht{i}", [128, F0], BF16) for i in range(2)]
    hfm = S("hfm", [128, F0], BF16)
    mrowL = [S(f"mrow{i}", [128, 1], F32) for i in range(2)]
    sqrowL = [S(f"sqrow{i}", [128, 1], F32) for i in range(2)]
    vrowL = [S(f"vrow{i}", [128, 1], F32) for i in range(1)]
    rstdL = [S(f"rstd{i}", [128, 1], F32) for i in range(2)]
    tmpA = S("tmpA", [128, D], F32)
    gsigL = [S(f"gsig{i}", [128, D], BF16) for i in range(2)]
    negGcL = [S(f"negGc{i}", [128, H], F32) for i in range(2)]
    a_tL = [S(f"a_t{i}", [128, H], F32) for i in range(2)]
    zrowL = [S(f"zrow{i}", [128, H], F32) for i in range(1)]
    azrowL = [S(f"azrow{i}", [128, H], F32) for i in range(1)]
    erowL = [S(f"erow{i}", [128, H], F32) for i in range(1)]
    lrowL = [S(f"lrow{i}", [128, H], F32) for i in range(1)]
    gdecL = [S(f"gdec{i}", [128, H], F32) for i in range(2)]
    csumL = [S(f"csum{i}", [128, H], F32) for i in range(2)]
    lamL = [S(f"lam{i}", [128, H], F32) for i in range(2)]
    lam_arL = [S(f"lam_ar{i}", [128, 8], F32) for i in range(2)]
    ekL = [S(f"ek{i}", [128, H], F32) for i in range(2)]
    gchmL = [S(f"gchm{i}", [16, 128], F32) for i in range(2)]
    Dt = [S(f"Dt{i}", [128, 128], BF16) for i in range(2)]
    At = [S(f"At{i}", [128, 128], BF16) for i in range(2)]
    qxnL = [S(f"qxn{i}", [128, 512], F32) for i in range(1)]
    qtmpL = [S(f"qtmp{i}", [128, 512], F32) for i in range(1)]
    qi1L = [S(f"qi1{i}", [128, 512], I32) for i in range(1)]
    qaL = [S(f"qa{i}", [128, 512], F32) for i in range(1)]
    qbL = [S(f"qb{i}", [128, 512], F32) for i in range(1)]
    qazL = [S(f"qaz{i}", [128, 512], F32) for i in range(1)]
    qm1L = [S(f"qm1{i}", [128, 512], U8) for i in range(1)]
    qm2L = [S(f"qm2{i}", [128, 512], U8) for i in range(1)]
    qbmL = [S(f"qbm{i}", [128, 32], F32) for i in range(1)]
    qscL = [S(f"qsc{i}", [128, 32], F32) for i in range(1)]
    qrcL = [S(f"qrc{i}", [128, 32], F32) for i in range(1)]
    hq32L = [S(f"hq32{i}", [128, 512], F32) for i in range(1)]
    sg32L = [S(f"sg32{i}", [128, 512], F32) for i in range(1)]
    wstg = [S("wstg0", [128, 1024], BF16)]
    qrstg = S("qrstg", [128, 512], BF16)

    P = nc.alloc_psum_tensor
    PC = P("PC", [128, 1024], F32)
    PD = P("PD", [128, 1024], F32)
    PA = P("PA", [128, 512], F32)
    PB = P("PB", [128, 512], F32)
    PE = P("PE", [128, 512], F32)
    PT = P("PT", [128, 1024], BF16)

    sch = Sched(nc)
    E = sch.emit

    def dma_q(dst, src, reads, writes, chain=None):
        E("q", lambda e: e.dma_start(out=dst, in_=src), reads=reads, writes=writes, dma=True,
          chain=chain)

    def dma_a(dst, src, reads, writes, chain=None):
        E("gd", lambda e: e.dma_start(out=dst, in_=src), reads=reads, writes=writes,
          dma=True, chain=chain)

    def dma_g(dst, src, reads, writes, slow=False, chain=None):
        E("gd", lambda e: e.dma_start(out=dst, in_=src, allow_slow_non_contiguous=slow),
          reads=reads, writes=writes, dma=True, chain=chain)

    def tp_q(dst, src, reads, writes, chain=None):
        E("q", lambda e: e.dma_start_transpose(dst, src), reads=reads, writes=writes, dma=True,
          chain=chain)

    def emit_ln(src, sk, dst, dk, p=0, dst2=None, d2k=None):
        E("v", lambda e: e.reduce_sum(mrowL[p][:, :], src[:, :], axis=AX.X), [sk], [("mrow", 0)])
        E("s", lambda e: e.activation(dst[:, :], src[:, :], AF.Square, accum_out=sqrowL[p][:, :]),
          [sk], [("sqrow", 0), dk])
        E("v", lambda e: e.tensor_scalar_mul(mrowL[p][:, :], mrowL[p][:, :], 1.0 / D), [("mrow", 0)], [("mrow", 0)])
        E("v", lambda e: e.tensor_scalar_mul(sqrowL[p][:, :], sqrowL[p][:, :], 1.0 / D), [("sqrow", 0)], [("sqrow", 0)])
        E("v", lambda e: e.tensor_tensor(vrowL[0][:, :], mrowL[p][:, :], mrowL[p][:, :], ALU.mult),
          [("mrow", 0)], [("vrow", 0)])
        E("v", lambda e: e.tensor_tensor(vrowL[0][:, :], sqrowL[p][:, :], vrowL[0][:, :], ALU.subtract),
          [("sqrow", 0), ("vrow", 0)], [("vrow", 0)])
        E("s", lambda e: e.activation(rstdL[p][:, :], vrowL[0][:, :], AF.Ln, bias=cst[:, 0:1]),
          [("vrow", 0), "cst"], [("rstd", 0)])
        E("s", lambda e: e.activation(rstdL[p][:, :], rstdL[p][:, :], AF.Exp, scale=-0.5),
          [("rstd", 0)], [("rstd", 0)])
        E("v", lambda e: e.tensor_scalar(dst[:, :], src[:, :], mrowL[p][:, 0:1], rstdL[p][:, 0:1],
                                         ALU.subtract, ALU.mult),
          [sk, ("mrow", 0), ("rstd", 0)], [dk])
        if dst2 is not None:
            E("v", lambda e: e.tensor_scalar(dst2[:, :], src[:, :], mrowL[p][:, 0:1], rstdL[p][:, 0:1],
                                             ALU.subtract, ALU.mult),
              [sk, ("mrow", 0), ("rstd", 0)], [d2k])

    def emit_tp8(src, sk, dst, dk, nblk=8, doff=0):
        for b in range(nblk):
            E("t", lambda e, b=b: e.transpose(PT[:, b * 128:(b + 1) * 128],
                                              src[:, b * 128:(b + 1) * 128], id_b[:, :]),
              [sk, "id_b"], ["PT"])
        E("v", lambda e: e.tensor_copy(dst[:, doff:doff + nblk * 128], PT[:, 0:nblk * 128]),
          ["PT"], [dk])

    def emit_proj(fm, fmk, wview, wk, psum, pk, N, nk=8, wN=None):
        wN = N if wN is None else wN
        nj = (N + 511) // 512
        for c in range(nk):
            for j in range(nj):
                n0, n1 = j * 512, min(N, (j + 1) * 512)
                E("t", lambda e, c=c, n0=n0, n1=n1: e.matmul(
                    psum[:, n0:n1], fm[:, c * 128:(c + 1) * 128], wview[:, c, n0:n1],
                    start=(c == 0), stop=(c == nk - 1)),
                  [fmk, wk], [pk])

    def emit_fp4(src, sk, dst, dk, Wd, p=0):
        nb = Wd // 16
        sv = src[:, 0:Wd].rearrange("p (b i) -> p b i", i=16)
        E("v", lambda e: e.reduce_max(qbmL[p][:, 0:nb], sv, axis=AX.X, apply_absolute_value=True),
          [sk], [("qbm", 0)])
        E("v", lambda e: e.tensor_scalar(qscL[p][:, 0:nb], qbmL[p][:, 0:nb], 1.0 / 6.0, 1e-12,
                                         ALU.mult, ALU.max), [("qbm", 0)], [("qsc", 0)])
        E("v", lambda e: e.reciprocal(qrcL[p][:, 0:nb], qscL[p][:, 0:nb]), [("qsc", 0)], [("qrc", 0)])
        rcb = qrcL[p][:, 0:nb].broadcast_to([128, nb, 16])
        E("v", lambda e: e.tensor_tensor(qxnL[p][:, 0:Wd].rearrange("p (b i) -> p b i", i=16),
                                         sv, rcb, ALU.mult), [sk, ("qrc", 0)], [("qxn", 0)])
        E("v", lambda e: e.tensor_scalar_mul(qi1L[p][:, 0:Wd], qxnL[p][:, 0:Wd], 2.0), [("qxn", 0)], [("qi1", 0)])
        E("v", lambda e: e.tensor_scalar_mul(qaL[p][:, 0:Wd], qi1L[p][:, 0:Wd], 0.5), [("qi1", 0)], [("qa", 0)])
        E("v", lambda e: e.tensor_copy(qi1L[p][:, 0:Wd], qxnL[p][:, 0:Wd]), [("qxn", 0)], [("qi1", 0)])
        E("g", lambda e: e.tensor_scalar(qbL[p][:, 0:Wd], qi1L[p][:, 0:Wd], 4.0, -4.0, ALU.min, ALU.max),
          [("qi1", 0)], [("qb", 0)])
        E("s", lambda e: e.activation(qazL[p][:, 0:Wd], qxnL[p][:, 0:Wd], AF.Abs),
          [("qxn", 0)], [("qaz", 0)])
        E("g", lambda e: e.tensor_single_scalar(qm1L[p][:, 0:Wd], qazL[p][:, 0:Wd], 2.25, ALU.is_lt),
          [("qaz", 0)], [("qm1", 0)])
        E("g", lambda e: e.tensor_single_scalar(qm2L[p][:, 0:Wd], qazL[p][:, 0:Wd], 5.0, ALU.is_lt),
          [("qaz", 0)], [("qm2", 0)])
        E("g", lambda e: e.tensor_single_scalar(qazL[p][:, 0:Wd], qxnL[p][:, 0:Wd], 0.0, ALU.is_ge),
          [("qxn", 0), ("qm1", 0), ("qm2", 0)], [("qaz", 0)])
        E("g", lambda e: e.tensor_scalar(qazL[p][:, 0:Wd], qazL[p][:, 0:Wd], 12.0, -6.0,
                                         ALU.mult, ALU.add), [("qaz", 0)], [("qaz", 0)])
        E("v", lambda e: e.select(qtmpL[p][:, 0:Wd], qm2L[p][:, 0:Wd], qbL[p][:, 0:Wd], qazL[p][:, 0:Wd],
                                  add_drain=True),
          [("qm2", 0), ("qb", 0), ("qaz", 0)], [("qtmp", 0)])
        E("v", lambda e: e.select(qbL[p][:, 0:Wd], qm1L[p][:, 0:Wd], qaL[p][:, 0:Wd], qtmpL[p][:, 0:Wd],
                                  add_drain=True),
          [("qm1", 0), ("qa", 0), ("qtmp", 0)], [("qb", 0)])
        scb = qscL[p][:, 0:nb].broadcast_to([128, nb, 16])
        E("v", lambda e: e.tensor_tensor(dst[:, 0:Wd].rearrange("p (b i) -> p b i", i=16),
                                         qbL[p][:, 0:Wd].rearrange("p (b i) -> p b i", i=16),
                                         scb, ALU.mult), [("qb", 0), ("qsc", 0)], [dk])

    def arA_view(slot, width=8192, cdim=8):
        off = slot * 8192
        return arA[:, off:off + width].rearrange("p (c n) -> p c n", c=cdim), ("arA", slot)

    def arB_view(slot, width=8192, cdim=8):
        off = slot * 8192
        return arB[:, off:off + width].rearrange("p (c n) -> p c n", c=cdim), ("arB", slot)

    E("g", lambda e: e.memset(ones_f[:, :], 1.0), [], ["ones_f"])
    E("g", lambda e: e.memset(ltri_f[:, :], 1.0), [], ["ltri_f"])
    E("g", lambda e: e.affine_select(ltri_f[:, :], ltri_f[:, :], [[1, 128]], ALU.is_ge, 0.0,
                                     base=0, channel_multiplier=-1), ["ltri_f"], ["ltri_f"])
    E("g", lambda e: e.memset(id_b[:, :], 1.0), [], ["id_b"])
    E("g", lambda e: e.affine_select(id_b[:, :], id_b[:, :], [[1, 128]], ALU.is_equal, 0.0,
                                     base=0, channel_multiplier=-1), ["id_b"], ["id_b"])
    E("g", lambda e: e.memset(id_f[:, :], 1.0), [], ["id_f"])
    E("g", lambda e: e.affine_select(id_f[:, :], id_f[:, :], [[1, 128]], ALU.is_equal, 0.0,
                                     base=0, channel_multiplier=-1), ["id_f"], ["id_f"])
    E("g", lambda e: e.memset(onehot[:, :], 1.0), [], ["onehot"])
    E("g", lambda e: e.affine_select(onehot[:, :], onehot[:, :], [[1, 16]],
                                     ALU.is_equal, 0.0, base=0, channel_multiplier=-1),
      ["onehot"], ["onehot"])
    E("g", lambda e: e.memset(onesb[:, :], 1.0), [], ["onesb"])
    E("g", lambda e: e.memset(cst[:, 0:1], LN_EPS), [], ["cst"])
    E("g", lambda e: e.memset(cst[:, 1:2], RMS_EPS), ["cst"], ["cst"])
    E("g", lambda e: e.memset(cst[:, 2:3], 1.0), ["cst"], ["cst"])
    E("g", lambda e: e.memset(cst[:, 3:4], L1_EPS), ["cst"], ["cst"])
    dma_g(bd_rep[:, :], W["a0_bd"].partition_broadcast(128), [], ["bd_rep"])
    dma_g(nw_rep[:, :], W["a0_nw"].partition_broadcast(128), [], ["rep0"])
    dma_g(emask_rep[:, :], emask_in.partition_broadcast(128), [], ["emask_rep"])
    dma_g(m1_rep[:, :], m1_in.partition_broadcast(128), [], ["m1_rep"])
    dma_g(rw_sb[:, :].rearrange("p (a f) -> p a f", a=8),
          W["a1_rw"].rearrange("(a b) d f -> (b d) a f", b=2), [], ["rw_sb"], slow=True)
    dma_g(bq_sb[:, :], W["a1_bq"].rearrange("(a b d) -> (b d) a", a=8, b=2), [], ["bq_sb"],
          slow=True)
    dma_g(bk_sb[:, :], W["a1_bk"].rearrange("(a b d) -> (b d) a", a=8, b=2), [], ["bk_sb"],
          slow=True)
    for nm in ["a0_Wq", "a0_Wk", "a0_Wg", "a0_Wv", "a0_Wog", "a0_Wo",
               "f0_wg", "f0_wu", "f0_wd", "a1_Wq", "a1_Wk"]:
        dma_g(stg[nm][:, :], W[nm][:, :], [], ["stg_" + nm])

    P1W = {}
    for i, nm in enumerate(["a0_Wq", "a0_Wk", "a0_Wv"]):
        view, key = arA_view(i)
        P1W[nm] = (view, key)
        for c in range(8):
            tp_q(view[:, c, :], stg[nm][:, c * 128:(c + 1) * 128], ["stg_" + nm], [key], chain=(key[0],))
    vg, kg = arB_view(2)
    P1W["a0_Wg"] = (vg, kg)
    for c in range(8):
        tp_q(vg[:, c, :], stg["a0_Wg"][:, c * 128:(c + 1) * 128], ["stg_a0_Wg"], [kg], chain=(kg[0],))
    dma_g(wstg[0][0:16, :], W["a0_Wd"][:, :], [], ["wstg0"])
    for c in range(8):
        E("t", lambda e, c=c: e.transpose(PT[:, c * 16:(c + 1) * 16],
                                          wstg[0][0:16, c * 128:(c + 1) * 128], id_b[0:16, 0:16]),
          ["wstg0", "id_b"], ["PT"])
    E("v", lambda e: e.tensor_copy(wdT_sb[:, :], PT[:, 0:128]), ["PT"], ["wdT_sb"])
    wdT_view = wdT_sb[:, :].rearrange("p (c n) -> p c n", c=8)
    P2W = {}
    for i, nm in enumerate(["a0_Wog", "a0_Wo"]):
        view, key = arB_view(i)
        P2W[nm] = (view, key)
        for c in range(8):
            tp_q(view[:, c, :], stg[nm][:, c * 128:(c + 1) * 128], ["stg_" + nm], [key], chain=(key[0],))

    E("v", lambda e: e.memset(S_run[:, :], 0.0), [], ["S_run"])
    E("v", lambda e: e.memset(G_run[:, :], 0.0), [], ["G_run"])

    def p1_tile(t):
        p = t % 2
        xk, b1k, b2k, b3k, b4k = ("xt", p), ("bt1", p), ("bt2", p), ("bt3", p), ("bt4", p)
        dma_q(xt[p][:, :], x_in[t * 128:(t + 1) * 128, :], [], [xk])
        emit_ln(xt[p], xk, bt1[p], b1k)
        emit_tp8(bt1[p], b1k, bt2[p], b2k)
        dma_a(xnfm_s[:, :, t * 128:(t + 1) * 128].rearrange("a p n -> p a n"),
              bt2[p][:, :].rearrange("p (a n) -> p a n", a=8), [b2k], [("xnfm_s", t)])
        emit_proj(bt2[p], b2k, *P1W["a0_Wg"], PC, "PC", D)
        for c in range(8):
            E("t", lambda e, c=c: e.matmul(PA[:, 0:16], bt2[p][:, c * 128:(c + 1) * 128],
                                           wdT_view[:, c, :], start=(c == 0), stop=(c == 7)),
              [b2k, "wdT_sb"], ["PA"])
        E("v", lambda e: e.tensor_tensor(zrowL[0][:, :], PA[:, 0:16], bd_rep[:, :], ALU.add),
          ["PA", "bd_rep"], [("zrow", 0)])
        E("s", lambda e: e.activation(azrowL[0][:, :], zrowL[0][:, :], AF.Abs),
          [("zrow", 0)], [("azrow", 0)])
        E("s", lambda e: e.activation(erowL[0][:, :], azrowL[0][:, :], AF.Exp, scale=-1.0),
          [("azrow", 0)], [("erow", 0)])
        E("s", lambda e: e.activation(lrowL[0][:, :], erowL[0][:, :], AF.Ln, bias=cst[:, 2:3]),
          [("erow", 0), "cst"], [("lrow", 0)])
        E("v", lambda e: e.tensor_scalar(gdecL[p][:, :], zrowL[0][:, :], -1.0, 0.0, ALU.mult, ALU.min),
          [("zrow", 0)], [("gdec", p)])
        E("v", lambda e: e.tensor_tensor(gdecL[p][:, :], gdecL[p][:, :], lrowL[0][:, :], ALU.subtract),
          [("gdec", p), ("lrow", 0)], [("gdec", p)])
        E("s", lambda e: e.activation(xt[p][:, :], PC[:, :], AF.Exp, scale=-1.0),
          ["PC"], [("xt", p)])
        E("v", lambda e: e.tensor_scalar_add(xt[p][:, :], xt[p][:, :], 1.0), [("xt", p)], [("xt", p)])
        E("v", lambda e: e.reciprocal(gsigL[p][:, :], xt[p][:, :]), [("xt", p)], [("gsig", p)])
        E("t", lambda e: e.matmul(PB[:, 0:16], ltri_f[:, :], gdecL[p][:, :], start=True, stop=True),
          ["ltri_f", ("gdec", p)], ["PB"])
        E("t", lambda e: e.matmul(PB[:, 16:32], ones_f[:, :], gdecL[p][:, :], start=True, stop=True),
          ["ones_f", ("gdec", p), "PB"], ["PB"])
        E("v", lambda e: e.tensor_copy(Gc_st[:, t * H:(t + 1) * H], PB[:, 0:16]),
          ["PB"], [("Gc_st", t)])
        E("v", lambda e: e.tensor_copy(csumL[p][:, :], PB[:, 16:32]), ["PB"], [("csum", p)])
        E("v", lambda e: e.tensor_copy(Gpre_st[:, t * H:(t + 1) * H], G_run[:, :]),
          ["G_run"], [("Gpre_st", t)])
        E("v", lambda e: e.tensor_tensor(G_run[:, :], G_run[:, :], csumL[p][:, :], ALU.add),
          ["G_run", ("csum", p)], ["G_run"])
        E("s", lambda e: e.activation(lamL[p][:, :], csumL[p][:, :], AF.Exp), [("csum", p)], [("lam", p)])
        E("v", lambda e: e.tensor_copy(lam_arL[p][0:64, :], lamL[p][0:64, 0:16:2]), [("lam", p)], [("lam_ar", p)])
        E("v", lambda e: e.tensor_copy(lam_arL[p][64:128, :], lamL[p][64:128, 1:16:2]),
          [("lam", p), ("lam_ar", p)], [("lam_ar", p)])
        emit_proj(bt2[p], b2k, *P1W["a0_Wq"], PC, "PC", D)
        emit_proj(bt2[p], b2k, *P1W["a0_Wk"], PD, "PD", D)
        E("v", lambda e: e.tensor_copy(bt3[p][:, :], PC[:, :]), ["PC"], [b3k])
        E("v", lambda e: e.tensor_tensor(bt1[p][:, :], PD[:, :], gsigL[p][:, :], ALU.mult),
          ["PD", ("gsig", p)], [b1k])
        emit_proj(bt2[p], b2k, *P1W["a0_Wv"], PC, "PC", D)
        E("v", lambda e: e.tensor_copy(bt4[p][:, :], PC[:, :]), ["PC"], [b4k])
        E("v", lambda e: e.tensor_tensor(ekL[p][:, :], csumL[p][:, :], Gc_st[:, t * H:(t + 1) * H],
                                         ALU.subtract), [("csum", p), ("Gc_st", t)], [("ek", p)])
        E("s", lambda e: e.activation(ekL[p][:, :], ekL[p][:, :], AF.Exp), [("ek", p)], [("ek", p)])
        ekb = ekL[p][:, :].broadcast_to([128, H, DH])
        E("v", lambda e, ekb=ekb: e.tensor_tensor(
            bt2[p][:, :].rearrange("p (h d) -> p h d", h=H),
            bt1[p][:, :].rearrange("p (h d) -> p h d", h=H), ekb, ALU.mult),
          [b1k, ("ek", p)], [b2k])
        for h in range(H):
            par, hp = h % 2, h // 2
            E("t", lambda e, h=h, par=par, hp=hp: e.matmul(
                PE[par * 64:par * 64 + 64, hp * 64:(hp + 1) * 64],
                bt2[p][:, h * 64:(h + 1) * 64], bt4[p][:, h * 64:(h + 1) * 64],
                start=True, stop=True, tile_position=(0, par * 64)),
              [b2k, b4k], ["PE"])
        E("v", lambda e: e.tensor_copy(sprefb[p][:, :], S_run[:, :]), ["S_run"], [("sprefb", p)])
        dma_a(spref_s[t, :, :], sprefb[p][:, :], [("sprefb", p)], [("spref_s", t)])
        lab = lam_arL[p][:, :].broadcast_to([128, 8, DH])
        E("v", lambda e, lab=lab: e.tensor_tensor(
            S_run[:, :].rearrange("p (a d) -> p a d", a=8),
            S_run[:, :].rearrange("p (a d) -> p a d", a=8), lab, ALU.mult),
          ["S_run", ("lam_ar", p)], ["S_run"])
        E("v", lambda e: e.tensor_tensor(S_run[:, :], S_run[:, :], PE[:, :], ALU.add),
          ["S_run", "PE"], ["S_run"])
        dma_a(q_s[t * 128:(t + 1) * 128, :], bt3[p][:, :], [b3k], [("q_s", t)])
        dma_a(kt_s[t * 128:(t + 1) * 128, :], bt1[p][:, :], [b1k], [("kt_s", t)])
        dma_a(v_s[t * 128:(t + 1) * 128, :], bt4[p][:, :], [b4k], [("v_s", t)])

    for t in range(nT):
        p1_tile(t)
    RGB = [[0, 1, 2, 3], [4, 5, 6, 7]]
    dma_a(cc1_in_s[:, :], S_run[:, :], ["S_run"], ["cc1_in_s"])
    dma_a(cc1_in_g[:, :], G_run[0:1, :], ["G_run"], ["cc1_in_g"])
    E("g", lambda e: e.collective_compute("AllGather", ALU.bypass, replica_groups=RGB,
                                          ins=[cc1_in_s[:, :].opt()],
                                          outs=[cc1_out_s[:, :].opt()]),
      ["cc1_in_s"], ["cc1_out_s"])
    E("g", lambda e: e.collective_compute("AllGather", ALU.bypass, replica_groups=RGB,
                                          ins=[cc1_in_g[:, :].opt()],
                                          outs=[cc1_out_g[:, :].opt()]),
      ["cc1_in_g"], ["cc1_out_g"])
    dma_g(gall[:, :], cc1_out_g[:, :].rearrange("r h -> (r h)").partition_broadcast(128),
          ["cc1_out_g"], ["gall"])
    for u in range(4):
        def mk(e, u=u):
            em = emask_rep[:, u:16:4].broadcast_to([128, 4, H])
            gu = gall[:, u * H:(u + 1) * H].broadcast_to([128, H, 4]).transpose([0, 2, 1])
            return e.tensor_tensor(tmpA[:, 0:64].rearrange("p (a h) -> p a h", a=4), em, gu,
                                   ALU.mult)
        E("v", mk, ["emask_rep", "gall"], ["tmpA"])
        if u == 0:
            E("v", lambda e: e.tensor_copy(ccoef[:, :], tmpA[:, 0:64]), ["tmpA"], ["ccoef"])
        else:
            E("v", lambda e: e.tensor_tensor(ccoef[:, :], ccoef[:, :], tmpA[:, 0:64], ALU.add),
              ["ccoef", "tmpA"], ["ccoef"])
    E("s", lambda e: e.activation(ccoef[:, :], ccoef[:, :], AF.Exp), ["ccoef"], ["ccoef"])
    m1b = m1_rep[:, :].broadcast_to([128, 4, H])
    E("v", lambda e: e.tensor_tensor(ccoef[:, :].rearrange("p (a h) -> p a h", a=4),
                                     ccoef[:, :].rearrange("p (a h) -> p a h", a=4), m1b,
                                     ALU.mult), ["ccoef", "m1_rep"], ["ccoef"])
    E("v", lambda e: e.memset(S_in[:, :], 0.0), [], ["S_in"])
    for pp in range(4):
        E("v", lambda e, pp=pp: e.tensor_copy(cfar[0:64, pp * 8:(pp + 1) * 8],
                                              ccoef[0:64, pp * H:(pp + 1) * H:2]),
          ["ccoef"], ["cfar"])
        E("v", lambda e, pp=pp: e.tensor_copy(cfar[64:128, pp * 8:(pp + 1) * 8],
                                              ccoef[64:128, pp * H + 1:(pp + 1) * H:2]),
          ["ccoef", "cfar"], ["cfar"])
        dma_q(qxnL[0][:, 0:512], cc1_out_s[pp * 128:(pp + 1) * 128, :], ["cc1_out_s"], [("qxn", 0)])
        E("v", lambda e, pp=pp: e.tensor_tensor(
            tmpA[:, 0:512].rearrange("p (a d) -> p a d", a=8),
            qxnL[0][:, 0:512].rearrange("p (a d) -> p a d", a=8),
            cfar[:, pp * 8:(pp + 1) * 8].broadcast_to([128, 8, DH]), ALU.mult),
          [("qxn", 0), "cfar"], ["tmpA"])
        E("v", lambda e: e.tensor_tensor(S_in[:, :], S_in[:, :], tmpA[:, 0:512], ALU.add),
          ["S_in", "tmpA"], ["S_in"])

    sch.rekey(["PA", "PB"], [("PA", 0), ("PA", 1), ("PA", 2), ("PA", 3),
                             ("PB", 0), ("PB", 1), ("PB", 2), ("PB", 3)])
    def p2_tile(t):
        p = t % 2
        b1k, b2k, b3k, b4k = ("bt1", p), ("bt2", p), ("bt3", p), ("bt4", p)
        hk = ("ht", p)
        dma_q(bt3[p][:, :], q_s[t * 128:(t + 1) * 128, :], [("q_s", t)], [b3k])
        dma_q(bt1[p][:, :], kt_s[t * 128:(t + 1) * 128, :], [("kt_s", t)], [b1k])
        dma_q(bt4[p][:, :], v_s[t * 128:(t + 1) * 128, :], [("v_s", t)], [b4k])
        dma_q(bt2[p][:, :].rearrange("p (a n) -> p a n", a=8),
              xnfm_s[:, :, t * 128:(t + 1) * 128].rearrange("a p n -> p a n"),
              [("xnfm_s", t)], [b2k])
        dma_q(sprefb[p][:, :], spref_s[t, :, :], [("spref_s", t)], [("sprefb", p)])
        dma_q(xt[p][:, :], x_in[t * 128:(t + 1) * 128, :], [], [("xt", p)])
        emit_proj(bt2[p], b2k, *P2W["a0_Wog"], PC, "PC", D)
        E("s", lambda e: e.activation(yt[p][:, :], PC[:, :], AF.Exp, scale=-1.0), ["PC"], [("yt", p)])
        E("v", lambda e: e.tensor_scalar_add(yt[p][:, :], yt[p][:, :], 1.0), [("yt", p)], [("yt", p)])
        E("v", lambda e: e.reciprocal(gsigL[p][:, :], yt[p][:, :]), [("yt", p)], [("gsig", p)])
        emit_tp8(bt3[p], b3k, ht[p], hk, doff=0)
        emit_tp8(bt1[p], b1k, ht[p], hk, doff=1024)
        qfm = ht[p][:, 0:1024]
        kfm = ht[p][:, 1024:2048]
        E("t", lambda e: e.transpose(PE[0:16, 0:128], Gc_st[:, t * H:(t + 1) * H], id_f[:, :]),
          [("Gc_st", t), "id_f"], ["PE"])
        E("v", lambda e: e.tensor_copy(gchmL[p][:, :], PE[0:16, 0:128]), ["PE"], [("gchm", p)])
        E("v", lambda e: e.tensor_scalar_mul(negGcL[p][:, :], Gc_st[:, t * H:(t + 1) * H], -1.0),
          [("Gc_st", t)], [("negGc", p)])
        E("s", lambda e: e.activation(a_tL[p][:, :], Gc_st[:, t * H:(t + 1) * H], AF.Exp),
          [("Gc_st", t)], [("a_t", p)])
        E("s", lambda e: e.activation(lamL[p][:, :], Gpre_st[:, t * H:(t + 1) * H], AF.Exp),
          [("Gpre_st", t)], [("lam", p)])
        E("v", lambda e: e.tensor_copy(lam_arL[p][0:64, :], lamL[p][0:64, 0:16:2]), [("lam", p)], [("lam_ar", p)])
        E("v", lambda e: e.tensor_copy(lam_arL[p][64:128, :], lamL[p][64:128, 1:16:2]),
          [("lam", p), ("lam_ar", p)], [("lam_ar", p)])
        E("v", lambda e: e.tensor_tensor(
            yt[p][:, 0:512].rearrange("p (a d) -> p a d", a=8),
            S_in[:, :].rearrange("p (a d) -> p a d", a=8),
            lam_arL[p][:, :].broadcast_to([128, 8, DH]), ALU.mult),
          ["S_in", ("lam_ar", p)], [("yt", p)])
        E("v", lambda e: e.tensor_tensor(sprefcL[p][:, :], sprefb[p][:, :], yt[p][:, 0:512], ALU.add),
          [("sprefb", p), ("yt", p)], [("sprefc", p)])
        for h in range(H):
            par, hp = h % 2, h // 2
            slot, ds = h % 4, h % 2
            E("t", lambda e, h=h, slot=slot: e.matmul(
                PA[:, slot * 128:(slot + 1) * 128],
                onehot[:, h:h + 1].broadcast_to([16, 1, 128]),
                gchmL[p][:, :], start=True, stop=True),
              ["onehot", ("gchm", p)], [("PA", slot)])
            E("s", lambda e, h=h, slot=slot, ds=ds: e.activation(
                Dt[ds][:, :], PA[:, slot * 128:(slot + 1) * 128], AF.Exp,
                bias=negGcL[p][:, h:h + 1]),
              [("PA", slot), ("negGc", p)], [("Dt", ds)])
            E("t", lambda e, par=par, hp=hp, slot=slot: e.matmul(
                PB[:, slot * 128:(slot + 1) * 128],
                kfm[par * 64:par * 64 + 64, hp * 128:(hp + 1) * 128],
                qfm[par * 64:par * 64 + 64, hp * 128:(hp + 1) * 128],
                start=True, stop=True),
              [hk], [("PB", slot)])
            E("v", lambda e, slot=slot, ds=ds: e.tensor_tensor(
                At[ds][:, :], PB[:, slot * 128:(slot + 1) * 128], Dt[ds][:, :], ALU.mult),
              [("PB", slot), ("Dt", ds)], [("At", ds)])
            E("g", lambda e, ds=ds: e.affine_select(
                At[ds][:, :], At[ds][:, :], [[1, 128]], ALU.is_ge, 0.0,
                base=0, channel_multiplier=-1),
              [("At", ds)], [("At", ds)])
            if dbg and t == 0 and h == 0:
                dma_g(dbg_D[:, :], Dt[ds][:, :], [("Dt", ds)], ["dbg_D"], chain=("dbgD",))
                dma_g(dbg_gchm[:, :], gchmL[p][:, :], [("gchm", p)], ["dbg_gchm"], chain=("dbgG",))
            E("t", lambda e, h=h, ds=ds: e.matmul(
                PC[:, h * 64:(h + 1) * 64], At[ds][:, :], bt4[p][:, h * 64:(h + 1) * 64],
                start=True, stop=True),
              [("At", ds), b4k], ["PC"])
            E("t", lambda e, h=h, par=par, hp=hp: e.matmul(
                PD[:, h * 64:(h + 1) * 64],
                qfm[par * 64:par * 64 + 64, hp * 128:(hp + 1) * 128],
                sprefcL[p][par * 64:par * 64 + 64, hp * 64:(hp + 1) * 64],
                start=True, stop=True),
              [hk, ("sprefc", p)], ["PD"])
        E("v", lambda e: e.tensor_tensor(
            yt[p][:, :].rearrange("p (h d) -> p h d", h=H),
            PD[:, :].rearrange("p (h d) -> p h d", h=H),
            a_tL[p][:, :].broadcast_to([128, H, DH]), ALU.mult),
          ["PD", ("a_t", p)], [("yt", p)])
        E("v", lambda e: e.tensor_tensor(yt[p][:, :], yt[p][:, :], PC[:, :], ALU.add),
          [("yt", p), "PC"], [("yt", p)])
        if dbg:
            dma_g(dbg_o[t * 128:(t + 1) * 128, :], yt[p][:, :], [("yt", p)], [("dbg_o", t)], chain=("dbgO",))
        E("s", lambda e: e.activation(bt1[p][:, :], yt[p][:, :], AF.Square,
                                      accum_out=sqrowL[p][:, :]),
          [("yt", p)], [("sqrow", p), b1k])
        E("v", lambda e: e.tensor_scalar_mul(sqrowL[p][:, :], sqrowL[p][:, :], 1.0 / D),
          [("sqrow", p)], [("sqrow", p)])
        E("s", lambda e: e.activation(rstdL[p][:, :], sqrowL[p][:, :], AF.Ln, bias=cst[:, 1:2]),
          [("sqrow", p), "cst"], [("rstd", p)])
        E("s", lambda e: e.activation(rstdL[p][:, :], rstdL[p][:, :], AF.Exp, scale=-0.5),
          [("rstd", p)], [("rstd", p)])
        if dbg:
            dma_g(dbg_m2[t * 128:(t + 1) * 128, :], yt[p][:, :], [("yt", p)], [("dbg_m2", t)])
        E("v", lambda e: e.tensor_scalar(yt[p][:, :], yt[p][:, :], rstdL[p][:, 0:1], None, ALU.mult),
          [("yt", p), ("rstd", p)], [("yt", p)])
        E("v", lambda e: e.tensor_tensor(yt[p][:, :], yt[p][:, :], nw_rep[:, :], ALU.mult),
          [("yt", p), "rep0"], [("yt", p)])
        if dbg:
            dma_g(dbg_m[t * 128:(t + 1) * 128, :], yt[p][:, :], [("yt", p)], [("dbg_m", t)])
            dma_g(dbg_s[t * 128:(t + 1) * 128, :], gsigL[p][:, :], [("gsig", p)], [("dbg_s", t)])
        E("v", lambda e: e.tensor_tensor(bt1[p][:, :], yt[p][:, :], gsigL[p][:, :], ALU.mult),
          [("yt", p), ("gsig", p)], [b1k])
        if dbg:
            dma_g(dbg_g[t * 128:(t + 1) * 128, :], bt1[p][:, :], [b1k], [("dbg_g", t)], chain=("dbgg",))
            dma_g(dbg_r[t * 128:(t + 1) * 128], rstdL[p][:, :], [("rstd", p)], [("dbg_r", t)], chain=("dbgr",))
        emit_tp8(bt1[p], b1k, ht[p], hk, doff=2048)
        emit_proj(ht[p][:, 2048:3072], hk, *P2W["a0_Wo"], PD, "PD", D)
        E("v", lambda e: e.tensor_tensor(yt[p][:, :], PD[:, :], xt[p][:, :], ALU.add),
          ["PD", ("xt", p)], [("yt", p)])
        dma_a(partial_s[t * 128:(t + 1) * 128, :], yt[p][:, :], [("yt", p)], [("partial_s", t)])

    for t in range(nT):
        p2_tile(t)
    sch.rekey([("PA", 0), ("PA", 1), ("PA", 2), ("PA", 3),
               ("PB", 0), ("PB", 1), ("PB", 2), ("PB", 3)], ["PA", "PB"])

    NF0 = F0 // 512

    def load_f0_pair(i6):
        half = (i6 % 2) * 8192
        gv_ = arA[:, half:half + 4096].rearrange("p (c n) -> p c n", c=8)
        gk_ = ("arA", (i6 % 2) * 2)
        uv_ = arA[:, half + 4096:half + 8192].rearrange("p (c n) -> p c n", c=8)
        uk_ = ("arA", (i6 % 2) * 2 + 1)
        for c in range(8):
            tp_q(gv_[:, c, :], stg["f0_wg"][i6 * 512:(i6 + 1) * 512, c * 128:(c + 1) * 128],
                 ["stg_f0_wg"], [gk_], chain=(gk_[0],))
            tp_q(uv_[:, c, :], stg["f0_wu"][i6 * 512:(i6 + 1) * 512, c * 128:(c + 1) * 128],
                 ["stg_f0_wu"], [uk_], chain=(uk_[0],))
        return (gv_, gk_), (uv_, uk_)

    def p3a_tile(t, i6, gv, gk, uv, uk):
        p = t % 2
        b2k = ("bt2", p)
        dma_q(bt2[p][:, :].rearrange("p (a n) -> p a n", a=8),
              xnfm_s[:, :, t * 128:(t + 1) * 128].rearrange("a p n -> p a n"),
              [("xnfm_s", t)], [b2k])
        emit_proj(bt2[p], b2k, gv, gk, PC, "PC", 512)
        emit_proj(bt2[p], b2k, uv, uk, PD, "PD", 512)
        E("s", lambda e: e.activation(sg32L[0][:, :], PC[:, 0:512], AF.Silu, scale=INV127),
          ["PC"], [("sg32", 0)])
        E("v", lambda e: e.tensor_tensor(bt1[p][:, 0:512], sg32L[0][:, :], PD[:, 0:512],
                                         ALU.mult), [("sg32", 0), "PD"], [("bt1", p)])
        dma_a(h0_s[t * 128:(t + 1) * 128, i6 * 512:(i6 + 1) * 512], bt1[p][:, 0:512],
              [("bt1", p)], [("h0_s", t, i6)])

    f0v = {0: load_f0_pair(0)}
    for i6 in range(NF0):
        if i6 + 1 < NF0:
            f0v[i6 + 1] = load_f0_pair(i6 + 1)
        (gv, gk), (uv, uk) = f0v.pop(i6)
        for t in range(nT):
            p3a_tile(t, i6, gv, gk, uv, uk)

    w3d_view = arB[:, 0:F0 // 128 * 1024].rearrange("p (c n) -> p c n", c=F0 // 128)
    w3d_keys = ("arB", 0)
    for c in range(F0 // 128):
        tp_q(w3d_view[:, c, :], stg["f0_wd"][:, c * 128:(c + 1) * 128],
             ["stg_f0_wd"], [("arB", 0), ("arB", 1), ("arB", 2)], chain=("arB",))
    def p3b_tile(t):
        p = t % 2
        hk = ("ht", p)
        dma_q(ht[p][:, :], h0_s[t * 128:(t + 1) * 128, :],
              [("h0_s", t, i) for i in range(6)], [hk])
        dma_q(yt[p][:, :], partial_s[t * 128:(t + 1) * 128, :], [("partial_s", t)], [("yt", p)])
        for r in range(3):
            for b in range(8):
                bb = r * 8 + b
                E("t", lambda e, bb=bb, b=b: e.transpose(
                    PT[:, b * 128:(b + 1) * 128], ht[p][:, bb * 128:(bb + 1) * 128], id_b[:, :]),
                  [hk, "id_b"], ["PT"])
            E("v", lambda e, r=r: e.tensor_copy(hfm[:, r * 1024:(r + 1) * 1024], PT[:, :]),
              ["PT"], ["hfm"])
        for c in range(F0 // 128):
            for j in range(2):
                E("t", lambda e, c=c, j=j: e.matmul(
                    PC[:, j * 512:(j + 1) * 512], hfm[:, c * 128:(c + 1) * 128],
                    w3d_view[:, c, j * 512:(j + 1) * 512],
                    start=(c == 0), stop=(c == F0 // 128 - 1)),
                  ["hfm", w3d_keys], ["PC"])
        E("v", lambda e: e.scalar_tensor_tensor(xt[p][:, :], PC[:, :], INV127 * INV127,
                                                yt[p][:, :], ALU.mult, ALU.add),
          ["PC", ("yt", p)], [("xt", p)])
        dma_a(xnew_s[t * 128:(t + 1) * 128, :], xt[p][:, :], [("xt", p)], [("xnew_s", t)])

    for t in range(nT):
        p3b_tile(t)
    if upto == "l0":
        for t in range(nT):
            dma_a(out_ext[t * 128:(t + 1) * 128, :], xnew_s[t * 128:(t + 1) * 128, :],
                  [("xnew_s", t)], [("out", t)])
        sch.finalize()
        return nc

    RG8 = [[0, 1, 2, 3, 4, 5, 6, 7]]
    rows3 = [(0, 128), (128, 256), (256, RG1)]
    for (src_, dst, keyd) in [(wsh_g, ccw_in_g, "ccw_in_g"), (wsh_u, ccw_in_u, "ccw_in_u")]:
        for (r0, r1) in rows3:
            n = r1 - r0
            for j2 in range(2):
                E("q", lambda e, src_=src_, r0=r0, r1=r1, j2=j2: e.dma_start(
                    out=qtmpL[0][0:r1 - r0, :], in_=src_[r0:r1, j2 * 512:(j2 + 1) * 512]),
                    [("qtmp", 0)], [("qtmp", 0)], dma=True)
                emit_fp4(qtmpL[0], ("qtmp", 0), bt4[0], ("bt4", 0), 512)
                E("gd", lambda e, dst=dst, r0=r0, n=n, j2=j2: e.dma_start(
                    out=dst[r0:r0 + n, j2 * 512:(j2 + 1) * 512], in_=bt4[0][0:n, 0:512]),
                    [("bt4", 0)], [keyd], dma=True)
    W6 = [512] * 5 + [256]
    for i6 in range(6):
        w6 = W6[i6]
        r0 = 512 * i6
        E("q", lambda e, r0=r0, w6=w6: e.dma_start(
            out=qtmpL[0][0:RGD, 0:w6], in_=wsh_d[:, r0:r0 + w6]), [("qtmp", 0)], [("qtmp", 0)], dma=True)
        emit_fp4(qtmpL[0], ("qtmp", 0), bt4[0], ("bt4", 0), w6)
        E("gd", lambda e, r0=r0, w6=w6: e.dma_start(
            out=ccw_in_d[:, r0:r0 + w6], in_=bt4[0][0:RGD, 0:w6]),
            [("bt4", 0)], ["ccw_in_d"], dma=True)
    E("g", lambda e: e.collective_compute("AllGather", ALU.bypass, replica_groups=RG8,
                                          ins=[ccw_in_g[:, :].opt()], outs=[wgq_s[:, :].opt()]),
      ["ccw_in_g"], ["wgq_s"])
    E("g", lambda e: e.collective_compute("AllGather", ALU.bypass, replica_groups=RG8,
                                          ins=[ccw_in_u[:, :].opt()], outs=[wuq_s[:, :].opt()]),
      ["ccw_in_u"], ["wuq_s"])
    E("g", lambda e: e.collective_compute("AllGather", ALU.bypass, replica_groups=RG8,
                                          ins=[ccw_in_d[:, :].opt()], outs=[wdq_s[:, :].opt()]),
      ["ccw_in_d"], ["wdq_s"])
    for (wnm, dst_s, dkey, bsb, bkey, bdst, bdkey) in [
            ("a1_Wq", qr_s, "qr_s", bq_sb, "bq_sb", bqr_s, "bqr_s"),
            ("a1_Wk", kr_s, "kr_s", bk_sb, "bk_sb", bkr_s, "bkr_s")]:
        for half in range(2):
            for hh in range(8):
                h = half * 8 + hh
                par, hp = h % 2, h // 2
                E("t", lambda e, par=par, hp=hp, hh=hh, bsb=bsb: e.matmul(
                    PB[0:1, hh * 64:(hh + 1) * 64],
                    bsb[par * 64:par * 64 + 64, hp:hp + 1],
                    rw_sb[par * 64:par * 64 + 64, hp * 64:(hp + 1) * 64],
                    start=True, stop=True),
                  [bkey, "rw_sb"], ["PB"])
            E("v", lambda e: e.tensor_copy(tmpA[0:1, 0:512], PB[0:1, 0:512]), ["PB"], ["tmpA"])
            E("gd", lambda e, bdst=bdst, half=half: e.dma_start(
                out=bdst[half * 512:(half + 1) * 512], in_=tmpA[0:1, 0:512]),
              ["tmpA"], [bdkey], dma=True)
        for co in range(8):
            wp = 0
            wk = ("wstg", wp)
            tp_q(wstg[wp][:, :], stg[wnm][:, co * 128:(co + 1) * 128], ["stg_" + wnm], [wk],
                 chain=("wstg",))
            for hh in range(2):
                h = co * 2 + hh
                par, hp = h % 2, h // 2
                for ci in range(8):
                    E("t", lambda e, par=par, hp=hp, ci=ci, wp=wp: e.matmul(
                        PA[:, ci * 64:(ci + 1) * 64],
                        wstg[wp][par * 64:par * 64 + 64, ci * 128:(ci + 1) * 128],
                        rw_sb[par * 64:par * 64 + 64, hp * 64:(hp + 1) * 64],
                        start=True, stop=True),
                      [wk, "rw_sb"], ["PA"])
                E("v", lambda e: e.tensor_copy(qrstg[:, :], PA[:, :]), ["PA"], ["qrstg"])
                E("gd", lambda e, h=h, dst_s=dst_s: e.dma_start(
                    out=dst_s[:, h * 64:(h + 1) * 64].rearrange("(c p) f -> p c f", c=8),
                    in_=qrstg[:, :].rearrange("p (c f) -> p c f", c=8)),
                  ["qrstg"], [(dkey, h)], dma=True)
    dma_g(bqr_rep[:, :], bqr_s.partition_broadcast(128), ["bqr_s"], ["rep1"])
    dma_g(bkr_rep[:, :], bkr_s.partition_broadcast(128), ["bkr_s"], ["rep2"])
    dma_g(bv_rep[:, :], W["a1_bv"].partition_broadcast(128), [], ["rep0"])
    qrv, qrk = arA_view(0)
    krv, krk = arA_view(1)
    wvv, wvk = arA_view(2)
    for c in range(8):
        E("q", lambda e, c=c: e.dma_start(out=qrv[:, c, :], in_=qr_s[c * 128:(c + 1) * 128, :]),
          [(("qr_s", hx)) for hx in range(H)], [qrk], dma=True, chain=(qrk[0],))
        E("q", lambda e, c=c: e.dma_start(out=krv[:, c, :], in_=kr_s[c * 128:(c + 1) * 128, :]),
          [(("kr_s", hx)) for hx in range(H)], [krk], dma=True, chain=(krk[0],))
        dma_g(wvv[:, c, :], W["a1_Wv"][c * 128:(c + 1) * 128, :], [], [wvk], chain=(wvk[0],))
    wo1v, wo1k = arB_view(0)
    for c in range(8):
        dma_g(wo1v[:, c, :], W["a1_Wo"][c * 128:(c + 1) * 128, :], [], [wo1k], chain=(wo1k[0],))

    def p4_tile(t):
        p = t % 2
        b1k, b2k, b3k, b4k = ("bt1", p), ("bt2", p), ("bt3", p), ("bt4", p)
        dma_q(xt[p][:, :], xnew_s[t * 128:(t + 1) * 128, :], [("xnew_s", t)], [("xt", p)])
        emit_ln(xt[p], ("xt", p), bt1[p], b1k, dst2=yt[p], d2k=("yt", p))
        dma_a(xn1_s[t * 128:(t + 1) * 128, :], yt[p][:, :], [("yt", p)], [("xn1_s", t)])
        emit_tp8(bt1[p], b1k, bt2[p], b2k)
        emit_proj(bt2[p], b2k, qrv, qrk, PC, "PC", D)
        emit_proj(bt2[p], b2k, krv, krk, PD, "PD", D)
        E("v", lambda e: e.tensor_tensor(yt[p][:, :], PC[:, :], bqr_rep[:, :], ALU.add),
          ["PC", "rep1"], [("yt", p)])
        E("v", lambda e: e.tensor_scalar(bt3[p][:, :], yt[p][:, :], 0.0, L1_EPS,
                                         ALU.max, ALU.add), [("yt", p)], [b3k])
        E("v", lambda e: e.tensor_tensor(yt[p][:, :], PD[:, :], bkr_rep[:, :], ALU.add),
          ["PD", "rep2"], [("yt", p)])
        E("v", lambda e: e.tensor_scalar(bt4[p][:, :], yt[p][:, :], 0.0, L1_EPS,
                                         ALU.max, ALU.add), [("yt", p)], [b4k])
        dma_a(qp_s[t * 128:(t + 1) * 128, :], bt3[p][:, :], [b3k], [("qp_s", t)])
        emit_proj(bt2[p], b2k, wvv, wvk, PC, "PC", D)
        E("v", lambda e: e.tensor_tensor(bt1[p][:, :], PC[:, :], bv_rep[:, :], ALU.add),
          ["PC", "rep0"], [b1k])
        for h in range(H):
            par, hp = h % 2, h // 2
            first = (t == 0 and h == 0)
            E("t", lambda e, h=h, par=par, hp=hp, first=first: e.matmul(
                PE[par * 64:par * 64 + 64, hp * 64:(hp + 1) * 64],
                bt4[p][:, h * 64:(h + 1) * 64], bt1[p][:, h * 64:(h + 1) * 64],
                start=first, stop=(t == nT - 1 and h == H - 1),
                tile_position=(0, par * 64), skip_group_check=True),
              [b4k, b1k], ["PE"])
            E("t", lambda e, h=h, par=par, hp=hp, first=first: e.matmul(
                PA[par * 64:par * 64 + 64, hp:hp + 1],
                bt4[p][:, h * 64:(h + 1) * 64], onesb[:, :],
                start=first, stop=(t == nT - 1 and h == H - 1),
                tile_position=(0, par * 64), skip_group_check=True),
              [b4k, "onesb"], ["PA"])

    for t in range(nT):
        p4_tile(t)
    dma_g(bo_rep[:, :], W["a1_bo"].partition_broadcast(128), [], ["rep1"])

    E("v", lambda e: e.tensor_copy(tmpA[:, 0:512], PE[:, :]), ["PE"], ["tmpA"])
    dma_a(cc2_in_kv[:, :], tmpA[:, 0:512], ["tmpA"], ["cc2_in_kv"])
    E("v", lambda e: e.tensor_copy(lamL[0][:, 0:8], PA[:, 0:8]), ["PA"], [("lam", 0)])
    dma_a(cc2_in_kp[:, :], lamL[0][:, 0:8], [("lam", 0)], ["cc2_in_kp"])
    E("g", lambda e: e.collective_compute("AllReduce", ALU.add, replica_groups=RGB,
                                          ins=[cc2_in_kv[:, :].opt()],
                                          outs=[cc2_out_kv[:, :].opt()]),
      ["cc2_in_kv"], ["cc2_out_kv"])
    E("g", lambda e: e.collective_compute("AllReduce", ALU.add, replica_groups=RGB,
                                          ins=[cc2_in_kp[:, :].opt()],
                                          outs=[cc2_out_kp[:, :].opt()]),
      ["cc2_in_kp"], ["cc2_out_kp"])
    dma_g(kvg_sb[:, :], cc2_out_kv[:, :], ["cc2_out_kv"], ["kvg_sb"])
    dma_g(kpg_sb[:, :], cc2_out_kp[:, :], ["cc2_out_kp"], ["kpg_sb"])

    def p5_tile(t):
        p = t % 2
        b1k, b3k = ("bt1", p), ("bt3", p)
        hk = ("ht", p)
        dma_q(bt3[p][:, :], qp_s[t * 128:(t + 1) * 128, :], [("qp_s", t)], [b3k])
        dma_q(xt[p][:, :], xnew_s[t * 128:(t + 1) * 128, :], [("xnew_s", t)], [("xt", p)])
        emit_tp8(bt3[p], b3k, ht[p], hk, doff=0)
        for h in range(H):
            par, hp = h % 2, h // 2
            E("t", lambda e, h=h, par=par, hp=hp: e.matmul(
                PC[:, h * 64:(h + 1) * 64],
                ht[p][par * 64:par * 64 + 64, hp * 128:(hp + 1) * 128],
                kvg_sb[par * 64:par * 64 + 64, hp * 64:(hp + 1) * 64],
                start=True, stop=True),
              [hk, "kvg_sb"], ["PC"])
            E("t", lambda e, h=h, par=par, hp=hp: e.matmul(
                PB[:, h:h + 1],
                ht[p][par * 64:par * 64 + 64, hp * 128:(hp + 1) * 128],
                kpg_sb[par * 64:par * 64 + 64, hp:hp + 1],
                start=True, stop=True),
              [hk, "kpg_sb"], ["PB"])
        E("v", lambda e: e.tensor_scalar_add(lamL[p][:, :], PB[:, 0:16], L1_EPS), ["PB"], [("lam", p)])
        E("v", lambda e: e.reciprocal(ekL[p][:, :], lamL[p][:, :]), [("lam", p)], [("ek", p)])
        E("v", lambda e: e.tensor_tensor(
            bt1[p][:, :].rearrange("p (h d) -> p h d", h=H),
            PC[:, :].rearrange("p (h d) -> p h d", h=H),
            ekL[p][:, :].broadcast_to([128, H, DH]), ALU.mult),
          ["PC", ("ek", p)], [b1k])
        emit_tp8(bt1[p], b1k, ht[p], hk, doff=1024)
        emit_proj(ht[p][:, 1024:2048], hk, wo1v, wo1k, PD, "PD", D)
        E("v", lambda e: e.tensor_tensor(yt[p][:, :], PD[:, :], bo_rep[:, :], ALU.add),
          ["PD", "rep1"], [("yt", p)])
        E("v", lambda e: e.tensor_tensor(yt[p][:, :], yt[p][:, :], xt[p][:, :], ALU.add),
          [("yt", p), ("xt", p)], [("yt", p)])
        dma_a(partial2_s[t * 128:(t + 1) * 128, :], yt[p][:, :], [("yt", p)],
              [("partial2_s", t)])

    for t in range(nT):
        p5_tile(t)
    W6a = [512] * 5 + [256]

    def load_f1_pair(i6):
        w6 = W6a[i6]
        half = (i6 % 2) * 8192
        gv_ = arA[:, half:half + 8 * w6].rearrange("p (c n) -> p c n", c=8)
        gk_ = ("arA", (i6 % 2) * 2)
        uv_ = arA[:, half + 8 * w6:half + 16 * w6].rearrange("p (c n) -> p c n", c=8)
        uk_ = ("arA", (i6 % 2) * 2 + 1)
        r0 = 512 * i6
        for c in range(8):
            tp_q(gv_[:, c, :], wgq_s[r0:r0 + w6, c * 128:(c + 1) * 128], ["wgq_s"], [gk_],
                 chain=(gk_[0],))
            tp_q(uv_[:, c, :], wuq_s[r0:r0 + w6, c * 128:(c + 1) * 128], ["wuq_s"], [uk_],
                 chain=(uk_[0],))
        return (gv_, gk_), (uv_, uk_)

    def p6a_tile(t, i6, gv, gk, uv, uk, w6, r0):
        p = t % 2
        b1k, b2k, b4k = ("bt1", p), ("bt2", p), ("bt4", p)
        if i6 == 0:
            dma_q(xt[p][:, :], xn1_s[t * 128:(t + 1) * 128, :], [("xn1_s", t)], [("xt", p)])
            emit_fp4(xt[p][:, 0:512], ("xt", p), bt1[p][:, 0:512], b1k, 512)
            emit_fp4(xt[p][:, 512:1024], ("xt", p), bt1[p][:, 512:1024], b1k, 512)
            emit_tp8(bt1[p], b1k, bt2[p], b2k)
            dma_a(xq_s[:, :, t * 128:(t + 1) * 128].rearrange("a p n -> p a n"),
                  bt2[p][:, :].rearrange("p (a n) -> p a n", a=8), [b2k], [("xq_s", t)])
        else:
            dma_q(bt2[p][:, :].rearrange("p (a n) -> p a n", a=8),
                  xq_s[:, :, t * 128:(t + 1) * 128].rearrange("a p n -> p a n"),
                  [("xq_s", t)], [b2k])
        emit_proj(bt2[p], b2k, gv, gk, PC, "PC", w6)
        emit_proj(bt2[p], b2k, uv, uk, PD, "PD", w6)
        E("s", lambda e, w6=w6: e.activation(sg32L[0][:, 0:w6], PC[:, 0:w6], AF.Silu),
          ["PC"], [("sg32", 0)])
        E("v", lambda e, w6=w6: e.tensor_tensor(hq32L[0][:, 0:w6], sg32L[0][:, 0:w6], PD[:, 0:w6],
                                                ALU.mult), [("sg32", 0), "PD"], [("hq32", 0)])
        emit_fp4(hq32L[0], ("hq32", 0), bt4[p], b4k, w6)
        dma_a(hq_s[t * 128:(t + 1) * 128, r0:r0 + w6], bt4[p][:, 0:w6], [b4k],
              [("hq_s", t, i6)])

    f1v = {0: load_f1_pair(0)}
    for i6 in range(6):
        if i6 + 1 < 6:
            f1v[i6 + 1] = load_f1_pair(i6 + 1)
        (gv, gk), (uv, uk) = f1v.pop(i6)
        for t in range(nT):
            p6a_tile(t, i6, gv, gk, uv, uk, W6a[i6], 512 * i6)

    w6d_view = arB[:, 0:F1 // 128 * 1024].rearrange("p (c n) -> p c n", c=F1 // 128)
    w6d_key = ("arB", 0)
    for c in range(F1 // 128):
        tp_q(w6d_view[:, c, :], wdq_s[:, c * 128:(c + 1) * 128],
             ["wdq_s"], [("arB", 0), ("arB", 1), ("arB", 2)], chain=("arB",))
    def p6b_tile(t):
        p = t % 2
        hk = ("ht", p)
        dma_q(ht[p][:, 0:F1], hq_s[t * 128:(t + 1) * 128, :],
              [("hq_s", t, i) for i in range(6)], [hk])
        dma_q(yt[p][:, :], partial2_s[t * 128:(t + 1) * 128, :], [("partial2_s", t)],
              [("yt", p)])
        nch = F1 // 128
        for r in range((nch + 7) // 8):
            nb = min(8, nch - r * 8)
            for b in range(nb):
                bb = r * 8 + b
                E("t", lambda e, bb=bb, b=b: e.transpose(
                    PT[:, b * 128:(b + 1) * 128], ht[p][:, bb * 128:(bb + 1) * 128], id_b[:, :]),
                  [hk, "id_b"], ["PT"])
            E("v", lambda e, r=r, nb=nb: e.tensor_copy(
                hfm[:, r * 1024:r * 1024 + nb * 128], PT[:, 0:nb * 128]), ["PT"], ["hfm"])
        for c in range(nch):
            for j in range(2):
                E("t", lambda e, c=c, j=j: e.matmul(
                    PC[:, j * 512:(j + 1) * 512], hfm[:, c * 128:(c + 1) * 128],
                    w6d_view[:, c, j * 512:(j + 1) * 512],
                    start=(c == 0), stop=(c == nch - 1)),
                  ["hfm", w6d_key], ["PC"])
        E("v", lambda e: e.tensor_tensor(xt[p][:, :], PC[:, :], yt[p][:, :], ALU.add),
          ["PC", ("yt", p)], [("xt", p)])
        dma_a(out_ext[t * 128:(t + 1) * 128, :], xt[p][:, :], [("xt", p)], [("out", t)])

    for t in range(nT):
        p6b_tile(t)
    sch.finalize()
    return nc


_BUILT = {}


def _inmaps(inputs, T):
    x = np.asarray(inputs["L0_attn_x"], np.float32)
    nm = {
        "a0_Wq": "L0_attn_Wq", "a0_Wk": "L0_attn_Wk", "a0_Wv": "L0_attn_Wv",
        "a0_Wo": "L0_attn_Wo", "a0_Wg": "L0_attn_Wg", "a0_Wog": "L0_attn_Wog",
        "a0_Wd": "L0_attn_Wd", "a0_bd": "L0_attn_bd", "a0_nw": "L0_attn_norm_w",
        "f0_wg": "L0_ffn_w_gate", "f0_wu": "L0_ffn_w_up", "f0_wd": "L0_ffn_w_down",
        "a1_Wq": "L1_attn_Wq", "a1_bq": "L1_attn_bq", "a1_Wk": "L1_attn_Wk",
        "a1_bk": "L1_attn_bk", "a1_Wv": "L1_attn_Wv", "a1_bv": "L1_attn_bv",
        "a1_Wo": "L1_attn_Wo", "a1_bo": "L1_attn_bo", "a1_rw": "L1_attn_rw",
    }
    base = {k: np.ascontiguousarray(np.asarray(inputs[v])) for k, v in nm.items()}
    RG1 = F1 // 8
    RGD = D // 8
    maps = []
    for c in range(8):
        b, qq = c // 4, c % 4
        r = c % 4
        emask = np.zeros((4, 4), np.float32)
        m1 = np.zeros((4,), np.float32)
        for pp in range(4):
            if pp < r:
                m1[pp] = 1.0
                for u in range(pp + 1, r):
                    emask[pp, u] = 1.0
        m = dict(base)
        m["x"] = np.ascontiguousarray(x[b, qq * T:(qq + 1) * T])
        m["emask"] = emask.reshape(-1)
        m["m1"] = m1
        fwg = np.asarray(inputs["L1_ffn_w_gate"], np.float32)
        fwu = np.asarray(inputs["L1_ffn_w_up"], np.float32)
        fwd = np.asarray(inputs["L1_ffn_w_down"], np.float32)
        m["wsh_g"] = np.ascontiguousarray(fwg[c * RG1:(c + 1) * RG1])
        m["wsh_u"] = np.ascontiguousarray(fwu[c * RG1:(c + 1) * RG1])
        m["wsh_d"] = np.ascontiguousarray(fwd[c * RGD:(c + 1) * RGD])
        maps.append(m)
    return maps


def kernel(**inputs):
    T = 4096
    key = ("full", T)
    if key not in _BUILT:
        _BUILT[key] = build(T, "full")
    nc = _BUILT[key]
    maps = _inmaps(inputs, T)
    res = run_bass_kernel_spmd(nc, maps, core_ids=list(range(8)))
    out = np.empty((2, 16384, D), np.float32)
    for c, r in enumerate(res.results):
        out[c // 4, (c % 4) * T:(c % 4 + 1) * T] = r["out"]
    return out


# version 05
# speedup vs baseline: 1.3088x; 1.3084x over previous
import numpy as np
import concourse.bass as bass
import concourse.mybir as mybir
from concourse.bass_utils import run_bass_kernel_spmd

F32 = mybir.dt.float32
BF16 = mybir.dt.bfloat16
I32 = mybir.dt.int32
U8 = mybir.dt.uint8
AF = mybir.ActivationFunctionType
ALU = mybir.AluOpType
AX = mybir.AxisListType

D = 1024
H = 16
DH = 64
F0 = 3072
F1 = 2816
INV127 = 1.0 / 127.0
LN_EPS = 1e-5
RMS_EPS = float(np.finfo(np.float32).eps)
L1_EPS = 1e-6

STREAMS = ("v", "s", "t", "g", "gd", "q", "ad")


class Sched:

    def __init__(self, nc):
        self.nc = nc
        self.csem = {e: nc.alloc_semaphore(f"S{e}") for e in ("v", "s", "t", "g")}
        self.ccum = {e: 0 for e in ("v", "s", "t", "g")}
        self.dsem = {}
        self.dcum = {}
        self.thunks = {e: [] for e in ("v", "s", "t", "g", "q")}
        self.last_write = {}
        self.readers = {}
        self.hiwater = {e: {} for e in ("v", "s", "t", "g", "q")}

    def _chain_sem(self, chain):
        if chain not in self.dsem:
            self.dsem[chain] = self.nc.alloc_semaphore(f"D{len(self.dsem)}")
            self.dcum[chain] = 0
        return self.dsem[chain]

    def emit(self, stream, fn, reads=(), writes=(), dma=False, chain=None):
        engine = {"gd": "g", "ad": "s"}.get(stream, stream)
        waits = {}

        def need(tok):
            if tok is not None:
                sem, v = tok
                if v > waits.get(sem, 0):
                    waits[sem] = v

        for k in reads:
            need(self.last_write.get(k))
        for k in writes:
            need(self.last_write.get(k))
            for tok in self.readers.get(k, {}).items():
                need((tok[0], tok[1]))
        if dma:
            if chain is None:
                w0 = writes[0]
                chain = (w0[0] if isinstance(w0, tuple) else w0,)
            ck = (stream,) + tuple(chain)
            sem = self._chain_sem(ck)
            prev = self.dcum[ck]
            if prev:
                if prev > waits.get(sem, 0):
                    waits[sem] = prev
            self.dcum[ck] += 16
            mytok = (sem, self.dcum[ck])
        else:
            sem = self.csem[engine]
            self.ccum[engine] += 1
            mytok = (sem, self.ccum[engine])
        hw = self.hiwater[engine]
        wl = []
        own = self.csem.get(engine)
        for s2, v in waits.items():
            if s2 is own and not dma:
                if v >= self.ccum[engine]:
                    continue
            if v > hw.get(s2, 0):
                hw[s2] = v
                wl.append((s2, v))

        nc = self.nc

        def thunk(eng, wl=tuple(wl), fn=fn, sem=sem, inc=(16 if dma else 1), dma=dma):
            for s2, v in wl:
                eng.wait_ge(s2, v)
            n0 = nc.n_instructions()
            fn(eng).then_inc(sem, inc)
            if dma:
                assert nc.n_instructions() == n0 + 1, \
                    f"DMA emit split into {nc.n_instructions() - n0} instructions"

        self.thunks[engine if engine != "g" else "g"].append(thunk)
        if engine == "q":
            self.thunks["q"] = self.thunks["q"]
        for k in writes:
            self.last_write[k] = mytok
            self.readers[k] = {}
        for k in reads:
            rd = self.readers.setdefault(k, {})
            if mytok[1] > rd.get(mytok[0], 0):
                rd[mytok[0]] = mytok[1]
        return mytok

    def rekey(self, old_keys, new_keys):
        self.emit("v", lambda e: e.engine_nop(), reads=list(old_keys), writes=list(new_keys))

    def finalize(self):
        nc = self.nc
        finals = [(self.csem[e], self.ccum[e]) for e in ("v", "s", "t", "g") if self.ccum[e]]
        finals += [(self.dsem[c], self.dcum[c]) for c in self.dsem]
        with nc.allow_low_precision(reason="bf16 matmul operand staging; fp32 accumulation"), \
                nc.Block() as block:
            @block.vector
            def _(e):
                for th in self.thunks["v"]:
                    th(e)

            @block.scalar
            def _(e):
                for th in self.thunks["s"]:
                    th(e)

            @block.tensor
            def _(e):
                for th in self.thunks["t"]:
                    th(e)

            @block.sync
            def _(e):
                for th in self.thunks["q"]:
                    th(e)

            @block.gpsimd
            def _(e):
                for th in self.thunks["g"]:
                    th(e)
                for sem, v in finals:
                    e.wait_ge(sem, v)


def build(T=4096, upto="full", dbg=False):
    nT = T // 128
    nc = bass.Bass("TRN2", target_bir_lowering=False)

    def din(name, shape, dt=F32):
        return nc.dram_tensor(name, shape, dt, kind="ExternalInput").ap()

    x_in = din("x", [T, D])
    W = {}
    for nm in ["Wq", "Wk", "Wv", "Wo", "Wg", "Wog"]:
        W["a0_" + nm] = din("a0_" + nm, [D, D])
    W["a0_Wd"] = din("a0_Wd", [H, D])
    W["a0_bd"] = din("a0_bd", [H])
    W["a0_nw"] = din("a0_nw", [D])
    W["f0_wg"] = din("f0_wg", [F0, D], I32)
    W["f0_wu"] = din("f0_wu", [F0, D], I32)
    W["f0_wd"] = din("f0_wd", [D, F0], I32)
    for nm in ["Wq", "Wk", "Wv", "Wo"]:
        W["a1_" + nm] = din("a1_" + nm, [D, D])
    for nm in ["bq", "bk", "bv", "bo"]:
        W["a1_" + nm] = din("a1_" + nm, [D])
    W["a1_rw"] = din("a1_rw", [H, DH, DH])
    RG1 = F1 // 8
    RGD = D // 8
    wsh_g = din("wsh_g", [RG1, D])
    wsh_u = din("wsh_u", [RG1, D])
    wsh_d = din("wsh_d", [RGD, F1])
    emask_in = din("emask", [16])
    m1_in = din("m1", [4])
    out_ext = nc.dram_tensor("out", [T, D], F32, kind="ExternalOutput").ap()

    def dram(name, shape, dt):
        return nc.dram_tensor(name, shape, dt, kind="Internal").ap()

    stg = {}
    for nm in ["a0_Wq", "a0_Wk", "a0_Wv", "a0_Wo", "a0_Wg", "a0_Wog", "a1_Wq", "a1_Wk"]:
        stg[nm] = dram("stg_" + nm, [D, D], BF16)
    stg["f0_wg"] = dram("stg_f0_wg", [F0, D], BF16)
    stg["f0_wu"] = dram("stg_f0_wu", [F0, D], BF16)
    stg["f0_wd"] = dram("stg_f0_wd", [D, F0], BF16)
    qr_s = dram("qr_s", [D, D], BF16)
    kr_s = dram("kr_s", [D, D], BF16)
    bqr_s = dram("bqr_s", [D], F32)
    bkr_s = dram("bkr_s", [D], F32)
    xnfm_s = dram("xnfm_s", [8, 128, T], BF16)
    q_s = dram("q_s", [T, D], BF16)
    kt_s = dram("kt_s", [T, D], BF16)
    v_s = dram("v_s", [T, D], BF16)
    spref_s = dram("spref_s", [nT, 128, 512], BF16)
    partial_s = dram("partial_s", [T, D], F32)
    h0_s = dram("h0_s", [T, F0], BF16)
    xnew_s = dram("xnew_s", [T, D], F32)
    xn1_s = dram("xn1_s", [T, D], F32)
    qp_s = dram("qp_s", [T, D], BF16)
    partial2_s = dram("partial2_s", [T, D], F32)
    xq_s = dram("xq_s", [8, 128, T], BF16)
    hq_s = dram("hq_s", [T, F1], BF16)
    cc1_in_s = dram("cc1_in_s", [128, 512], F32)
    cc1_in_g = dram("cc1_in_g", [1, H], F32)
    cc1_out_s = dram("cc1_out_s", [512, 512], F32)
    cc1_out_g = dram("cc1_out_g", [4, H], F32)
    cc2_in_kv = dram("cc2_in_kv", [128, 512], F32)
    cc2_in_kp = dram("cc2_in_kp", [128, 8], F32)
    cc2_out_kv = dram("cc2_out_kv", [128, 512], F32)
    cc2_out_kp = dram("cc2_out_kp", [128, 8], F32)
    ccw_in_g = dram("ccw_in_g", [RG1, D], BF16)
    ccw_in_u = dram("ccw_in_u", [RG1, D], BF16)
    ccw_in_d = dram("ccw_in_d", [RGD, F1], BF16)
    wgq_s = dram("wgq_s", [F1, D], BF16)
    if dbg:
        dbg_o = dram("dbg_o", [T, D], F32)
        dbg_g = dram("dbg_g", [T, D], BF16)
        dbg_r = dram("dbg_r", [T], F32)
        dbg_m = dram("dbg_m", [T, D], F32)
        dbg_m2 = dram("dbg_m2", [T, D], F32)
        dbg_s = dram("dbg_s", [T, D], BF16)
        dbg_D = dram("dbg_D", [128, 128], F32)
        dbg_gchm = dram("dbg_gchm", [16, 128], F32)
    wuq_s = dram("wuq_s", [F1, D], BF16)
    wdq_s = dram("wdq_s", [D, F1], BF16)

    S = nc.alloc_sbuf_tensor
    ones_f = S("ones_f", [128, 128], F32)
    ltri_f = S("ltri_f", [128, 128], F32)
    id_b = S("id_b", [128, 128], BF16)
    id_f = S("id_f", [128, 128], F32)
    onehot = S("onehot", [16, 16], F32)
    onesb = S("onesb", [128, 1], BF16)
    cst = S("cst", [128, 4], F32)
    bd_rep = S("bd_rep", [128, H], F32)
    rep = [S(f"rep{i}", [128, D], F32) for i in range(3)]
    nw_rep, bqr_rep, bkr_rep = rep[0], rep[1], rep[2]
    bv_rep = rep[0]
    bo_rep = rep[1]
    emask_rep = S("emask_rep", [128, 16], F32)
    m1_rep = S("m1_rep", [128, 4], F32)
    rw_sb = S("rw_sb", [128, 8 * DH], BF16)
    bq_sb = S("bq_sb", [128, 8], BF16)
    bk_sb = S("bk_sb", [128, 8], BF16)
    wdT_sb = S("wdT_sb", [128, 8 * 16], BF16)
    arA = S("arA", [128, 24576], BF16)
    arB = S("arB", [128, 24576], BF16)
    S_run = S("S_run", [128, 512], F32)
    G_run = S("G_run", [128, H], F32)
    Gc_st = S("Gc_st", [128, nT * H], F32)
    Gpre_st = S("Gpre_st", [128, nT * H], F32)
    S_in = S("S_in", [128, 512], F32)
    gall = S("gall", [128, 4 * H], F32)
    ccoef = S("ccoef", [128, 64], F32)
    cfar = S("cfar", [128, 32], F32)
    kvg_sb = S("kvg_sb", [128, 512], BF16)
    kpg_sb = S("kpg_sb", [128, 8], BF16)
    xt = [S(f"xt{i}", [128, D], F32) for i in range(2)]
    yt = [S(f"yt{i}", [128, D], F32) for i in range(2)]
    bt1 = [S(f"bt1_{i}", [128, D], BF16) for i in range(2)]
    bt2 = [S(f"bt2_{i}", [128, D], BF16) for i in range(2)]
    bt3 = [S(f"bt3_{i}", [128, D], BF16) for i in range(2)]
    bt4 = [S(f"bt4_{i}", [128, D], BF16) for i in range(2)]
    sprefb = [S(f"sprefb{i}", [128, 512], BF16) for i in range(2)]
    sprefcL = [S(f"sprefc{i}", [128, 512], BF16) for i in range(2)]
    ht = [S(f"ht{i}", [128, F0], BF16) for i in range(2)]
    hfm = S("hfm", [128, F0], BF16)
    mrowL = [S(f"mrow{i}", [128, 1], F32) for i in range(2)]
    sqrowL = [S(f"sqrow{i}", [128, 1], F32) for i in range(2)]
    vrowL = [S(f"vrow{i}", [128, 1], F32) for i in range(1)]
    rstdL = [S(f"rstd{i}", [128, 1], F32) for i in range(2)]
    tmpA = S("tmpA", [128, D], F32)
    gsigL = [S(f"gsig{i}", [128, D], BF16) for i in range(2)]
    negGcL = [S(f"negGc{i}", [128, H], F32) for i in range(2)]
    a_tL = [S(f"a_t{i}", [128, H], F32) for i in range(2)]
    zrowL = [S(f"zrow{i}", [128, H], F32) for i in range(1)]
    azrowL = [S(f"azrow{i}", [128, H], F32) for i in range(1)]
    erowL = [S(f"erow{i}", [128, H], F32) for i in range(1)]
    lrowL = [S(f"lrow{i}", [128, H], F32) for i in range(1)]
    gdecL = [S(f"gdec{i}", [128, H], F32) for i in range(2)]
    csumL = [S(f"csum{i}", [128, H], F32) for i in range(2)]
    lamL = [S(f"lam{i}", [128, H], F32) for i in range(2)]
    lam_arL = [S(f"lam_ar{i}", [128, 8], F32) for i in range(2)]
    ekL = [S(f"ek{i}", [128, H], F32) for i in range(2)]
    gchmL = [S(f"gchm{i}", [16, 128], F32) for i in range(2)]
    Dt = [S(f"Dt{i}", [128, 128], BF16) for i in range(2)]
    At = [S(f"At{i}", [128, 128], BF16) for i in range(2)]
    qxnL = [S(f"qxn{i}", [128, 512], F32) for i in range(1)]
    qtmpL = [S(f"qtmp{i}", [128, 512], F32) for i in range(1)]
    qi1L = [S(f"qi1{i}", [128, 512], I32) for i in range(1)]
    qaL = [S(f"qa{i}", [128, 512], F32) for i in range(1)]
    qbL = [S(f"qb{i}", [128, 512], F32) for i in range(1)]
    qazL = [S(f"qaz{i}", [128, 512], F32) for i in range(1)]
    qm1L = [S(f"qm1{i}", [128, 512], U8) for i in range(1)]
    qm2L = [S(f"qm2{i}", [128, 512], U8) for i in range(1)]
    qbmL = [S(f"qbm{i}", [128, 32], F32) for i in range(1)]
    qscL = [S(f"qsc{i}", [128, 32], F32) for i in range(1)]
    qrcL = [S(f"qrc{i}", [128, 32], F32) for i in range(1)]
    hq32L = [S(f"hq32{i}", [128, 512], F32) for i in range(1)]
    sg32L = [S(f"sg32{i}", [128, 512], F32) for i in range(1)]
    wstg = [S("wstg0", [128, 1024], BF16)]
    qrstg = S("qrstg", [128, 512], BF16)

    P = nc.alloc_psum_tensor
    PC = P("PC", [128, 1024], F32)
    PD = P("PD", [128, 1024], F32)
    PA = P("PA", [128, 512], F32)
    PB = P("PB", [128, 512], F32)
    PE = P("PE", [128, 512], F32)
    PT = P("PT", [128, 1024], BF16)

    sch = Sched(nc)
    E = sch.emit

    def dma_q(dst, src, reads, writes, chain=None):
        E("q", lambda e: e.dma_start(out=dst, in_=src), reads=reads, writes=writes, dma=True,
          chain=chain)

    def dma_a(dst, src, reads, writes, chain=None):
        E("ad", lambda e: e.dma_start(out=dst, in_=src), reads=reads, writes=writes,
          dma=True, chain=chain)

    def dma_g(dst, src, reads, writes, slow=False, chain=None):
        E("gd", lambda e: e.dma_start(out=dst, in_=src, allow_slow_non_contiguous=slow),
          reads=reads, writes=writes, dma=True, chain=chain)

    def tp_q(dst, src, reads, writes, chain=None):
        E("q", lambda e: e.dma_start_transpose(dst, src), reads=reads, writes=writes, dma=True,
          chain=chain)

    def emit_ln(src, sk, dst, dk, p=0, dst2=None, d2k=None):
        E("v", lambda e: e.reduce_sum(mrowL[p][:, :], src[:, :], axis=AX.X), [sk], [("mrow", 0)])
        E("s", lambda e: e.activation(dst[:, :], src[:, :], AF.Square, accum_out=sqrowL[p][:, :]),
          [sk], [("sqrow", 0), dk])
        E("v", lambda e: e.tensor_scalar_mul(mrowL[p][:, :], mrowL[p][:, :], 1.0 / D), [("mrow", 0)], [("mrow", 0)])
        E("v", lambda e: e.tensor_scalar_mul(sqrowL[p][:, :], sqrowL[p][:, :], 1.0 / D), [("sqrow", 0)], [("sqrow", 0)])
        E("v", lambda e: e.tensor_tensor(vrowL[0][:, :], mrowL[p][:, :], mrowL[p][:, :], ALU.mult),
          [("mrow", 0)], [("vrow", 0)])
        E("v", lambda e: e.tensor_tensor(vrowL[0][:, :], sqrowL[p][:, :], vrowL[0][:, :], ALU.subtract),
          [("sqrow", 0), ("vrow", 0)], [("vrow", 0)])
        E("s", lambda e: e.activation(rstdL[p][:, :], vrowL[0][:, :], AF.Ln, bias=cst[:, 0:1]),
          [("vrow", 0), "cst"], [("rstd", 0)])
        E("s", lambda e: e.activation(rstdL[p][:, :], rstdL[p][:, :], AF.Exp, scale=-0.5),
          [("rstd", 0)], [("rstd", 0)])
        E("v", lambda e: e.tensor_scalar(dst[:, :], src[:, :], mrowL[p][:, 0:1], rstdL[p][:, 0:1],
                                         ALU.subtract, ALU.mult),
          [sk, ("mrow", 0), ("rstd", 0)], [dk])
        if dst2 is not None:
            E("v", lambda e: e.tensor_scalar(dst2[:, :], src[:, :], mrowL[p][:, 0:1], rstdL[p][:, 0:1],
                                             ALU.subtract, ALU.mult),
              [sk, ("mrow", 0), ("rstd", 0)], [d2k])

    def emit_tp8(src, sk, dst, dk, nblk=8, doff=0):
        for b in range(nblk):
            E("t", lambda e, b=b: e.transpose(PT[:, b * 128:(b + 1) * 128],
                                              src[:, b * 128:(b + 1) * 128], id_b[:, :]),
              [sk, "id_b"], ["PT"])
        E("v", lambda e: e.tensor_copy(dst[:, doff:doff + nblk * 128], PT[:, 0:nblk * 128]),
          ["PT"], [dk])

    def emit_proj(fm, fmk, wview, wk, psum, pk, N, nk=8, wN=None):
        wN = N if wN is None else wN
        nj = (N + 511) // 512
        for c in range(nk):
            for j in range(nj):
                n0, n1 = j * 512, min(N, (j + 1) * 512)
                E("t", lambda e, c=c, n0=n0, n1=n1: e.matmul(
                    psum[:, n0:n1], fm[:, c * 128:(c + 1) * 128], wview[:, c, n0:n1],
                    start=(c == 0), stop=(c == nk - 1)),
                  [fmk, wk], [pk])

    def emit_fp4(src, sk, dst, dk, Wd, p=0):
        nb = Wd // 16
        sv = src[:, 0:Wd].rearrange("p (b i) -> p b i", i=16)
        E("v", lambda e: e.reduce_max(qbmL[p][:, 0:nb], sv, axis=AX.X, apply_absolute_value=True),
          [sk], [("qbm", 0)])
        E("v", lambda e: e.tensor_scalar(qscL[p][:, 0:nb], qbmL[p][:, 0:nb], 1.0 / 6.0, 1e-12,
                                         ALU.mult, ALU.max), [("qbm", 0)], [("qsc", 0)])
        E("v", lambda e: e.reciprocal(qrcL[p][:, 0:nb], qscL[p][:, 0:nb]), [("qsc", 0)], [("qrc", 0)])
        rcb = qrcL[p][:, 0:nb].broadcast_to([128, nb, 16])
        E("v", lambda e: e.tensor_tensor(qxnL[p][:, 0:Wd].rearrange("p (b i) -> p b i", i=16),
                                         sv, rcb, ALU.mult), [sk, ("qrc", 0)], [("qxn", 0)])
        E("v", lambda e: e.tensor_scalar_mul(qi1L[p][:, 0:Wd], qxnL[p][:, 0:Wd], 2.0), [("qxn", 0)], [("qi1", 0)])
        E("v", lambda e: e.tensor_scalar_mul(qaL[p][:, 0:Wd], qi1L[p][:, 0:Wd], 0.5), [("qi1", 0)], [("qa", 0)])
        E("v", lambda e: e.tensor_copy(qi1L[p][:, 0:Wd], qxnL[p][:, 0:Wd]), [("qxn", 0)], [("qi1", 0)])
        E("g", lambda e: e.tensor_scalar(qbL[p][:, 0:Wd], qi1L[p][:, 0:Wd], 4.0, -4.0, ALU.min, ALU.max),
          [("qi1", 0)], [("qb", 0)])
        E("s", lambda e: e.activation(qazL[p][:, 0:Wd], qxnL[p][:, 0:Wd], AF.Abs),
          [("qxn", 0)], [("qaz", 0)])
        E("v", lambda e: e.tensor_single_scalar(qm1L[p][:, 0:Wd], qazL[p][:, 0:Wd], 2.25, ALU.is_lt),
          [("qaz", 0)], [("qm1", 0)])
        E("v", lambda e: e.tensor_single_scalar(qm2L[p][:, 0:Wd], qazL[p][:, 0:Wd], 5.0, ALU.is_lt),
          [("qaz", 0)], [("qm2", 0)])
        E("v", lambda e: e.tensor_single_scalar(qazL[p][:, 0:Wd], qxnL[p][:, 0:Wd], 0.0, ALU.is_ge),
          [("qxn", 0), ("qm1", 0), ("qm2", 0)], [("qaz", 0)])
        E("g", lambda e: e.tensor_scalar(qazL[p][:, 0:Wd], qazL[p][:, 0:Wd], 12.0, -6.0,
                                         ALU.mult, ALU.add), [("qaz", 0)], [("qaz", 0)])
        E("v", lambda e: e.select(qtmpL[p][:, 0:Wd], qm2L[p][:, 0:Wd], qbL[p][:, 0:Wd], qazL[p][:, 0:Wd],
                                  add_drain=True),
          [("qm2", 0), ("qb", 0), ("qaz", 0)], [("qtmp", 0)])
        E("v", lambda e: e.select(qbL[p][:, 0:Wd], qm1L[p][:, 0:Wd], qaL[p][:, 0:Wd], qtmpL[p][:, 0:Wd],
                                  add_drain=True),
          [("qm1", 0), ("qa", 0), ("qtmp", 0)], [("qb", 0)])
        scb = qscL[p][:, 0:nb].broadcast_to([128, nb, 16])
        E("v", lambda e: e.tensor_tensor(dst[:, 0:Wd].rearrange("p (b i) -> p b i", i=16),
                                         qbL[p][:, 0:Wd].rearrange("p (b i) -> p b i", i=16),
                                         scb, ALU.mult), [("qb", 0), ("qsc", 0)], [dk])

    def arA_view(slot, width=8192, cdim=8):
        off = slot * 8192
        return arA[:, off:off + width].rearrange("p (c n) -> p c n", c=cdim), ("arA", slot)

    def arB_view(slot, width=8192, cdim=8):
        off = slot * 8192
        return arB[:, off:off + width].rearrange("p (c n) -> p c n", c=cdim), ("arB", slot)

    E("g", lambda e: e.memset(ones_f[:, :], 1.0), [], ["ones_f"])
    E("g", lambda e: e.memset(ltri_f[:, :], 1.0), [], ["ltri_f"])
    E("g", lambda e: e.affine_select(ltri_f[:, :], ltri_f[:, :], [[1, 128]], ALU.is_ge, 0.0,
                                     base=0, channel_multiplier=-1), ["ltri_f"], ["ltri_f"])
    E("g", lambda e: e.memset(id_b[:, :], 1.0), [], ["id_b"])
    E("g", lambda e: e.affine_select(id_b[:, :], id_b[:, :], [[1, 128]], ALU.is_equal, 0.0,
                                     base=0, channel_multiplier=-1), ["id_b"], ["id_b"])
    E("g", lambda e: e.memset(id_f[:, :], 1.0), [], ["id_f"])
    E("g", lambda e: e.affine_select(id_f[:, :], id_f[:, :], [[1, 128]], ALU.is_equal, 0.0,
                                     base=0, channel_multiplier=-1), ["id_f"], ["id_f"])
    E("g", lambda e: e.memset(onehot[:, :], 1.0), [], ["onehot"])
    E("g", lambda e: e.affine_select(onehot[:, :], onehot[:, :], [[1, 16]],
                                     ALU.is_equal, 0.0, base=0, channel_multiplier=-1),
      ["onehot"], ["onehot"])
    E("g", lambda e: e.memset(onesb[:, :], 1.0), [], ["onesb"])
    E("g", lambda e: e.memset(cst[:, 0:1], LN_EPS), [], ["cst"])
    E("g", lambda e: e.memset(cst[:, 1:2], RMS_EPS), ["cst"], ["cst"])
    E("g", lambda e: e.memset(cst[:, 2:3], 1.0), ["cst"], ["cst"])
    E("g", lambda e: e.memset(cst[:, 3:4], L1_EPS), ["cst"], ["cst"])
    dma_g(bd_rep[:, :], W["a0_bd"].partition_broadcast(128), [], ["bd_rep"])
    dma_g(nw_rep[:, :], W["a0_nw"].partition_broadcast(128), [], ["rep0"])
    dma_g(emask_rep[:, :], emask_in.partition_broadcast(128), [], ["emask_rep"])
    dma_g(m1_rep[:, :], m1_in.partition_broadcast(128), [], ["m1_rep"])
    dma_g(rw_sb[:, :].rearrange("p (a f) -> p a f", a=8),
          W["a1_rw"].rearrange("(a b) d f -> (b d) a f", b=2), [], ["rw_sb"], slow=True)
    dma_g(bq_sb[:, :], W["a1_bq"].rearrange("(a b d) -> (b d) a", a=8, b=2), [], ["bq_sb"],
          slow=True)
    dma_g(bk_sb[:, :], W["a1_bk"].rearrange("(a b d) -> (b d) a", a=8, b=2), [], ["bk_sb"],
          slow=True)
    for nm in ["a0_Wq", "a0_Wk", "a0_Wg", "a0_Wv", "a0_Wog", "a0_Wo",
               "f0_wg", "f0_wu", "f0_wd", "a1_Wq", "a1_Wk"]:
        dma_g(stg[nm][:, :], W[nm][:, :], [], ["stg_" + nm])

    P1W = {}
    for i, nm in enumerate(["a0_Wq", "a0_Wk", "a0_Wv"]):
        view, key = arA_view(i)
        P1W[nm] = (view, key)
        for c in range(8):
            tp_q(view[:, c, :], stg[nm][:, c * 128:(c + 1) * 128], ["stg_" + nm], [key], chain=(key[0],))
    vg, kg = arB_view(2)
    P1W["a0_Wg"] = (vg, kg)
    for c in range(8):
        tp_q(vg[:, c, :], stg["a0_Wg"][:, c * 128:(c + 1) * 128], ["stg_a0_Wg"], [kg], chain=(kg[0],))
    dma_g(wstg[0][0:16, :], W["a0_Wd"][:, :], [], ["wstg0"])
    for c in range(8):
        E("t", lambda e, c=c: e.transpose(PT[:, c * 16:(c + 1) * 16],
                                          wstg[0][0:16, c * 128:(c + 1) * 128], id_b[0:16, 0:16]),
          ["wstg0", "id_b"], ["PT"])
    E("v", lambda e: e.tensor_copy(wdT_sb[:, :], PT[:, 0:128]), ["PT"], ["wdT_sb"])
    wdT_view = wdT_sb[:, :].rearrange("p (c n) -> p c n", c=8)
    P2W = {}
    for i, nm in enumerate(["a0_Wog", "a0_Wo"]):
        view, key = arB_view(i)
        P2W[nm] = (view, key)
        for c in range(8):
            tp_q(view[:, c, :], stg[nm][:, c * 128:(c + 1) * 128], ["stg_" + nm], [key], chain=(key[0],))

    E("v", lambda e: e.memset(S_run[:, :], 0.0), [], ["S_run"])
    E("v", lambda e: e.memset(G_run[:, :], 0.0), [], ["G_run"])

    def p1_tile(t):
        p = t % 2
        xk, b1k, b2k, b3k, b4k = ("xt", p), ("bt1", p), ("bt2", p), ("bt3", p), ("bt4", p)
        dma_q(xt[p][:, :], x_in[t * 128:(t + 1) * 128, :], [], [xk])
        emit_ln(xt[p], xk, bt1[p], b1k)
        emit_tp8(bt1[p], b1k, bt2[p], b2k)
        dma_a(xnfm_s[:, :, t * 128:(t + 1) * 128].rearrange("a p n -> p a n"),
              bt2[p][:, :].rearrange("p (a n) -> p a n", a=8), [b2k], [("xnfm_s", t)])
        emit_proj(bt2[p], b2k, *P1W["a0_Wg"], PC, "PC", D)
        for c in range(8):
            E("t", lambda e, c=c: e.matmul(PA[:, 0:16], bt2[p][:, c * 128:(c + 1) * 128],
                                           wdT_view[:, c, :], start=(c == 0), stop=(c == 7)),
              [b2k, "wdT_sb"], ["PA"])
        E("v", lambda e: e.tensor_tensor(zrowL[0][:, :], PA[:, 0:16], bd_rep[:, :], ALU.add),
          ["PA", "bd_rep"], [("zrow", 0)])
        E("s", lambda e: e.activation(azrowL[0][:, :], zrowL[0][:, :], AF.Abs),
          [("zrow", 0)], [("azrow", 0)])
        E("s", lambda e: e.activation(erowL[0][:, :], azrowL[0][:, :], AF.Exp, scale=-1.0),
          [("azrow", 0)], [("erow", 0)])
        E("s", lambda e: e.activation(lrowL[0][:, :], erowL[0][:, :], AF.Ln, bias=cst[:, 2:3]),
          [("erow", 0), "cst"], [("lrow", 0)])
        E("v", lambda e: e.tensor_scalar(gdecL[p][:, :], zrowL[0][:, :], -1.0, 0.0, ALU.mult, ALU.min),
          [("zrow", 0)], [("gdec", p)])
        E("v", lambda e: e.tensor_tensor(gdecL[p][:, :], gdecL[p][:, :], lrowL[0][:, :], ALU.subtract),
          [("gdec", p), ("lrow", 0)], [("gdec", p)])
        E("s", lambda e: e.activation(xt[p][:, :], PC[:, :], AF.Exp, scale=-1.0),
          ["PC"], [("xt", p)])
        E("v", lambda e: e.tensor_scalar_add(xt[p][:, :], xt[p][:, :], 1.0), [("xt", p)], [("xt", p)])
        E("v", lambda e: e.reciprocal(gsigL[p][:, :], xt[p][:, :]), [("xt", p)], [("gsig", p)])
        E("t", lambda e: e.matmul(PB[:, 0:16], ltri_f[:, :], gdecL[p][:, :], start=True, stop=True),
          ["ltri_f", ("gdec", p)], ["PB"])
        E("t", lambda e: e.matmul(PB[:, 16:32], ones_f[:, :], gdecL[p][:, :], start=True, stop=True),
          ["ones_f", ("gdec", p), "PB"], ["PB"])
        E("v", lambda e: e.tensor_copy(Gc_st[:, t * H:(t + 1) * H], PB[:, 0:16]),
          ["PB"], [("Gc_st", t)])
        E("v", lambda e: e.tensor_copy(csumL[p][:, :], PB[:, 16:32]), ["PB"], [("csum", p)])
        E("v", lambda e: e.tensor_copy(Gpre_st[:, t * H:(t + 1) * H], G_run[:, :]),
          ["G_run"], [("Gpre_st", t)])
        E("v", lambda e: e.tensor_tensor(G_run[:, :], G_run[:, :], csumL[p][:, :], ALU.add),
          ["G_run", ("csum", p)], ["G_run"])
        E("s", lambda e: e.activation(lamL[p][:, :], csumL[p][:, :], AF.Exp), [("csum", p)], [("lam", p)])
        E("v", lambda e: e.tensor_copy(lam_arL[p][0:64, :], lamL[p][0:64, 0:16:2]), [("lam", p)], [("lam_ar", p)])
        E("v", lambda e: e.tensor_copy(lam_arL[p][64:128, :], lamL[p][64:128, 1:16:2]),
          [("lam", p), ("lam_ar", p)], [("lam_ar", p)])
        emit_proj(bt2[p], b2k, *P1W["a0_Wq"], PC, "PC", D)
        emit_proj(bt2[p], b2k, *P1W["a0_Wk"], PD, "PD", D)
        E("v", lambda e: e.tensor_copy(bt3[p][:, :], PC[:, :]), ["PC"], [b3k])
        E("v", lambda e: e.tensor_tensor(bt1[p][:, :], PD[:, :], gsigL[p][:, :], ALU.mult),
          ["PD", ("gsig", p)], [b1k])
        emit_proj(bt2[p], b2k, *P1W["a0_Wv"], PC, "PC", D)
        E("v", lambda e: e.tensor_copy(bt4[p][:, :], PC[:, :]), ["PC"], [b4k])
        E("v", lambda e: e.tensor_tensor(ekL[p][:, :], csumL[p][:, :], Gc_st[:, t * H:(t + 1) * H],
                                         ALU.subtract), [("csum", p), ("Gc_st", t)], [("ek", p)])
        E("s", lambda e: e.activation(ekL[p][:, :], ekL[p][:, :], AF.Exp), [("ek", p)], [("ek", p)])
        ekb = ekL[p][:, :].broadcast_to([128, H, DH])
        E("v", lambda e, ekb=ekb: e.tensor_tensor(
            bt2[p][:, :].rearrange("p (h d) -> p h d", h=H),
            bt1[p][:, :].rearrange("p (h d) -> p h d", h=H), ekb, ALU.mult),
          [b1k, ("ek", p)], [b2k])
        for h in range(H):
            par, hp = h % 2, h // 2
            E("t", lambda e, h=h, par=par, hp=hp: e.matmul(
                PE[par * 64:par * 64 + 64, hp * 64:(hp + 1) * 64],
                bt2[p][:, h * 64:(h + 1) * 64], bt4[p][:, h * 64:(h + 1) * 64],
                start=True, stop=True, tile_position=(0, par * 64)),
              [b2k, b4k], ["PE"])
        E("v", lambda e: e.tensor_copy(sprefb[p][:, :], S_run[:, :]), ["S_run"], [("sprefb", p)])
        dma_a(spref_s[t, :, :], sprefb[p][:, :], [("sprefb", p)], [("spref_s", t)])
        lab = lam_arL[p][:, :].broadcast_to([128, 8, DH])
        E("v", lambda e, lab=lab: e.tensor_tensor(
            S_run[:, :].rearrange("p (a d) -> p a d", a=8),
            S_run[:, :].rearrange("p (a d) -> p a d", a=8), lab, ALU.mult),
          ["S_run", ("lam_ar", p)], ["S_run"])
        E("v", lambda e: e.tensor_tensor(S_run[:, :], S_run[:, :], PE[:, :], ALU.add),
          ["S_run", "PE"], ["S_run"])
        dma_a(q_s[t * 128:(t + 1) * 128, :], bt3[p][:, :], [b3k], [("q_s", t)])
        dma_a(kt_s[t * 128:(t + 1) * 128, :], bt1[p][:, :], [b1k], [("kt_s", t)])
        dma_a(v_s[t * 128:(t + 1) * 128, :], bt4[p][:, :], [b4k], [("v_s", t)])

    for t in range(nT):
        p1_tile(t)
    RGB = [[0, 1, 2, 3], [4, 5, 6, 7]]
    dma_a(cc1_in_s[:, :], S_run[:, :], ["S_run"], ["cc1_in_s"])
    dma_a(cc1_in_g[:, :], G_run[0:1, :], ["G_run"], ["cc1_in_g"])
    E("g", lambda e: e.collective_compute("AllGather", ALU.bypass, replica_groups=RGB,
                                          ins=[cc1_in_s[:, :].opt()],
                                          outs=[cc1_out_s[:, :].opt()]),
      ["cc1_in_s"], ["cc1_out_s"])
    E("g", lambda e: e.collective_compute("AllGather", ALU.bypass, replica_groups=RGB,
                                          ins=[cc1_in_g[:, :].opt()],
                                          outs=[cc1_out_g[:, :].opt()]),
      ["cc1_in_g"], ["cc1_out_g"])
    dma_g(gall[:, :], cc1_out_g[:, :].rearrange("r h -> (r h)").partition_broadcast(128),
          ["cc1_out_g"], ["gall"])
    for u in range(4):
        def mk(e, u=u):
            em = emask_rep[:, u:16:4].broadcast_to([128, 4, H])
            gu = gall[:, u * H:(u + 1) * H].broadcast_to([128, H, 4]).transpose([0, 2, 1])
            return e.tensor_tensor(tmpA[:, 0:64].rearrange("p (a h) -> p a h", a=4), em, gu,
                                   ALU.mult)
        E("v", mk, ["emask_rep", "gall"], ["tmpA"])
        if u == 0:
            E("v", lambda e: e.tensor_copy(ccoef[:, :], tmpA[:, 0:64]), ["tmpA"], ["ccoef"])
        else:
            E("v", lambda e: e.tensor_tensor(ccoef[:, :], ccoef[:, :], tmpA[:, 0:64], ALU.add),
              ["ccoef", "tmpA"], ["ccoef"])
    E("s", lambda e: e.activation(ccoef[:, :], ccoef[:, :], AF.Exp), ["ccoef"], ["ccoef"])
    m1b = m1_rep[:, :].broadcast_to([128, 4, H])
    E("v", lambda e: e.tensor_tensor(ccoef[:, :].rearrange("p (a h) -> p a h", a=4),
                                     ccoef[:, :].rearrange("p (a h) -> p a h", a=4), m1b,
                                     ALU.mult), ["ccoef", "m1_rep"], ["ccoef"])
    E("v", lambda e: e.memset(S_in[:, :], 0.0), [], ["S_in"])
    for pp in range(4):
        E("v", lambda e, pp=pp: e.tensor_copy(cfar[0:64, pp * 8:(pp + 1) * 8],
                                              ccoef[0:64, pp * H:(pp + 1) * H:2]),
          ["ccoef"], ["cfar"])
        E("v", lambda e, pp=pp: e.tensor_copy(cfar[64:128, pp * 8:(pp + 1) * 8],
                                              ccoef[64:128, pp * H + 1:(pp + 1) * H:2]),
          ["ccoef", "cfar"], ["cfar"])
        dma_q(qxnL[0][:, 0:512], cc1_out_s[pp * 128:(pp + 1) * 128, :], ["cc1_out_s"], [("qxn", 0)])
        E("v", lambda e, pp=pp: e.tensor_tensor(
            tmpA[:, 0:512].rearrange("p (a d) -> p a d", a=8),
            qxnL[0][:, 0:512].rearrange("p (a d) -> p a d", a=8),
            cfar[:, pp * 8:(pp + 1) * 8].broadcast_to([128, 8, DH]), ALU.mult),
          [("qxn", 0), "cfar"], ["tmpA"])
        E("v", lambda e: e.tensor_tensor(S_in[:, :], S_in[:, :], tmpA[:, 0:512], ALU.add),
          ["S_in", "tmpA"], ["S_in"])

    sch.rekey(["PA", "PB"], [("PA", 0), ("PA", 1), ("PA", 2), ("PA", 3),
                             ("PB", 0), ("PB", 1), ("PB", 2), ("PB", 3)])
    def p2_tile(t):
        p = t % 2
        b1k, b2k, b3k, b4k = ("bt1", p), ("bt2", p), ("bt3", p), ("bt4", p)
        hk = ("ht", p)
        dma_q(bt3[p][:, :], q_s[t * 128:(t + 1) * 128, :], [("q_s", t)], [b3k])
        dma_q(bt1[p][:, :], kt_s[t * 128:(t + 1) * 128, :], [("kt_s", t)], [b1k])
        dma_q(bt4[p][:, :], v_s[t * 128:(t + 1) * 128, :], [("v_s", t)], [b4k])
        dma_q(bt2[p][:, :].rearrange("p (a n) -> p a n", a=8),
              xnfm_s[:, :, t * 128:(t + 1) * 128].rearrange("a p n -> p a n"),
              [("xnfm_s", t)], [b2k])
        dma_q(sprefb[p][:, :], spref_s[t, :, :], [("spref_s", t)], [("sprefb", p)])
        dma_q(xt[p][:, :], x_in[t * 128:(t + 1) * 128, :], [], [("xt", p)])
        emit_proj(bt2[p], b2k, *P2W["a0_Wog"], PC, "PC", D)
        E("s", lambda e: e.activation(yt[p][:, :], PC[:, :], AF.Exp, scale=-1.0), ["PC"], [("yt", p)])
        E("v", lambda e: e.tensor_scalar_add(yt[p][:, :], yt[p][:, :], 1.0), [("yt", p)], [("yt", p)])
        E("v", lambda e: e.reciprocal(gsigL[p][:, :], yt[p][:, :]), [("yt", p)], [("gsig", p)])
        emit_tp8(bt3[p], b3k, ht[p], hk, doff=0)
        emit_tp8(bt1[p], b1k, ht[p], hk, doff=1024)
        qfm = ht[p][:, 0:1024]
        kfm = ht[p][:, 1024:2048]
        E("t", lambda e: e.transpose(PE[0:16, 0:128], Gc_st[:, t * H:(t + 1) * H], id_f[:, :]),
          [("Gc_st", t), "id_f"], ["PE"])
        E("v", lambda e: e.tensor_copy(gchmL[p][:, :], PE[0:16, 0:128]), ["PE"], [("gchm", p)])
        E("v", lambda e: e.tensor_scalar_mul(negGcL[p][:, :], Gc_st[:, t * H:(t + 1) * H], -1.0),
          [("Gc_st", t)], [("negGc", p)])
        E("s", lambda e: e.activation(a_tL[p][:, :], Gc_st[:, t * H:(t + 1) * H], AF.Exp),
          [("Gc_st", t)], [("a_t", p)])
        E("s", lambda e: e.activation(lamL[p][:, :], Gpre_st[:, t * H:(t + 1) * H], AF.Exp),
          [("Gpre_st", t)], [("lam", p)])
        E("v", lambda e: e.tensor_copy(lam_arL[p][0:64, :], lamL[p][0:64, 0:16:2]), [("lam", p)], [("lam_ar", p)])
        E("v", lambda e: e.tensor_copy(lam_arL[p][64:128, :], lamL[p][64:128, 1:16:2]),
          [("lam", p), ("lam_ar", p)], [("lam_ar", p)])
        E("v", lambda e: e.tensor_tensor(
            yt[p][:, 0:512].rearrange("p (a d) -> p a d", a=8),
            S_in[:, :].rearrange("p (a d) -> p a d", a=8),
            lam_arL[p][:, :].broadcast_to([128, 8, DH]), ALU.mult),
          ["S_in", ("lam_ar", p)], [("yt", p)])
        E("v", lambda e: e.tensor_tensor(sprefcL[p][:, :], sprefb[p][:, :], yt[p][:, 0:512], ALU.add),
          [("sprefb", p), ("yt", p)], [("sprefc", p)])
        for h in range(H):
            par, hp = h % 2, h // 2
            slot, ds = h % 4, h % 2
            E("t", lambda e, h=h, slot=slot: e.matmul(
                PA[:, slot * 128:(slot + 1) * 128],
                onehot[:, h:h + 1].broadcast_to([16, 1, 128]),
                gchmL[p][:, :], start=True, stop=True),
              ["onehot", ("gchm", p)], [("PA", slot)])
            E("s", lambda e, h=h, slot=slot, ds=ds: e.activation(
                Dt[ds][:, :], PA[:, slot * 128:(slot + 1) * 128], AF.Exp,
                bias=negGcL[p][:, h:h + 1]),
              [("PA", slot), ("negGc", p)], [("Dt", ds)])
            E("t", lambda e, par=par, hp=hp, slot=slot: e.matmul(
                PB[:, slot * 128:(slot + 1) * 128],
                kfm[par * 64:par * 64 + 64, hp * 128:(hp + 1) * 128],
                qfm[par * 64:par * 64 + 64, hp * 128:(hp + 1) * 128],
                start=True, stop=True),
              [hk], [("PB", slot)])
            E("v", lambda e, slot=slot, ds=ds: e.tensor_tensor(
                At[ds][:, :], PB[:, slot * 128:(slot + 1) * 128], Dt[ds][:, :], ALU.mult),
              [("PB", slot), ("Dt", ds)], [("At", ds)])
            E("g", lambda e, ds=ds: e.affine_select(
                At[ds][:, :], At[ds][:, :], [[1, 128]], ALU.is_ge, 0.0,
                base=0, channel_multiplier=-1),
              [("At", ds)], [("At", ds)])
            if dbg and t == 0 and h == 0:
                dma_g(dbg_D[:, :], Dt[ds][:, :], [("Dt", ds)], ["dbg_D"], chain=("dbgD",))
                dma_g(dbg_gchm[:, :], gchmL[p][:, :], [("gchm", p)], ["dbg_gchm"], chain=("dbgG",))
            E("t", lambda e, h=h, ds=ds: e.matmul(
                PC[:, h * 64:(h + 1) * 64], At[ds][:, :], bt4[p][:, h * 64:(h + 1) * 64],
                start=True, stop=True),
              [("At", ds), b4k], ["PC"])
            E("t", lambda e, h=h, par=par, hp=hp: e.matmul(
                PD[:, h * 64:(h + 1) * 64],
                qfm[par * 64:par * 64 + 64, hp * 128:(hp + 1) * 128],
                sprefcL[p][par * 64:par * 64 + 64, hp * 64:(hp + 1) * 64],
                start=True, stop=True),
              [hk, ("sprefc", p)], ["PD"])
        E("v", lambda e: e.tensor_tensor(
            yt[p][:, :].rearrange("p (h d) -> p h d", h=H),
            PD[:, :].rearrange("p (h d) -> p h d", h=H),
            a_tL[p][:, :].broadcast_to([128, H, DH]), ALU.mult),
          ["PD", ("a_t", p)], [("yt", p)])
        E("v", lambda e: e.tensor_tensor(yt[p][:, :], yt[p][:, :], PC[:, :], ALU.add),
          [("yt", p), "PC"], [("yt", p)])
        if dbg:
            dma_g(dbg_o[t * 128:(t + 1) * 128, :], yt[p][:, :], [("yt", p)], [("dbg_o", t)], chain=("dbgO",))
        E("s", lambda e: e.activation(bt1[p][:, :], yt[p][:, :], AF.Square,
                                      accum_out=sqrowL[p][:, :]),
          [("yt", p)], [("sqrow", p), b1k])
        E("v", lambda e: e.tensor_scalar_mul(sqrowL[p][:, :], sqrowL[p][:, :], 1.0 / D),
          [("sqrow", p)], [("sqrow", p)])
        E("s", lambda e: e.activation(rstdL[p][:, :], sqrowL[p][:, :], AF.Ln, bias=cst[:, 1:2]),
          [("sqrow", p), "cst"], [("rstd", p)])
        E("s", lambda e: e.activation(rstdL[p][:, :], rstdL[p][:, :], AF.Exp, scale=-0.5),
          [("rstd", p)], [("rstd", p)])
        if dbg:
            dma_g(dbg_m2[t * 128:(t + 1) * 128, :], yt[p][:, :], [("yt", p)], [("dbg_m2", t)])
        E("v", lambda e: e.tensor_scalar(yt[p][:, :], yt[p][:, :], rstdL[p][:, 0:1], None, ALU.mult),
          [("yt", p), ("rstd", p)], [("yt", p)])
        E("v", lambda e: e.tensor_tensor(yt[p][:, :], yt[p][:, :], nw_rep[:, :], ALU.mult),
          [("yt", p), "rep0"], [("yt", p)])
        if dbg:
            dma_g(dbg_m[t * 128:(t + 1) * 128, :], yt[p][:, :], [("yt", p)], [("dbg_m", t)])
            dma_g(dbg_s[t * 128:(t + 1) * 128, :], gsigL[p][:, :], [("gsig", p)], [("dbg_s", t)])
        E("v", lambda e: e.tensor_tensor(bt1[p][:, :], yt[p][:, :], gsigL[p][:, :], ALU.mult),
          [("yt", p), ("gsig", p)], [b1k])
        if dbg:
            dma_g(dbg_g[t * 128:(t + 1) * 128, :], bt1[p][:, :], [b1k], [("dbg_g", t)], chain=("dbgg",))
            dma_g(dbg_r[t * 128:(t + 1) * 128], rstdL[p][:, :], [("rstd", p)], [("dbg_r", t)], chain=("dbgr",))
        emit_tp8(bt1[p], b1k, ht[p], hk, doff=2048)
        emit_proj(ht[p][:, 2048:3072], hk, *P2W["a0_Wo"], PD, "PD", D)
        E("v", lambda e: e.tensor_tensor(yt[p][:, :], PD[:, :], xt[p][:, :], ALU.add),
          ["PD", ("xt", p)], [("yt", p)])
        dma_a(partial_s[t * 128:(t + 1) * 128, :], yt[p][:, :], [("yt", p)], [("partial_s", t)])

    for t in range(nT):
        p2_tile(t)
    sch.rekey([("PA", 0), ("PA", 1), ("PA", 2), ("PA", 3),
               ("PB", 0), ("PB", 1), ("PB", 2), ("PB", 3)], ["PA", "PB"])

    NF0 = F0 // 512

    def load_f0_pair(i6):
        half = (i6 % 2) * 8192
        gv_ = arA[:, half:half + 4096].rearrange("p (c n) -> p c n", c=8)
        gk_ = ("arA", (i6 % 2) * 2)
        uv_ = arA[:, half + 4096:half + 8192].rearrange("p (c n) -> p c n", c=8)
        uk_ = ("arA", (i6 % 2) * 2 + 1)
        for c in range(8):
            tp_q(gv_[:, c, :], stg["f0_wg"][i6 * 512:(i6 + 1) * 512, c * 128:(c + 1) * 128],
                 ["stg_f0_wg"], [gk_], chain=(gk_[0],))
            tp_q(uv_[:, c, :], stg["f0_wu"][i6 * 512:(i6 + 1) * 512, c * 128:(c + 1) * 128],
                 ["stg_f0_wu"], [uk_], chain=(uk_[0],))
        return (gv_, gk_), (uv_, uk_)

    def p3a_tile(t, i6, gv, gk, uv, uk):
        p = t % 2
        b2k = ("bt2", p)
        dma_q(bt2[p][:, :].rearrange("p (a n) -> p a n", a=8),
              xnfm_s[:, :, t * 128:(t + 1) * 128].rearrange("a p n -> p a n"),
              [("xnfm_s", t)], [b2k])
        emit_proj(bt2[p], b2k, gv, gk, PC, "PC", 512)
        emit_proj(bt2[p], b2k, uv, uk, PD, "PD", 512)
        E("s", lambda e: e.activation(sg32L[0][:, :], PC[:, 0:512], AF.Silu, scale=INV127),
          ["PC"], [("sg32", 0)])
        E("v", lambda e: e.tensor_tensor(bt1[p][:, 0:512], sg32L[0][:, :], PD[:, 0:512],
                                         ALU.mult), [("sg32", 0), "PD"], [("bt1", p)])
        dma_a(h0_s[t * 128:(t + 1) * 128, i6 * 512:(i6 + 1) * 512], bt1[p][:, 0:512],
              [("bt1", p)], [("h0_s", t, i6)])

    f0v = {0: load_f0_pair(0)}
    for i6 in range(NF0):
        if i6 + 1 < NF0:
            f0v[i6 + 1] = load_f0_pair(i6 + 1)
        (gv, gk), (uv, uk) = f0v.pop(i6)
        for t in range(nT):
            p3a_tile(t, i6, gv, gk, uv, uk)

    w3d_view = arB[:, 0:F0 // 128 * 1024].rearrange("p (c n) -> p c n", c=F0 // 128)
    w3d_keys = ("arB", 0)
    for c in range(F0 // 128):
        tp_q(w3d_view[:, c, :], stg["f0_wd"][:, c * 128:(c + 1) * 128],
             ["stg_f0_wd"], [("arB", 0), ("arB", 1), ("arB", 2)], chain=("arB",))
    def p3b_tile(t):
        p = t % 2
        hk = ("ht", p)
        dma_q(ht[p][:, :], h0_s[t * 128:(t + 1) * 128, :],
              [("h0_s", t, i) for i in range(6)], [hk])
        dma_q(yt[p][:, :], partial_s[t * 128:(t + 1) * 128, :], [("partial_s", t)], [("yt", p)])
        for r in range(3):
            for b in range(8):
                bb = r * 8 + b
                E("t", lambda e, bb=bb, b=b: e.transpose(
                    PT[:, b * 128:(b + 1) * 128], ht[p][:, bb * 128:(bb + 1) * 128], id_b[:, :]),
                  [hk, "id_b"], ["PT"])
            E("v", lambda e, r=r: e.tensor_copy(hfm[:, r * 1024:(r + 1) * 1024], PT[:, :]),
              ["PT"], ["hfm"])
        for c in range(F0 // 128):
            for j in range(2):
                E("t", lambda e, c=c, j=j: e.matmul(
                    PC[:, j * 512:(j + 1) * 512], hfm[:, c * 128:(c + 1) * 128],
                    w3d_view[:, c, j * 512:(j + 1) * 512],
                    start=(c == 0), stop=(c == F0 // 128 - 1)),
                  ["hfm", w3d_keys], ["PC"])
        E("v", lambda e: e.scalar_tensor_tensor(xt[p][:, :], PC[:, :], INV127 * INV127,
                                                yt[p][:, :], ALU.mult, ALU.add),
          ["PC", ("yt", p)], [("xt", p)])
        dma_a(xnew_s[t * 128:(t + 1) * 128, :], xt[p][:, :], [("xt", p)], [("xnew_s", t)])

    for t in range(nT):
        p3b_tile(t)
    if upto == "l0":
        for t in range(nT):
            dma_a(out_ext[t * 128:(t + 1) * 128, :], xnew_s[t * 128:(t + 1) * 128, :],
                  [("xnew_s", t)], [("out", t)])
        sch.finalize()
        return nc

    RG8 = [[0, 1, 2, 3, 4, 5, 6, 7]]
    rows3 = [(0, 128), (128, 256), (256, RG1)]
    for (src_, dst, keyd) in [(wsh_g, ccw_in_g, "ccw_in_g"), (wsh_u, ccw_in_u, "ccw_in_u")]:
        for (r0, r1) in rows3:
            n = r1 - r0
            for j2 in range(2):
                E("q", lambda e, src_=src_, r0=r0, r1=r1, j2=j2: e.dma_start(
                    out=qtmpL[0][0:r1 - r0, :], in_=src_[r0:r1, j2 * 512:(j2 + 1) * 512]),
                    [("qtmp", 0)], [("qtmp", 0)], dma=True)
                emit_fp4(qtmpL[0], ("qtmp", 0), bt4[0], ("bt4", 0), 512)
                E("gd", lambda e, dst=dst, r0=r0, n=n, j2=j2: e.dma_start(
                    out=dst[r0:r0 + n, j2 * 512:(j2 + 1) * 512], in_=bt4[0][0:n, 0:512]),
                    [("bt4", 0)], [keyd], dma=True)
    W6 = [512] * 5 + [256]
    for i6 in range(6):
        w6 = W6[i6]
        r0 = 512 * i6
        E("q", lambda e, r0=r0, w6=w6: e.dma_start(
            out=qtmpL[0][0:RGD, 0:w6], in_=wsh_d[:, r0:r0 + w6]), [("qtmp", 0)], [("qtmp", 0)], dma=True)
        emit_fp4(qtmpL[0], ("qtmp", 0), bt4[0], ("bt4", 0), w6)
        E("gd", lambda e, r0=r0, w6=w6: e.dma_start(
            out=ccw_in_d[:, r0:r0 + w6], in_=bt4[0][0:RGD, 0:w6]),
            [("bt4", 0)], ["ccw_in_d"], dma=True)
    E("g", lambda e: e.collective_compute("AllGather", ALU.bypass, replica_groups=RG8,
                                          ins=[ccw_in_g[:, :].opt()], outs=[wgq_s[:, :].opt()]),
      ["ccw_in_g"], ["wgq_s"])
    E("g", lambda e: e.collective_compute("AllGather", ALU.bypass, replica_groups=RG8,
                                          ins=[ccw_in_u[:, :].opt()], outs=[wuq_s[:, :].opt()]),
      ["ccw_in_u"], ["wuq_s"])
    E("g", lambda e: e.collective_compute("AllGather", ALU.bypass, replica_groups=RG8,
                                          ins=[ccw_in_d[:, :].opt()], outs=[wdq_s[:, :].opt()]),
      ["ccw_in_d"], ["wdq_s"])
    for (wnm, dst_s, dkey, bsb, bkey, bdst, bdkey) in [
            ("a1_Wq", qr_s, "qr_s", bq_sb, "bq_sb", bqr_s, "bqr_s"),
            ("a1_Wk", kr_s, "kr_s", bk_sb, "bk_sb", bkr_s, "bkr_s")]:
        for half in range(2):
            for hh in range(8):
                h = half * 8 + hh
                par, hp = h % 2, h // 2
                E("t", lambda e, par=par, hp=hp, hh=hh, bsb=bsb: e.matmul(
                    PB[0:1, hh * 64:(hh + 1) * 64],
                    bsb[par * 64:par * 64 + 64, hp:hp + 1],
                    rw_sb[par * 64:par * 64 + 64, hp * 64:(hp + 1) * 64],
                    start=True, stop=True),
                  [bkey, "rw_sb"], ["PB"])
            E("v", lambda e: e.tensor_copy(tmpA[0:1, 0:512], PB[0:1, 0:512]), ["PB"], ["tmpA"])
            E("gd", lambda e, bdst=bdst, half=half: e.dma_start(
                out=bdst[half * 512:(half + 1) * 512], in_=tmpA[0:1, 0:512]),
              ["tmpA"], [bdkey], dma=True)
        for co in range(8):
            wp = 0
            wk = ("wstg", wp)
            tp_q(wstg[wp][:, :], stg[wnm][:, co * 128:(co + 1) * 128], ["stg_" + wnm], [wk],
                 chain=("wstg",))
            for hh in range(2):
                h = co * 2 + hh
                par, hp = h % 2, h // 2
                for ci in range(8):
                    E("t", lambda e, par=par, hp=hp, ci=ci, wp=wp: e.matmul(
                        PA[:, ci * 64:(ci + 1) * 64],
                        wstg[wp][par * 64:par * 64 + 64, ci * 128:(ci + 1) * 128],
                        rw_sb[par * 64:par * 64 + 64, hp * 64:(hp + 1) * 64],
                        start=True, stop=True),
                      [wk, "rw_sb"], ["PA"])
                E("v", lambda e: e.tensor_copy(qrstg[:, :], PA[:, :]), ["PA"], ["qrstg"])
                E("gd", lambda e, h=h, dst_s=dst_s: e.dma_start(
                    out=dst_s[:, h * 64:(h + 1) * 64].rearrange("(c p) f -> p c f", c=8),
                    in_=qrstg[:, :].rearrange("p (c f) -> p c f", c=8)),
                  ["qrstg"], [(dkey, h)], dma=True)
    dma_g(bqr_rep[:, :], bqr_s.partition_broadcast(128), ["bqr_s"], ["rep1"])
    dma_g(bkr_rep[:, :], bkr_s.partition_broadcast(128), ["bkr_s"], ["rep2"])
    dma_g(bv_rep[:, :], W["a1_bv"].partition_broadcast(128), [], ["rep0"])
    qrv, qrk = arA_view(0)
    krv, krk = arA_view(1)
    wvv, wvk = arA_view(2)
    for c in range(8):
        E("q", lambda e, c=c: e.dma_start(out=qrv[:, c, :], in_=qr_s[c * 128:(c + 1) * 128, :]),
          [(("qr_s", hx)) for hx in range(H)], [qrk], dma=True, chain=(qrk[0],))
        E("q", lambda e, c=c: e.dma_start(out=krv[:, c, :], in_=kr_s[c * 128:(c + 1) * 128, :]),
          [(("kr_s", hx)) for hx in range(H)], [krk], dma=True, chain=(krk[0],))
        dma_g(wvv[:, c, :], W["a1_Wv"][c * 128:(c + 1) * 128, :], [], [wvk], chain=(wvk[0],))
    wo1v, wo1k = arB_view(0)
    for c in range(8):
        dma_g(wo1v[:, c, :], W["a1_Wo"][c * 128:(c + 1) * 128, :], [], [wo1k], chain=(wo1k[0],))

    def p4_tile(t):
        p = t % 2
        b1k, b2k, b3k, b4k = ("bt1", p), ("bt2", p), ("bt3", p), ("bt4", p)
        dma_q(xt[p][:, :], xnew_s[t * 128:(t + 1) * 128, :], [("xnew_s", t)], [("xt", p)])
        emit_ln(xt[p], ("xt", p), bt1[p], b1k, dst2=yt[p], d2k=("yt", p))
        dma_a(xn1_s[t * 128:(t + 1) * 128, :], yt[p][:, :], [("yt", p)], [("xn1_s", t)])
        emit_tp8(bt1[p], b1k, bt2[p], b2k)
        emit_proj(bt2[p], b2k, qrv, qrk, PC, "PC", D)
        emit_proj(bt2[p], b2k, krv, krk, PD, "PD", D)
        E("v", lambda e: e.tensor_tensor(yt[p][:, :], PC[:, :], bqr_rep[:, :], ALU.add),
          ["PC", "rep1"], [("yt", p)])
        E("v", lambda e: e.tensor_scalar(bt3[p][:, :], yt[p][:, :], 0.0, L1_EPS,
                                         ALU.max, ALU.add), [("yt", p)], [b3k])
        E("v", lambda e: e.tensor_tensor(yt[p][:, :], PD[:, :], bkr_rep[:, :], ALU.add),
          ["PD", "rep2"], [("yt", p)])
        E("v", lambda e: e.tensor_scalar(bt4[p][:, :], yt[p][:, :], 0.0, L1_EPS,
                                         ALU.max, ALU.add), [("yt", p)], [b4k])
        dma_a(qp_s[t * 128:(t + 1) * 128, :], bt3[p][:, :], [b3k], [("qp_s", t)])
        emit_proj(bt2[p], b2k, wvv, wvk, PC, "PC", D)
        E("v", lambda e: e.tensor_tensor(bt1[p][:, :], PC[:, :], bv_rep[:, :], ALU.add),
          ["PC", "rep0"], [b1k])
        for h in range(H):
            par, hp = h % 2, h // 2
            first = (t == 0 and h == 0)
            E("t", lambda e, h=h, par=par, hp=hp, first=first: e.matmul(
                PE[par * 64:par * 64 + 64, hp * 64:(hp + 1) * 64],
                bt4[p][:, h * 64:(h + 1) * 64], bt1[p][:, h * 64:(h + 1) * 64],
                start=first, stop=(t == nT - 1 and h == H - 1),
                tile_position=(0, par * 64), skip_group_check=True),
              [b4k, b1k], ["PE"])
            E("t", lambda e, h=h, par=par, hp=hp, first=first: e.matmul(
                PA[par * 64:par * 64 + 64, hp:hp + 1],
                bt4[p][:, h * 64:(h + 1) * 64], onesb[:, :],
                start=first, stop=(t == nT - 1 and h == H - 1),
                tile_position=(0, par * 64), skip_group_check=True),
              [b4k, "onesb"], ["PA"])

    for t in range(nT):
        p4_tile(t)
    dma_g(bo_rep[:, :], W["a1_bo"].partition_broadcast(128), [], ["rep1"])

    E("v", lambda e: e.tensor_copy(tmpA[:, 0:512], PE[:, :]), ["PE"], ["tmpA"])
    dma_a(cc2_in_kv[:, :], tmpA[:, 0:512], ["tmpA"], ["cc2_in_kv"])
    E("v", lambda e: e.tensor_copy(lamL[0][:, 0:8], PA[:, 0:8]), ["PA"], [("lam", 0)])
    dma_a(cc2_in_kp[:, :], lamL[0][:, 0:8], [("lam", 0)], ["cc2_in_kp"])
    E("g", lambda e: e.collective_compute("AllReduce", ALU.add, replica_groups=RGB,
                                          ins=[cc2_in_kv[:, :].opt()],
                                          outs=[cc2_out_kv[:, :].opt()]),
      ["cc2_in_kv"], ["cc2_out_kv"])
    E("g", lambda e: e.collective_compute("AllReduce", ALU.add, replica_groups=RGB,
                                          ins=[cc2_in_kp[:, :].opt()],
                                          outs=[cc2_out_kp[:, :].opt()]),
      ["cc2_in_kp"], ["cc2_out_kp"])
    dma_g(kvg_sb[:, :], cc2_out_kv[:, :], ["cc2_out_kv"], ["kvg_sb"])
    dma_g(kpg_sb[:, :], cc2_out_kp[:, :], ["cc2_out_kp"], ["kpg_sb"])

    def p5_tile(t):
        p = t % 2
        b1k, b3k = ("bt1", p), ("bt3", p)
        hk = ("ht", p)
        dma_q(bt3[p][:, :], qp_s[t * 128:(t + 1) * 128, :], [("qp_s", t)], [b3k])
        dma_q(xt[p][:, :], xnew_s[t * 128:(t + 1) * 128, :], [("xnew_s", t)], [("xt", p)])
        emit_tp8(bt3[p], b3k, ht[p], hk, doff=0)
        for h in range(H):
            par, hp = h % 2, h // 2
            E("t", lambda e, h=h, par=par, hp=hp: e.matmul(
                PC[:, h * 64:(h + 1) * 64],
                ht[p][par * 64:par * 64 + 64, hp * 128:(hp + 1) * 128],
                kvg_sb[par * 64:par * 64 + 64, hp * 64:(hp + 1) * 64],
                start=True, stop=True),
              [hk, "kvg_sb"], ["PC"])
            E("t", lambda e, h=h, par=par, hp=hp: e.matmul(
                PB[:, h:h + 1],
                ht[p][par * 64:par * 64 + 64, hp * 128:(hp + 1) * 128],
                kpg_sb[par * 64:par * 64 + 64, hp:hp + 1],
                start=True, stop=True),
              [hk, "kpg_sb"], ["PB"])
        E("v", lambda e: e.tensor_scalar_add(lamL[p][:, :], PB[:, 0:16], L1_EPS), ["PB"], [("lam", p)])
        E("v", lambda e: e.reciprocal(ekL[p][:, :], lamL[p][:, :]), [("lam", p)], [("ek", p)])
        E("v", lambda e: e.tensor_tensor(
            bt1[p][:, :].rearrange("p (h d) -> p h d", h=H),
            PC[:, :].rearrange("p (h d) -> p h d", h=H),
            ekL[p][:, :].broadcast_to([128, H, DH]), ALU.mult),
          ["PC", ("ek", p)], [b1k])
        emit_tp8(bt1[p], b1k, ht[p], hk, doff=1024)
        emit_proj(ht[p][:, 1024:2048], hk, wo1v, wo1k, PD, "PD", D)
        E("v", lambda e: e.tensor_tensor(yt[p][:, :], PD[:, :], bo_rep[:, :], ALU.add),
          ["PD", "rep1"], [("yt", p)])
        E("v", lambda e: e.tensor_tensor(yt[p][:, :], yt[p][:, :], xt[p][:, :], ALU.add),
          [("yt", p), ("xt", p)], [("yt", p)])
        dma_a(partial2_s[t * 128:(t + 1) * 128, :], yt[p][:, :], [("yt", p)],
              [("partial2_s", t)])

    for t in range(nT):
        p5_tile(t)
    W6a = [512] * 5 + [256]

    def load_f1_pair(i6):
        w6 = W6a[i6]
        half = (i6 % 2) * 8192
        gv_ = arA[:, half:half + 8 * w6].rearrange("p (c n) -> p c n", c=8)
        gk_ = ("arA", (i6 % 2) * 2)
        uv_ = arA[:, half + 8 * w6:half + 16 * w6].rearrange("p (c n) -> p c n", c=8)
        uk_ = ("arA", (i6 % 2) * 2 + 1)
        r0 = 512 * i6
        for c in range(8):
            tp_q(gv_[:, c, :], wgq_s[r0:r0 + w6, c * 128:(c + 1) * 128], ["wgq_s"], [gk_],
                 chain=(gk_[0],))
            tp_q(uv_[:, c, :], wuq_s[r0:r0 + w6, c * 128:(c + 1) * 128], ["wuq_s"], [uk_],
                 chain=(uk_[0],))
        return (gv_, gk_), (uv_, uk_)

    def p6a_tile(t, i6, gv, gk, uv, uk, w6, r0):
        p = t % 2
        b1k, b2k, b4k = ("bt1", p), ("bt2", p), ("bt4", p)
        if i6 == 0:
            dma_q(xt[p][:, :], xn1_s[t * 128:(t + 1) * 128, :], [("xn1_s", t)], [("xt", p)])
            emit_fp4(xt[p][:, 0:512], ("xt", p), bt1[p][:, 0:512], b1k, 512)
            emit_fp4(xt[p][:, 512:1024], ("xt", p), bt1[p][:, 512:1024], b1k, 512)
            emit_tp8(bt1[p], b1k, bt2[p], b2k)
            dma_a(xq_s[:, :, t * 128:(t + 1) * 128].rearrange("a p n -> p a n"),
                  bt2[p][:, :].rearrange("p (a n) -> p a n", a=8), [b2k], [("xq_s", t)])
        else:
            dma_q(bt2[p][:, :].rearrange("p (a n) -> p a n", a=8),
                  xq_s[:, :, t * 128:(t + 1) * 128].rearrange("a p n -> p a n"),
                  [("xq_s", t)], [b2k])
        emit_proj(bt2[p], b2k, gv, gk, PC, "PC", w6)
        emit_proj(bt2[p], b2k, uv, uk, PD, "PD", w6)
        E("s", lambda e, w6=w6: e.activation(sg32L[0][:, 0:w6], PC[:, 0:w6], AF.Silu),
          ["PC"], [("sg32", 0)])
        E("v", lambda e, w6=w6: e.tensor_tensor(hq32L[0][:, 0:w6], sg32L[0][:, 0:w6], PD[:, 0:w6],
                                                ALU.mult), [("sg32", 0), "PD"], [("hq32", 0)])
        emit_fp4(hq32L[0], ("hq32", 0), bt4[p], b4k, w6)
        dma_a(hq_s[t * 128:(t + 1) * 128, r0:r0 + w6], bt4[p][:, 0:w6], [b4k],
              [("hq_s", t, i6)])

    f1v = {0: load_f1_pair(0)}
    for i6 in range(6):
        if i6 + 1 < 6:
            f1v[i6 + 1] = load_f1_pair(i6 + 1)
        (gv, gk), (uv, uk) = f1v.pop(i6)
        for t in range(nT):
            p6a_tile(t, i6, gv, gk, uv, uk, W6a[i6], 512 * i6)

    w6d_view = arB[:, 0:F1 // 128 * 1024].rearrange("p (c n) -> p c n", c=F1 // 128)
    w6d_key = ("arB", 0)
    for c in range(F1 // 128):
        tp_q(w6d_view[:, c, :], wdq_s[:, c * 128:(c + 1) * 128],
             ["wdq_s"], [("arB", 0), ("arB", 1), ("arB", 2)], chain=("arB",))
    def p6b_tile(t):
        p = t % 2
        hk = ("ht", p)
        dma_q(ht[p][:, 0:F1], hq_s[t * 128:(t + 1) * 128, :],
              [("hq_s", t, i) for i in range(6)], [hk])
        dma_q(yt[p][:, :], partial2_s[t * 128:(t + 1) * 128, :], [("partial2_s", t)],
              [("yt", p)])
        nch = F1 // 128
        for r in range((nch + 7) // 8):
            nb = min(8, nch - r * 8)
            for b in range(nb):
                bb = r * 8 + b
                E("t", lambda e, bb=bb, b=b: e.transpose(
                    PT[:, b * 128:(b + 1) * 128], ht[p][:, bb * 128:(bb + 1) * 128], id_b[:, :]),
                  [hk, "id_b"], ["PT"])
            E("v", lambda e, r=r, nb=nb: e.tensor_copy(
                hfm[:, r * 1024:r * 1024 + nb * 128], PT[:, 0:nb * 128]), ["PT"], ["hfm"])
        for c in range(nch):
            for j in range(2):
                E("t", lambda e, c=c, j=j: e.matmul(
                    PC[:, j * 512:(j + 1) * 512], hfm[:, c * 128:(c + 1) * 128],
                    w6d_view[:, c, j * 512:(j + 1) * 512],
                    start=(c == 0), stop=(c == nch - 1)),
                  ["hfm", w6d_key], ["PC"])
        E("v", lambda e: e.tensor_tensor(xt[p][:, :], PC[:, :], yt[p][:, :], ALU.add),
          ["PC", ("yt", p)], [("xt", p)])
        dma_a(out_ext[t * 128:(t + 1) * 128, :], xt[p][:, :], [("xt", p)], [("out", t)])

    for t in range(nT):
        p6b_tile(t)
    sch.finalize()
    return nc


_BUILT = {}


def _inmaps(inputs, T):
    x = np.asarray(inputs["L0_attn_x"], np.float32)
    nm = {
        "a0_Wq": "L0_attn_Wq", "a0_Wk": "L0_attn_Wk", "a0_Wv": "L0_attn_Wv",
        "a0_Wo": "L0_attn_Wo", "a0_Wg": "L0_attn_Wg", "a0_Wog": "L0_attn_Wog",
        "a0_Wd": "L0_attn_Wd", "a0_bd": "L0_attn_bd", "a0_nw": "L0_attn_norm_w",
        "f0_wg": "L0_ffn_w_gate", "f0_wu": "L0_ffn_w_up", "f0_wd": "L0_ffn_w_down",
        "a1_Wq": "L1_attn_Wq", "a1_bq": "L1_attn_bq", "a1_Wk": "L1_attn_Wk",
        "a1_bk": "L1_attn_bk", "a1_Wv": "L1_attn_Wv", "a1_bv": "L1_attn_bv",
        "a1_Wo": "L1_attn_Wo", "a1_bo": "L1_attn_bo", "a1_rw": "L1_attn_rw",
    }
    base = {k: np.ascontiguousarray(np.asarray(inputs[v])) for k, v in nm.items()}
    RG1 = F1 // 8
    RGD = D // 8
    maps = []
    for c in range(8):
        b, qq = c // 4, c % 4
        r = c % 4
        emask = np.zeros((4, 4), np.float32)
        m1 = np.zeros((4,), np.float32)
        for pp in range(4):
            if pp < r:
                m1[pp] = 1.0
                for u in range(pp + 1, r):
                    emask[pp, u] = 1.0
        m = dict(base)
        m["x"] = np.ascontiguousarray(x[b, qq * T:(qq + 1) * T])
        m["emask"] = emask.reshape(-1)
        m["m1"] = m1
        fwg = np.asarray(inputs["L1_ffn_w_gate"], np.float32)
        fwu = np.asarray(inputs["L1_ffn_w_up"], np.float32)
        fwd = np.asarray(inputs["L1_ffn_w_down"], np.float32)
        m["wsh_g"] = np.ascontiguousarray(fwg[c * RG1:(c + 1) * RG1])
        m["wsh_u"] = np.ascontiguousarray(fwu[c * RG1:(c + 1) * RG1])
        m["wsh_d"] = np.ascontiguousarray(fwd[c * RGD:(c + 1) * RGD])
        maps.append(m)
    return maps


def kernel(**inputs):
    T = 4096
    key = ("full", T)
    if key not in _BUILT:
        _BUILT[key] = build(T, "full")
    nc = _BUILT[key]
    maps = _inmaps(inputs, T)
    res = run_bass_kernel_spmd(nc, maps, core_ids=list(range(8)))
    out = np.empty((2, 16384, D), np.float32)
    for c, r in enumerate(res.results):
        out[c // 4, (c % 4) * T:(c % 4 + 1) * T] = r["out"]
    return out
